# version 02
# baseline (speedup 1.0000x reference)
import numpy as np
import ml_dtypes
import concourse.bass as bass
import concourse.mybir as mybir
from concourse.bass_utils import run_bass_kernel_spmd

F32 = mybir.dt.float32
BF16 = mybir.dt.bfloat16
I32 = mybir.dt.int32
ALU = mybir.AluOpType
ACT = mybir.ActivationFunctionType
AX = mybir.AxisListType.X
BF = ml_dtypes.bfloat16

TOK = 1024
NT = 8
D = 2048
DI = 3072
IFF = 5632
NO1 = 88
NI2 = 44
MAGIC = 12582912.0
MASK_HI = int(np.int32(np.uint32(0xFFF00000)))

ENG = ("sync", "gpsimd", "tensor", "vector", "scalar")
KEY = {"sync": "ld", "gpsimd": "gd", "tensor": "t", "vector": "v", "scalar": "s"}
INC = {"ld": 16, "gd": 16, "t": 1, "v": 1, "s": 1, "g": 1}


PHASES = 99


def build(upto=6):
    nc = bass.Bass("TRN2", target_bir_lowering=False)

    xin = nc.dram_tensor("xin", [TOK, D], F32, kind="ExternalInput").ap()
    w_hg = nc.dram_tensor("w_hg", [D, 2 * DI], F32, kind="ExternalInput").ap()
    w_out = nc.dram_tensor("w_out", [DI, D], F32, kind="ExternalInput").ap()
    w_gu = nc.dram_tensor("w_gu", [2 * IFF, D], F32, kind="ExternalInput").ap()
    w_dn = nc.dram_tensor("w_dn", [D, IFF], F32, kind="ExternalInput").ap()
    wq_in = nc.dram_tensor("wq_in", [D, D], F32, kind="ExternalInput").ap()
    wk_in = nc.dram_tensor("wk_in", [512, D], F32, kind="ExternalInput").ap()
    wv_in = nc.dram_tensor("wv_in", [512, D], F32, kind="ExternalInput").ap()
    wo_in = nc.dram_tensor("wo_in", [D, D], F32, kind="ExternalInput").ap()
    wl_in = nc.dram_tensor("wl_in", [D, D], BF16, kind="ExternalInput").ap()
    idbf_in = nc.dram_tensor("idbf_in", [128, 128], BF16, kind="ExternalInput").ap()
    idf_in = nc.dram_tensor("idf_in", [128, 128], F32, kind="ExternalInput").ap()
    maskT_in = nc.dram_tensor("maskT_in", [128, 128], F32, kind="ExternalInput").ap()
    cos_in = nc.dram_tensor("cos_in", [TOK, 64], F32, kind="ExternalInput").ap()
    sin_in = nc.dram_tensor("sin_in", [TOK, 64], F32, kind="ExternalInput").ap()
    role_in = nc.dram_tensor("role_in", [128, 1], F32, kind="ExternalInput").ap()
    sel_in = nc.dram_tensor("sel_in", [4, 512], F32, kind="ExternalInput").ap()
    xout = nc.dram_tensor("xout", [TOK, D], F32, kind="ExternalOutput").ap()

    st_w1 = nc.dram_tensor("st_w1", [NO1, 128, D], BF16)
    st_w2 = nc.dram_tensor("st_w2", [NI2, 128, D], BF16)
    st_wl = nc.dram_tensor("st_wl", [16, 128, D], BF16)
    st_ao = nc.dram_tensor("st_ao", [16, 128, TOK], BF16)
    sp_h = nc.dram_tensor("sp_h", [24, 128, TOK], BF16)
    sp_cp = nc.dram_tensor("sp_cp", [24, 128, TOK], BF16)
    ag1_in = nc.dram_tensor("ag1_in", [128, 24], F32)
    ag1_out = nc.dram_tensor("ag1_out", [256, 24], F32)
    ag2_in = nc.dram_tensor("ag2_in", [128, 516], F32)
    ag2_out = nc.dram_tensor("ag2_out", [256, 516], F32)
    ar_in = nc.dram_tensor("ar_in", [128, 1], F32)
    ar_out = nc.dram_tensor("ar_out", [128, 1], F32)

    ops = {e: [] for e in ENG}
    cnt = {k: 0 for k in INC}
    waited = {e: {} for e in ENG}
    last_w = {}
    readers = {}
    sem = {}

    def _wait(eng, sk, val):
        if val <= 0 or waited[eng].get(sk, 0) >= val:
            return
        waited[eng][sk] = val
        ops[eng].append(lambda e, s=sk, v=val: e.wait_ge(sem[s], v))

    def emit(eng, fn, ins=(), outs=(), semkey=None, inc=None):
        mykey = semkey or KEY[eng]
        myinc = inc if inc is not None else INC[mykey]
        for nm in ins:
            lw = last_w.get(nm)
            if lw is not None and (lw[0] != mykey or eng != "tensor"):
                _wait(eng, lw[0], lw[1])
        for nm in outs:
            lw = last_w.get(nm)
            if lw is not None and lw[0] != mykey:
                _wait(eng, lw[0], lw[1])
            for sk, v in readers.get(nm, {}).items():
                if sk != mykey:
                    _wait(eng, sk, v)
        cnt[mykey] += myinc
        myval = cnt[mykey]
        ops[eng].append(lambda e, f=fn, s=mykey, i=myinc: f(e).then_inc(sem[s], i))
        for nm in outs:
            last_w[nm] = (mykey, myval)
            readers.pop(nm, None)
        for nm in ins:
            readers.setdefault(nm, {})[mykey] = myval
        return myval

    def barrier():
        for e in ENG:
            for sk in INC:
                if sk != KEY[e]:
                    _wait(e, sk, cnt[sk])

    def T(fn, ins=(), outs=()):
        emit("tensor", fn, ins, outs)

    def V(fn, ins=(), outs=()):
        emit("vector", fn, ins, outs)

    def S(fn, ins=(), outs=()):
        emit("scalar", fn, ins, outs)

    def G(fn, ins=(), outs=()):
        emit("gpsimd", fn, ins, outs, semkey="g", inc=1)

    def DMA(fn, ins=(), outs=(), q="sync"):
        emit(q, fn, ins, outs)

    scopes = []
    top_cms = []

    _uid = [0]

    def salloc(name, shape, dt):
        _uid[0] += 1
        cm = nc.sbuf_tensor(f"{name}_{_uid[0]}", shape, dt)
        t = cm.__enter__()
        (scopes[-1] if scopes else top_cms).append(cm)
        return t

    def push_scope():
        scopes.append([])

    def pop_scope():
        barrier()
        for cm in reversed(scopes.pop()):
            cm.__exit__(None, None, None)

    x = salloc("x", [128, NT, D], F32)
    xT = salloc("xT", [128, 16, TOK], BF16)
    y = salloc("y", [128, D], F32)
    sA = salloc("sA", [128, D], F32)
    sBF = salloc("sBF", [128, 16, 128], BF16)
    stats = salloc("stats", [128, 16], F32)
    idbf = salloc("idbf", [128, 128], BF16)
    idf = salloc("idf", [128, 128], F32)
    maskT = salloc("maskT", [128, 128], F32)
    cosb = salloc("cosb", [128, NT, 64], F32)
    sinb = salloc("sinb", [128, NT, 64], F32)
    role = salloc("role", [128, 1], F32)
    onesf = salloc("onesf", [1, 128], F32)
    ceps = salloc("ceps", [128, 2], F32)
    hend = salloc("hend", [128, 24], F32)
    hendp = salloc("hendp", [128, 24], F32)
    selc = salloc("selc", [4, 512], F32)

    def palloc(name, shape, dt):
        cm = nc.psum_tensor(name, shape, dt)
        t = cm.__enter__()
        top_cms.append(cm)
        return t

    pb = [palloc(f"pb{i}", [128, 512], F32) for i in range(5)]
    pt = [palloc(f"pt{i}", [128, 128], BF16) for i in range(3)]

    for sk in INC:
        cm = nc.semaphore("sem_" + sk)
        sem[sk] = cm.__enter__()
        top_cms.append(cm)

    def ln_tile(tt, out_ap, out_name):
        sa = x[:, tt, :]
        V(lambda e, a=sa: e.reduce_sum(stats[:, 0:1], a, AX), ("x",), ("stats",))
        S(lambda e, a=sa: e.activation(y[:, :], a, ACT.Square, accum_out=stats[:, 1:2]),
          ("x",), ("y", "stats"))
        V(lambda e: e.tensor_scalar(stats[:, 0:1], stats[:, 0:1], -1.0 / D, None, ALU.mult),
          ("stats",), ("stats",))
        V(lambda e: e.tensor_scalar(stats[:, 1:2], stats[:, 1:2], 1.0 / D, None, ALU.mult),
          ("stats",), ("stats",))
        V(lambda e: e.tensor_tensor(stats[:, 2:3], stats[:, 0:1], stats[:, 0:1], ALU.mult),
          ("stats",), ("stats",))
        V(lambda e: e.tensor_tensor(stats[:, 1:2], stats[:, 1:2], stats[:, 2:3], ALU.subtract),
          ("stats",), ("stats",))
        S(lambda e: e.activation(stats[:, 3:4], stats[:, 1:2], ACT.Sqrt, ceps[:, 0:1], 1.0),
          ("stats", "ceps"), ("stats",))
        V(lambda e: e.reciprocal(stats[:, 3:4], stats[:, 3:4]), ("stats",), ("stats",))
        S(lambda e: e.mul(stats[:, 2:3], stats[:, 0:1], stats[:, 3:4]), ("stats",), ("stats",))
        S(lambda e, o=out_ap, a=sa: e.activation(o, a, ACT.Identity, stats[:, 2:3], stats[:, 3:4]),
          ("x", "stats"), (out_name,))

    def tp16(src_name, dst, dst_name, col0, src=None):
        s = sBF if src is None else src
        for k in range(16):
            b, bn = pt[k % 3], f"pt{k % 3}"
            T(lambda e, kk=k, bb=b, ss=s: e.transpose(bb[:, :], ss[:, kk, :], idbf[:, :]),
              (src_name, "idbf"), (bn,))
            V(lambda e, kk=k, bb=b, d=dst, c=col0: e.tensor_copy(d[:, kk, c:c + 128], bb[:, :]),
              (bn,), (dst_name,))

    def rne_clip(ap, name, qmax=127.0):
        V(lambda e, a=ap: e.tensor_scalar(a, a, MAGIC, None, ALU.add), (name,), (name,))
        V(lambda e, a=ap: e.tensor_scalar(a, a, -MAGIC, None, ALU.add), (name,), (name,))
        V(lambda e, a=ap, q=qmax: e.tensor_scalar(a, a, q, -q, ALU.min, ALU.max),
          (name,), (name,))

    def e4m3_round(fap, iap, nf, ni):
        V(lambda e: e.tensor_scalar(iap, fap.bitcast(I32), 20, None,
                                    ALU.logical_shift_right), (nf,), (ni,))
        V(lambda e: e.tensor_scalar(iap, iap, 1, None, ALU.bitwise_and), (ni,), (ni,))
        V(lambda e: e.tensor_tensor(iap, iap, fap.bitcast(I32), ALU.add), (nf, ni), (ni,))
        V(lambda e: e.tensor_scalar(iap, iap, 0x0007FFFF, None, ALU.add), (ni,), (ni,))
        V(lambda e: e.tensor_scalar(iap, iap, MASK_HI, None, ALU.bitwise_and), (ni,), (ni,))
        V(lambda e: e.tensor_copy(fap, iap.bitcast(F32)), (ni,), (nf,))

    DMA(lambda e: e.dma_start(out=idbf[:, :], in_=idbf_in), (), ("idbf",))
    DMA(lambda e: e.dma_start(out=idf[:, :], in_=idf_in), (), ("idf",))
    DMA(lambda e: e.dma_start(out=maskT[:, :], in_=maskT_in), (), ("maskT",))
    DMA(lambda e: e.dma_start(out=cosb[:, :, :],
                              in_=cos_in.rearrange("(t p) c -> p t c", p=128)), (), ("cosb",))
    DMA(lambda e: e.dma_start(out=sinb[:, :, :],
                              in_=sin_in.rearrange("(t p) c -> p t c", p=128)), (), ("sinb",))
    DMA(lambda e: e.dma_start(out=role[:, :], in_=role_in), (), ("role",))
    DMA(lambda e: e.dma_start(out=x[:, :, :],
                              in_=xin.rearrange("(t p) d -> p t d", p=128)), (), ("x",))
    V(lambda e: e.memset(onesf[:, :], 1.0), (), ("onesf",))
    DMA(lambda e: e.dma_start(out=selc[:, :], in_=sel_in), (), ("selc",))
    V(lambda e: e.memset(ceps[:, 0:1], 1e-5), (), ("ceps",))

    push_scope()
    wrow = salloc("wrow", [128, IFF], F32)
    wqd = salloc("wqd", [128, IFF], BF16)
    sbT = salloc("sbT", [128, NI2, 128], BF16)
    wli = salloc("wli", [128, D], I32)
    wlrow = salloc("wlrow", [128, D], BF16)

    def wq_tile(src, sname, ncols, bi, ot):
        b, bn = wrow, "wrow"
        DMA(lambda e, o=ot, n=ncols, bb=b: e.dma_start(
            out=bb[:, 0:n], in_=src[o * 128:(o + 1) * 128, 0:n]),
            (sname,), (bn, bn + "L", bn + "H"))
        V(lambda e, n=ncols, bb=b: e.tensor_reduce(stats[:, 4:5], bb[:, 0:n], AX, ALU.max,
                                                   apply_absolute_value=True),
          (bn, bn + "L", bn + "H"), ("stats",))
        V(lambda e: e.tensor_scalar(stats[:, 5:6], stats[:, 4:5], 1.0 / 127.0, 1e-8,
                                    ALU.mult, ALU.max), ("stats",), ("stats",))
        V(lambda e: e.reciprocal(stats[:, 6:7], stats[:, 5:6]), ("stats",), ("stats",))
        nh = ncols // 2
        for EE, c0, c1, sfx in ((V, 0, nh, "L"), (G, nh, ncols, "H")):
            hn, wn = bn + sfx, "wqd" + sfx
            EE(lambda e, bb=b, a=c0, z=c1: e.tensor_scalar(bb[:, a:z], bb[:, a:z], stats[:, 6:7],
                                                           None, ALU.mult), (hn, "stats"), (hn,))
            EE(lambda e, bb=b, a=c0, z=c1: e.tensor_scalar(bb[:, a:z], bb[:, a:z], MAGIC, None,
                                                           ALU.add), (hn,), (hn,))
            EE(lambda e, bb=b, a=c0, z=c1: e.tensor_scalar(bb[:, a:z], bb[:, a:z], -MAGIC, None,
                                                           ALU.add), (hn,), (hn,))
            EE(lambda e, bb=b, a=c0, z=c1: e.tensor_scalar(bb[:, a:z], bb[:, a:z], 127.0, -127.0,
                                                           ALU.min, ALU.max), (hn,), (hn,))
            EE(lambda e, bb=b, a=c0, z=c1: e.tensor_scalar(wqd[:, a:z], bb[:, a:z], stats[:, 5:6],
                                                           None, ALU.mult), (hn, "stats"), (wn,))

    def stash_tp(nk, store_fn):
        for k in range(nk):
            b, bn = pt[k % 3], f"pt{k % 3}"
            half = "wqdL" if k < nk // 2 else "wqdH"
            T(lambda e, kk=k, bb=b: e.transpose(bb[:, :], wqd[:, kk * 128:(kk + 1) * 128], idbf[:, :]),
              (half, "idbf"), (bn,))
            S(lambda e, kk=k, bb=b: e.copy(sbT[:, kk, :], bb[:, :]), (bn,), ("sbT",))
        store_fn()

    for ot in range(NO1):
        wq_tile(w_gu, "w_gu", D, ot % 2, ot)
        stash_tp(16, lambda o=ot: DMA(lambda e: e.dma_start(
            out=st_w1.ap()[o].rearrange("p (a b) -> p a b", a=16), in_=sbT[:, 0:16, :]),
            ("sbT",), ("st_w1",), q="gpsimd"))
    for ot in range(16):
        wq_tile(w_dn, "w_dn", IFF, ot % 2, ot)

        def store_w2(o=ot):
            for h in range(2):
                DMA(lambda e, hh=h: e.dma_start(
                    out=st_w2.ap()[hh * 22:(hh + 1) * 22, :, o * 128:(o + 1) * 128]
                        .rearrange("a p b -> p a b"),
                    in_=sbT[:, hh * 22:(hh + 1) * 22, :]), ("sbT",), ("st_w2",), q="gpsimd")
        stash_tp(NI2, store_w2)

    V(lambda e: e.memset(stats[:, 8:9], 0.0), (), ("stats",))
    for ot in range(16):
        DMA(lambda e, o=ot: e.dma_start(
            out=wlrow[:, :], in_=wl_in[o * 128:(o + 1) * 128, :]),
            ("wl_in",), ("wlrow",))
        V(lambda e: e.tensor_reduce(stats[:, 9:10], wlrow[:, :],
                                    AX, ALU.max, apply_absolute_value=True),
          ("wlrow",), ("stats",))
        V(lambda e: e.tensor_tensor(stats[:, 8:9], stats[:, 8:9], stats[:, 9:10], ALU.max),
          ("stats",), ("stats",))
    T(lambda e: e.transpose(pb[4][0:1, 0:128], stats[:, 8:9], idf[:, :]), ("stats", "idf"), ("pb4",))
    V(lambda e: e.reduce_max(stats[0:1, 9:10], pb[4][0:1, 0:128], AX), ("pb4",), ("stats",))
    T(lambda e: e.matmul(pb[4][:, 0:1], onesf[0:1, :], stats[0:1, 9:10], start=True, stop=True),
      ("onesf", "stats"), ("pb4",))
    V(lambda e: e.tensor_scalar(stats[:, 12:13], pb[4][:, 0:1], 1.0 / 448.0, 1e-12,
                                ALU.mult, ALU.max), ("pb4",), ("stats",))
    V(lambda e: e.reciprocal(stats[:, 13:14], stats[:, 12:13]), ("stats",), ("stats",))
    for ot in range(16):
        DMA(lambda e, o=ot: e.dma_start(
            out=wlrow[:, :], in_=wl_in[o * 128:(o + 1) * 128, :]),
            ("wl_in",), ("wlrow",))
        V(lambda e: e.tensor_scalar(sA[:, :], wlrow[:, :],
                                    stats[:, 13:14], None, ALU.mult),
          ("wlrow", "stats"), ("sA",))
        e4m3_round(sA[:, :], wli[:, :], "sA", "wli")
        V(lambda e: e.tensor_copy(wqd[:, 0:D], sA[:, :]), ("sA",), ("wqd", "wqdL", "wqdH"))
        stash_tp(16, lambda o=ot: DMA(lambda e: e.dma_start(
            out=st_wl.ap()[:, :, o * 128:(o + 1) * 128].rearrange("a p b -> p a b"),
            in_=sbT[:, 0:16, :]), ("sbT",), ("st_wl",), q="gpsimd"))
    pop_scope()

    def l0_ffn():
        push_scope()
        hsb = salloc("hsb", [128, NI2, 512], BF16)
        for half in range(2 if PHASES >= 5 else 1):
            tts = list(range(half * 4, half * 4 + 4))
            push_scope()
            w1g2 = [salloc("w1g", [128, 16, 128], BF16) for _ in range(2)]
            w1u2 = [salloc("w1u", [128, 16, 128], BF16) for _ in range(2)]
            sB = salloc("sB", [128, 512], F32)
            sC = salloc("sC", [128, 512], F32)
            sCg = salloc("sCg", [128, 512], F32)
            hax = salloc("hax", [128, 4], F32)
            hrecT = salloc("hrecT", [4, 128], F32)
            hsclT = salloc("hsclT", [4, 128], F32)
            recb = salloc("recb", [128, 512], F32)
            sclb = salloc("sclb", [128, 512], F32)
            for tt in tts:
                ln_tile(tt, y[:, :], "y")
                V(lambda e: e.tensor_reduce(stats[:, 4:5], y[:, :], AX, ALU.max,
                                            apply_absolute_value=True), ("y",), ("stats",))
                V(lambda e: e.tensor_scalar(stats[:, 5:6], stats[:, 4:5], 1.0 / 127.0, 1e-8,
                                            ALU.mult, ALU.max), ("stats",), ("stats",))
                V(lambda e: e.reciprocal(stats[:, 6:7], stats[:, 5:6]), ("stats",), ("stats",))
                V(lambda e: e.tensor_scalar(sA[:, :], y[:, :], stats[:, 6:7], None, ALU.mult),
                  ("y", "stats"), ("sA",))
                rne_clip(sA[:, :], "sA")
                V(lambda e: e.tensor_scalar(sBF[:, :, :].rearrange("p a b -> p (a b)"),
                                            sA[:, :], stats[:, 5:6], None, ALU.mult),
                  ("sA", "stats"), ("sBF",))
                tp16("sBF", xT, "xT", (tt - half * 4) * 128)
            V(lambda e: e.memset(hax[:, :], 0.0), (), ("hax",))
            for mp in range(NI2 if PHASES >= 2 else 0):
                bg, bgn = w1g2[mp % 2], f"w1g{mp % 2}"
                bu, bun = w1u2[mp % 2], f"w1u{mp % 2}"
                DMA(lambda e, m=mp, b=bg: e.dma_start(
                    out=b[:, :, :], in_=st_w1.ap()[m].rearrange("p (a b) -> p a b", a=16)),
                    ("st_w1",), (bgn,))
                DMA(lambda e, m=mp, b=bu: e.dma_start(
                    out=b[:, :, :], in_=st_w1.ap()[m + NI2].rearrange("p (a b) -> p a b", a=16)),
                    ("st_w1",), (bun,))
                for k in range(16):
                    T(lambda e, kk=k, b=bg: e.matmul(
                        pb[0][:, :], b[:, kk, :], xT[:, kk, 0:512],
                        start=(kk == 0), stop=(kk == 15)), (bgn, "xT"), ("pb0",))
                for k in range(16):
                    T(lambda e, kk=k, b=bu: e.matmul(
                        pb[1][:, :], b[:, kk, :], xT[:, kk, 0:512],
                        start=(kk == 0), stop=(kk == 15)), (bun, "xT"), ("pb1",))
                S(lambda e: e.activation(sB[:, :], pb[0][:, :], ACT.Silu), ("pb0",), ("sB",))
                V(lambda e, m=mp: e.tensor_tensor(hsb[:, m, :], sB[:, :], pb[1][:, :], ALU.mult),
                  ("sB", "pb1"), ("hsb", "hsb0", "hsb1"))
                for c in range(4):
                    b, bn = pt[c % 3], f"pt{c % 3}"
                    T(lambda e, m=mp, cc=c, bb=b: e.transpose(
                        bb[:, :], hsb[:, m, cc * 128:(cc + 1) * 128], idbf[:, :]),
                        ("hsb", "idbf"), (bn,))
                    V(lambda e, bb=b: e.tensor_reduce(stats[:, 7:8], bb[:, :], AX, ALU.max,
                                                      apply_absolute_value=True),
                      (bn,), ("stats",))
                    V(lambda e, cc=c: e.tensor_tensor(hax[:, cc:cc + 1], hax[:, cc:cc + 1],
                                                      stats[:, 7:8], ALU.max),
                      ("stats", "hax"), ("hax",))
            V(lambda e: e.tensor_scalar(hax[:, 0:4], hax[:, 0:4], 1.0 / 127.0, 1e-8,
                                        ALU.mult, ALU.max), ("hax",), ("hax",))
            T(lambda e: e.transpose(pb[4][0:4, 0:128], hax[:, 0:4], idf[:, :]), ("hax", "idf"), ("pb4",))
            V(lambda e: e.tensor_copy(hsclT[0:4, :], pb[4][0:4, 0:128]), ("pb4",), ("hsclT",))
            V(lambda e: e.reciprocal(hrecT[0:4, :], hsclT[0:4, :]), ("hsclT",), ("hrecT",))
            for c in range(4):
                T(lambda e, cc=c: e.matmul(pb[4][:, cc * 128:(cc + 1) * 128],
                                           selc[0:4, cc * 128:(cc + 1) * 128],
                                           hrecT[0:4, :], start=True, stop=True),
                  ("selc", "hrecT"), ("pb4",))
            V(lambda e: e.tensor_copy(recb[:, :], pb[4][:, 0:512]), ("pb4",), ("recb",))
            for c in range(4):
                T(lambda e, cc=c: e.matmul(pb[4][:, cc * 128:(cc + 1) * 128],
                                           selc[0:4, cc * 128:(cc + 1) * 128],
                                           hsclT[0:4, :], start=True, stop=True),
                  ("selc", "hsclT"), ("pb4",))
            V(lambda e: e.tensor_copy(sclb[:, :], pb[4][:, 0:512]), ("pb4",), ("sclb",))
            for mp in range(NI2 if PHASES >= 3 else 0):
                EE = V if mp % 2 == 0 else G
                tb = sC if mp % 2 == 0 else sCg
                tn = "sC" if mp % 2 == 0 else "sCg"
                hn = f"hsb{mp % 2}"
                EE(lambda e, m=mp, t=tb: e.tensor_copy(t[:, :], hsb[:, m, :]), (hn, "hsb"), (tn,))
                EE(lambda e, t=tb: e.tensor_tensor(t[:, :], t[:, :], recb[:, :], ALU.mult),
                   (tn, "recb"), (tn,))
                EE(lambda e, t=tb: e.tensor_scalar(t[:, :], t[:, :], MAGIC, None, ALU.add),
                   (tn,), (tn,))
                EE(lambda e, t=tb: e.tensor_scalar(t[:, :], t[:, :], -MAGIC, None, ALU.add),
                   (tn,), (tn,))
                EE(lambda e, t=tb: e.tensor_scalar(t[:, :], t[:, :], 127.0, -127.0,
                                                   ALU.min, ALU.max), (tn,), (tn,))
                EE(lambda e, t=tb: e.tensor_tensor(t[:, :], t[:, :], sclb[:, :], ALU.mult),
                   (tn, "sclb"), (tn,))
                EE(lambda e, m=mp, t=tb: e.tensor_copy(hsb[:, m, :], t[:, :]), (tn,), (hn,))
            pop_scope()
            push_scope()
            w2b = salloc("w2b", [128, NI2, 256], BF16)
            for nch in range(8 if PHASES >= 4 else 0):
                b, bn = w2b, "w2b"
                for h2 in range(2):
                    DMA(lambda e, n=nch, hh=h2, bb=b: e.dma_start(
                        out=bb[:, hh * 22:(hh + 1) * 22, :],
                        in_=st_w2.ap()[hh * 22:(hh + 1) * 22, :, n * 256:(n + 1) * 256]
                            .rearrange("a p b -> p a b")), ("st_w2",), (bn,))
                for i, tt in enumerate(tts):
                    pbk, pbn = pb[i % 2], f"pb{i % 2}"
                    for k in range(NI2):
                        T(lambda e, kk=k, lt=tt - half * 4, bb=b, p=pbk: e.matmul(
                            p[:, 0:256], hsb[:, kk, lt * 128:(lt + 1) * 128], bb[:, kk, :],
                            start=(kk == 0), stop=(kk == NI2 - 1)), (bn, "hsb0", "hsb1"), (pbn,))
                    V(lambda e, t=tt, n=nch, p=pbk: e.scalar_tensor_tensor(
                        x[:, t, n * 256:(n + 1) * 256], p[:, 0:256], 0.5,
                        x[:, t, n * 256:(n + 1) * 256], ALU.mult, ALU.add),
                      (pbn, "x"), ("x",))
            pop_scope()
        pop_scope()

    def mingru():
        push_scope()
        whf2 = [salloc("whf", [128, 16, 128], F32) for _ in range(2)]
        whb = [salloc(f"whb{i}", [128, 16, 128], BF16) for i in range(2)]
        wgb = [salloc(f"wgb{i}", [128, 16, 128], BF16) for i in range(2)]
        ct = salloc("ct", [128, TOK], F32)
        zt = salloc("zt", [128, TOK], F32)
        gt = salloc("gt", [128, TOK], F32)
        ht = salloc("ht", [128, TOK], F32)
        cp = salloc("cp", [128, TOK], F32)
        mki = salloc("mki", [128, TOK], I32)
        hbf = salloc("hbf", [128, TOK], BF16)
        cpb = salloc("cpb", [128, TOK], BF16)
        for tt in range(NT):
            ln_tile(tt, sBF[:, :, :].rearrange("p a b -> p (a b)"), "sBF")
            tp16("sBF", xT, "xT", tt * 128)
        for j in range(24):
            bh, bhn = whb[j % 2], f"whb{j % 2}"
            bg, bgn = wgb[j % 2], f"wgb{j % 2}"
            wf, wfn = whf2[j % 2], f"whf{j % 2}"
            DMA(lambda e, jj=j, w=wf: e.dma_start(
                out=w[:, :, :],
                in_=w_hg[:, jj * 128:(jj + 1) * 128].rearrange("(a p) c -> p a c", p=128)),
                ("w_hg",), (wfn,))
            V(lambda e, b=bh, w=wf: e.tensor_copy(
                b[:, :, :].rearrange("p a b -> p (a b)"),
                w[:, :, :].rearrange("p a b -> p (a b)")), (wfn,), (bhn,))
            wf2, wfn2 = whf2[(j + 1) % 2], f"whf{(j + 1) % 2}"
            DMA(lambda e, jj=j, w=wf2: e.dma_start(
                out=w[:, :, :],
                in_=w_hg[:, DI + jj * 128:DI + (jj + 1) * 128]
                    .rearrange("(a p) c -> p a c", p=128)), ("w_hg",), (wfn2,))
            G(lambda e, b=bg, w=wf2: e.tensor_copy(
                b[:, :, :].rearrange("p a b -> p (a b)"),
                w[:, :, :].rearrange("p a b -> p (a b)")), (wfn2,), (bgn,))
            for hf in range(2):
                for k in range(16):
                    T(lambda e, kk=k, h=hf, b=bh: e.matmul(
                        pb[h][:, :], b[:, kk, :], xT[:, kk, h * 512:(h + 1) * 512],
                        start=(kk == 0), stop=(kk == 15)), (bhn, "xT"), (f"pb{hf}",))
                for k in range(16):
                    T(lambda e, kk=k, h=hf, b=bg: e.matmul(
                        pb[2 + h][:, :], b[:, kk, :], xT[:, kk, h * 512:(h + 1) * 512],
                        start=(kk == 0), stop=(kk == 15)), (bgn, "xT"), (f"pb{2 + hf}",))
            for hf in range(2):
                sl = slice(hf * 512, (hf + 1) * 512)
                S(lambda e, h=hf, s=sl: e.activation(ct[:, s], pb[2 + h][:, :], ACT.Sigmoid,
                                                     0.0, -1.0), (f"pb{2 + hf}",), ("ct",))
                S(lambda e, h=hf, s=sl: e.activation(zt[:, s], pb[2 + h][:, :], ACT.Sigmoid),
                  (f"pb{2 + hf}",), ("zt",))
                S(lambda e, h=hf, s=sl: e.activation(gt[:, s], pb[h][:, :], ACT.Sigmoid),
                  (f"pb{hf}",), ("gt",))
                V(lambda e, h=hf, s=sl: e.tensor_scalar(ht[:, s], pb[h][:, :], 0.5, None,
                                                        ALU.add), (f"pb{hf}",), ("ht",))
                V(lambda e, h=hf, s=sl: e.tensor_scalar(mki[:, s], pb[h][:, :], 0.0, None,
                                                        ALU.is_ge), (f"pb{hf}",), ("mki",))
            V(lambda e: e.copy_predicated(gt[:, :], mki[:, :], ht[:, :]),
              ("mki", "ht", "gt"), ("gt",))
            V(lambda e: e.tensor_tensor(gt[:, :], zt[:, :], gt[:, :], ALU.mult),
              ("zt", "gt"), ("gt",))
            V(lambda e: e.tensor_tensor_scan(ht[:, 0:512], ct[:, 0:512], gt[:, 0:512],
                                             0.0, ALU.mult, ALU.add), ("ct", "gt"), ("ht",))
            V(lambda e: e.tensor_tensor_scan(ht[:, 512:1024], ct[:, 512:1024], gt[:, 512:1024],
                                             ht[:, 511:512], ALU.mult, ALU.add),
              ("ct", "gt", "ht"), ("ht",))
            V(lambda e: e.tensor_tensor_scan(cp[:, 0:512], ct[:, 0:512], gt[:, 0:512],
                                             1.0, ALU.mult, ALU.bypass), ("ct", "gt"), ("cp",))
            V(lambda e: e.tensor_tensor_scan(cp[:, 512:1024], ct[:, 512:1024], gt[:, 512:1024],
                                             cp[:, 511:512], ALU.mult, ALU.bypass),
              ("ct", "gt", "cp"), ("cp",))
            V(lambda e, jj=j: e.tensor_copy(hend[:, jj:jj + 1], ht[:, 1023:1024]),
              ("ht",), ("hend",))
            S(lambda e: e.copy(hbf[:, :], ht[:, :]), ("ht",), ("hbf",))
            S(lambda e: e.copy(cpb[:, :], cp[:, :]), ("cp",), ("cpb",))
            DMA(lambda e, jj=j: e.dma_start(out=sp_h.ap()[jj], in_=hbf[:, :]),
                ("hbf",), ("sp_h",), q="gpsimd")
            DMA(lambda e, jj=j: e.dma_start(out=sp_cp.ap()[jj], in_=cpb[:, :]),
                ("cpb",), ("sp_cp",), q="gpsimd")
        DMA(lambda e: e.dma_start(out=ag1_in.ap(), in_=hend[:, :]), ("hend",), ("ag1_in",),
            q="gpsimd")
        G(lambda e: e.collective_compute("AllGather", ALU.bypass,
                                         replica_groups=[[0, 1], [2, 3], [4, 5], [6, 7]],
                                         ins=[ag1_in.ap()], outs=[ag1_out.ap()]),
          ("ag1_in",), ("ag1_out",))
        DMA(lambda e: e.dma_start(out=hendp[:, :], in_=ag1_out.ap()[0:128, :]),
            ("ag1_out",), ("hendp",))
        V(lambda e: e.tensor_scalar(hendp[:, :], hendp[:, :], role[:, 0:1], None, ALU.mult),
          ("hendp", "role"), ("hendp",))
        pop_scope()
        push_scope()
        hfb = salloc("hfb", [128, TOK], BF16)
        cpb2 = salloc("cpb2", [128, TOK], BF16)
        sB2 = salloc("sB2", [128, TOK], F32)
        sC2 = salloc("sC2", [128, TOK], F32)
        for k in range(24):
            DMA(lambda e, kk=k: e.dma_start(out=hfb[:, :], in_=sp_h.ap()[kk]),
                ("sp_h",), ("hfb",), q="gpsimd")
            DMA(lambda e, kk=k: e.dma_start(out=cpb2[:, :], in_=sp_cp.ap()[kk]),
                ("sp_cp",), ("cpb2",), q="gpsimd")
            V(lambda e, kk=k: e.tensor_scalar(sB2[:, :], cpb2[:, :], hendp[:, kk:kk + 1],
                                              None, ALU.mult), ("cpb2", "hendp"), ("sB2",))
            V(lambda e: e.tensor_copy(sC2[:, :], hfb[:, :]), ("hfb",), ("sC2",))
            V(lambda e: e.tensor_tensor(sC2[:, :], sC2[:, :], sB2[:, :], ALU.add),
              ("sB2", "sC2"), ("sC2",))
            V(lambda e: e.tensor_copy(hfb[:, :], sC2[:, :]), ("sC2",), ("hfb",))
            DMA(lambda e, kk=k: e.dma_start(out=sp_h.ap()[kk], in_=hfb[:, :]),
                ("hfb",), ("sp_h",), q="gpsimd")
        pop_scope()
        push_scope()
        hfx = salloc("hfx", [128, 12, TOK], BF16)
        wob = salloc("wob", [128, 12, 512], BF16)
        wcf = salloc("wcf", [128, 512], F32)
        for nch in range(4):
            for kh in range(2):
                for k in range(12):
                    DMA(lambda e, kk=k, k0=kh * 12: e.dma_start(
                        out=hfx[:, kk, :], in_=sp_h.ap()[k0 + kk]),
                        ("sp_h",), ("hfx",), q="gpsimd")
                    DMA(lambda e, kk=k, k0=kh * 12, n=nch: e.dma_start(
                        out=wcf[:, :],
                        in_=w_out[(k0 + kk) * 128:(k0 + kk + 1) * 128,
                                  n * 512:(n + 1) * 512]), ("w_out",), ("wcf",))
                    V(lambda e, kk=k: e.tensor_copy(wob[:, kk, :], wcf[:, :]),
                      ("wcf",), ("wob",))
                for tt in range(NT):
                    pbk, pbn = pb[tt % 2], f"pb{tt % 2}"
                    for k in range(12):
                        T(lambda e, kk=k, t=tt, p=pbk: e.matmul(
                            p[:, :], hfx[:, kk, t * 128:(t + 1) * 128], wob[:, kk, :],
                            start=(kk == 0), stop=(kk == 11)), ("hfx", "wob"), (pbn,))
                    V(lambda e, t=tt, n=nch, p=pbk: e.scalar_tensor_tensor(
                        x[:, t, n * 512:(n + 1) * 512], p[:, :], 1.0,
                        x[:, t, n * 512:(n + 1) * 512], ALU.mult, ALU.add), (pbn, "x"), ("x",))
        pop_scope()

    def l1_ffn():
        push_scope()
        yall = salloc("yall", [128, NT, D], BF16)
        l1b2 = [salloc("l1b", [128, 16, 512], BF16) for _ in range(2)]
        sI = salloc("sI", [128, D], I32)
        obf = salloc("obf", [128, 512], BF16)
        sB3 = salloc("sB3", [128, 512], F32)
        V(lambda e: e.memset(stats[:, 8:9], 0.0), (), ("stats",))
        for tt in range(NT):
            ln_tile(tt, y[:, :], "y")
            V(lambda e: e.tensor_reduce(stats[:, 9:10], y[:, :], AX, ALU.max,
                                        apply_absolute_value=True), ("y",), ("stats",))
            V(lambda e: e.tensor_tensor(stats[:, 8:9], stats[:, 8:9], stats[:, 9:10], ALU.max),
              ("stats",), ("stats",))
            S(lambda e, t=tt: e.copy(yall[:, t, :], y[:, :]), ("y",), ("yall",))
        T(lambda e: e.transpose(pb[4][0:1, 0:128], stats[:, 8:9], idf[:, :]), ("stats", "idf"), ("pb4",))
        V(lambda e: e.reduce_max(stats[0:1, 9:10], pb[4][0:1, 0:128], AX), ("pb4",), ("stats",))
        T(lambda e: e.matmul(pb[4][:, 0:1], onesf[0:1, :], stats[0:1, 9:10],
                             start=True, stop=True), ("onesf", "stats"), ("pb4",))
        V(lambda e: e.tensor_copy(stats[:, 10:11], pb[4][:, 0:1]), ("pb4",), ("stats",))
        DMA(lambda e: e.dma_start(out=ar_in.ap(), in_=stats[:, 10:11]), ("stats",), ("ar_in",),
            q="gpsimd")
        G(lambda e: e.collective_compute("AllReduce", ALU.max,
                                         replica_groups=[list(range(8))],
                                         ins=[ar_in.ap()], outs=[ar_out.ap()]),
          ("ar_in",), ("ar_out",))
        DMA(lambda e: e.dma_start(out=stats[:, 10:11], in_=ar_out.ap()), ("ar_out",), ("stats",))
        V(lambda e: e.tensor_scalar(stats[:, 4:5], stats[:, 10:11], 1.0 / 448.0, 1e-12,
                                    ALU.mult, ALU.max), ("stats",), ("stats",))
        V(lambda e: e.reciprocal(stats[:, 5:6], stats[:, 4:5]), ("stats",), ("stats",))
        V(lambda e: e.tensor_tensor(stats[:, 6:7], stats[:, 4:5], stats[:, 12:13], ALU.mult),
          ("stats",), ("stats",))
        for tt in range(NT):
            V(lambda e, t=tt: e.tensor_scalar(sA[:, :], yall[:, t, :], stats[:, 5:6],
                                              None, ALU.mult), ("yall", "stats"), ("sA",))
            e4m3_round(sA[:, :], sI[:, :], "sA", "sI")
            V(lambda e: e.tensor_copy(sBF[:, :, :].rearrange("p a b -> p (a b)"), sA[:, :]),
              ("sA",), ("sBF",))
            tp16("sBF", xT, "xT", tt * 128)
        for nch in range(4):
            b, bn = l1b2[nch % 2], f"l1b{nch % 2}"
            DMA(lambda e, n=nch, bb=b: e.dma_start(
                out=bb[:, :, :],
                in_=st_wl.ap()[:, :, n * 512:(n + 1) * 512].rearrange("a p b -> p a b")),
                ("st_wl",), (bn,))
            for tt in range(NT):
                pbk, pbn = pb[tt % 2], f"pb{tt % 2}"
                for k in range(16):
                    T(lambda e, kk=k, t=tt, bb=b, p=pbk: e.matmul(
                        p[:, :], xT[:, kk, t * 128:(t + 1) * 128], bb[:, kk, :],
                        start=(kk == 0), stop=(kk == 15)), (bn, "xT"), (pbn,))
                V(lambda e, p=pbk: e.tensor_scalar(obf[:, :], p[:, :], stats[:, 6:7],
                                                   None, ALU.mult), (pbn, "stats"), ("obf",))
                V(lambda e: e.tensor_scalar(sB3[:, :], obf[:, :], 0.5, None, ALU.mult),
                  ("obf",), ("sB3",))
                V(lambda e, t=tt, n=nch: e.tensor_tensor(
                    x[:, t, n * 512:(n + 1) * 512], x[:, t, n * 512:(n + 1) * 512],
                    sB3[:, :], ALU.add), ("sB3", "x"), ("x",))
        pop_scope()

    def l1_attn():
        push_scope()
        kfT = salloc("kfT", [128, 4, TOK], BF16)
        ktok = salloc("ktok", [128, NT, 512], BF16)
        vaug = salloc("vaug", [128, NT, 516], BF16)
        wqt = salloc("wqt", [128, 16, 512], BF16)
        qtile = salloc("qtile", [128, 512], F32)
        mki2 = salloc("mki2", [128, 512], I32)
        for tt in range(NT):
            ln_tile(tt, sBF[:, :, :].rearrange("p a b -> p (a b)"), "sBF")
            tp16("sBF", xT, "xT", tt * 128)

        def load_wqt(src, sname, mb, nrows):
            for r in range(nrows):
                DMA(lambda e, rr=r, m=mb: e.dma_start(
                    out=sA[:, :], in_=src[(m * 4 + rr) * 128:(m * 4 + rr + 1) * 128, :]),
                    (sname,), ("sA",))
                V(lambda e: e.tensor_copy(sBF[:, :, :].rearrange("p a b -> p (a b)"), sA[:, :]),
                  ("sA",), ("sBF",))
                for k in range(16):
                    b, bn = pt[k % 3], f"pt{k % 3}"
                    T(lambda e, kk=k, bb=b: e.transpose(bb[:, :], sBF[:, kk, :], idbf[:, :]),
                      ("sBF", "idbf"), (bn,))
                    V(lambda e, kk=k, rr=r, bb=b: e.tensor_copy(
                        wqt[:, kk, rr * 128:(rr + 1) * 128], bb[:, :]), (bn,), ("wqt",))

        def rope_elu(nheads, tt):
            for hh in range(nheads):
                a = qtile[:, hh * 128:hh * 128 + 64]
                bq = qtile[:, hh * 128 + 64:hh * 128 + 128]
                V(lambda e, A=a: e.tensor_tensor(y[:, 0:64], A, cosb[:, tt, :], ALU.mult),
                  ("qtile", "cosb"), ("y",))
                V(lambda e, B=bq: e.tensor_tensor(y[:, 64:128], B, sinb[:, tt, :], ALU.mult),
                  ("qtile", "sinb"), ("y",))
                V(lambda e, B=bq: e.tensor_tensor(y[:, 128:192], B, cosb[:, tt, :], ALU.mult),
                  ("qtile", "cosb"), ("y",))
                V(lambda e, A=a: e.tensor_tensor(y[:, 192:256], A, sinb[:, tt, :], ALU.mult),
                  ("qtile", "sinb"), ("y",))
                V(lambda e, A=a: e.tensor_tensor(A, y[:, 0:64], y[:, 64:128], ALU.subtract),
                  ("y",), ("qtile",))
                V(lambda e, B=bq: e.tensor_tensor(B, y[:, 128:192], y[:, 192:256], ALU.add),
                  ("y",), ("qtile",))
            n = nheads * 128
            S(lambda e, m=n: e.activation(y[:, 0:m], qtile[:, 0:m], ACT.Exp),
              ("qtile",), ("y",))
            V(lambda e, m=n: e.tensor_scalar(mki2[:, 0:m], qtile[:, 0:m], 0.0, None, ALU.is_ge),
              ("qtile",), ("mki2",))
            V(lambda e, m=n: e.tensor_scalar(qtile[:, 0:m], qtile[:, 0:m], 1.0, None, ALU.add),
              ("qtile",), ("qtile",))
            V(lambda e, m=n: e.copy_predicated(y[:, 0:m], mki2[:, 0:m], qtile[:, 0:m]),
              ("mki2", "qtile", "y"), ("y",))
            V(lambda e, m=n: e.tensor_copy(
                sBF[:, :, :].rearrange("p a b -> p (a b)")[:, 0:m], y[:, 0:m]),
              ("y",), ("sBF",))

        load_wqt(wk_in, "wk_in", 0, 4)
        for tt in range(NT):
            for k in range(16):
                T(lambda e, kk=k, t=tt: e.matmul(pb[0][:, :], xT[:, kk, t * 128:(t + 1) * 128],
                                                 wqt[:, kk, :], start=(kk == 0), stop=(kk == 15)),
                  ("wqt", "xT"), ("pb0",))
            V(lambda e: e.tensor_copy(qtile[:, :], pb[0][:, :]), ("pb0",), ("qtile",))
            rope_elu(4, tt)
            V(lambda e, t=tt: e.tensor_copy(
                ktok[:, t, :], sBF[:, :, :].rearrange("p a b -> p (a b)")[:, 0:512]),
              ("sBF",), ("ktok",))
            for g in range(4):
                b, bn = pt[g % 3], f"pt{g % 3}"
                T(lambda e, gg=g, bb=b: e.transpose(bb[:, :], sBF[:, gg, :], idbf[:, :]),
                  ("sBF", "idbf"), (bn,))
                V(lambda e, gg=g, t=tt, bb=b: e.tensor_copy(
                    kfT[:, gg, t * 128:(t + 1) * 128], bb[:, :]), (bn,), ("kfT",))
        load_wqt(wv_in, "wv_in", 0, 4)
        for g in range(4):
            V(lambda e, gg=g: e.memset(vaug[:, :, gg * 129 + 128:gg * 129 + 129], 1.0),
              (), ("vaug",))
        for tt in range(NT):
            for k in range(16):
                T(lambda e, kk=k, t=tt: e.matmul(pb[0][:, :], xT[:, kk, t * 128:(t + 1) * 128],
                                                 wqt[:, kk, :], start=(kk == 0), stop=(kk == 15)),
                  ("wqt", "xT"), ("pb0",))
            for g in range(4):
                V(lambda e, gg=g, t=tt: e.tensor_copy(
                    vaug[:, t, gg * 129:gg * 129 + 128], pb[0][:, gg * 128:(gg + 1) * 128]),
                  ("pb0",), ("vaug",))
        snaps = salloc("snaps", [128, 36, 129], BF16)
        stacc = salloc("stacc", [128, 129], F32)
        stfin = salloc("stfin", [128, 516], F32)
        stp = salloc("stp", [128, 516], F32)
        stpb = salloc("stpb", [128, 516], BF16)
        for g in range(4):
            V(lambda e: e.memset(stacc[:, :], 0.0), (), ("stacc",))
            for c in range(NT):
                T(lambda e, gg=g, cc=c: e.matmul(
                    pb[2][:, 0:129], ktok[:, cc, gg * 128:(gg + 1) * 128],
                    vaug[:, cc, gg * 129:(gg + 1) * 129], start=True, stop=True),
                    ("ktok", "vaug"), ("pb2",))
                V(lambda e: e.tensor_tensor(stacc[:, :], stacc[:, :], pb[2][:, 0:129], ALU.add),
                  ("pb2", "stacc"), ("stacc",))
                V(lambda e, gg=g, cc=c: e.tensor_copy(snaps[:, gg * 9 + cc + 1, :], stacc[:, :]),
                  ("stacc",), ("snaps",))
            V(lambda e, gg=g: e.tensor_copy(stfin[:, gg * 129:(gg + 1) * 129], stacc[:, :]),
              ("stacc",), ("stfin",))
        DMA(lambda e: e.dma_start(out=ag2_in.ap(), in_=stfin[:, :]), ("stfin",), ("ag2_in",),
            q="gpsimd")
        G(lambda e: e.collective_compute("AllGather", ALU.bypass,
                                         replica_groups=[[0, 1], [2, 3], [4, 5], [6, 7]],
                                         ins=[ag2_in.ap()], outs=[ag2_out.ap()]),
          ("ag2_in",), ("ag2_out",))
        DMA(lambda e: e.dma_start(out=stp[:, :], in_=ag2_out.ap()[0:128, :]),
            ("ag2_out",), ("stp",))
        V(lambda e: e.tensor_scalar(stp[:, :], stp[:, :], role[:, 0:1], None, ALU.mult),
          ("stp", "role"), ("stp",))
        V(lambda e: e.tensor_copy(stpb[:, :], stp[:, :]), ("stp",), ("stpb",))
        for g in range(4):
            V(lambda e, gg=g: e.tensor_copy(snaps[:, gg * 9, :], stpb[:, gg * 129:(gg + 1) * 129]),
              ("stpb",), ("snaps",))
            for c in range(1, 9):
                V(lambda e, gg=g, cc=c: e.tensor_tensor(
                    snaps[:, gg * 9 + cc, :], snaps[:, gg * 9 + cc, :],
                    stpb[:, gg * 129:(gg + 1) * 129], ALU.add), ("stpb", "snaps"), ("snaps",))
        qfT = salloc("qfT", [128, 4, TOK], BF16)
        scb = salloc("scb", [128, 128], BF16)
        otr = salloc("otr", [128, 128], BF16)
        for g in range(4):
            load_wqt(wq_in, "wq_in", g, 4)
            for tt in range(NT):
                for k in range(16):
                    T(lambda e, kk=k, t=tt: e.matmul(
                        pb[0][:, :], xT[:, kk, t * 128:(t + 1) * 128], wqt[:, kk, :],
                        start=(kk == 0), stop=(kk == 15)), ("wqt", "xT"), ("pb0",))
                V(lambda e: e.tensor_copy(qtile[:, :], pb[0][:, :]), ("pb0",), ("qtile",))
                rope_elu(4, tt)
                for hh in range(4):
                    b, bn = pt[hh % 3], f"pt{hh % 3}"
                    T(lambda e, h2=hh, bb=b: e.transpose(bb[:, :], sBF[:, h2, :], idbf[:, :]),
                      ("sBF", "idbf"), (bn,))
                    V(lambda e, h2=hh, t=tt, bb=b: e.tensor_copy(
                        qfT[:, h2, t * 128:(t + 1) * 128], bb[:, :]), (bn,), ("qfT",))
            for hh in range(4):
                h = g * 4 + hh
                for c in range(NT):
                    T(lambda e, gg=g, h2=hh, cc=c: e.matmul(
                        pb[2][:, 0:128], kfT[:, gg, cc * 128:(cc + 1) * 128],
                        qfT[:, h2, cc * 128:(cc + 1) * 128], start=True, stop=True),
                        ("kfT", "qfT"), ("pb2",))
                    V(lambda e: e.tensor_tensor(scb[:, :], pb[2][:, 0:128], maskT[:, :],
                                                ALU.mult), ("pb2", "maskT"), ("scb",))
                    T(lambda e, gg=g, cc=c: e.matmul(
                        pb[3][:, 0:129], scb[:, :], vaug[:, cc, gg * 129:(gg + 1) * 129],
                        start=True, stop=False), ("scb", "vaug"), ("pb3",))
                    T(lambda e, gg=g, h2=hh, cc=c: e.matmul(
                        pb[3][:, 0:129], qfT[:, h2, cc * 128:(cc + 1) * 128],
                        snaps[:, gg * 9 + cc, :], start=False, stop=True),
                        ("qfT", "snaps"), ("pb3",))
                    V(lambda e: e.tensor_scalar(stats[:, 4:5], pb[3][:, 128:129], 1e-6, None,
                                                ALU.add), ("pb3",), ("stats",))
                    V(lambda e: e.reciprocal(stats[:, 5:6], stats[:, 4:5]), ("stats",), ("stats",))
                    V(lambda e: e.tensor_scalar(scb[:, :], pb[3][:, 0:128], stats[:, 5:6],
                                                None, ALU.mult), ("pb3", "stats"), ("scb",))
                    T(lambda e: e.transpose(pt[2][:, :], scb[:, :], idbf[:, :]),
                      ("scb", "idbf"), ("pt2",))
                    S(lambda e: e.copy(otr[:, :], pt[2][:, :]), ("pt2",), ("otr",))
                    DMA(lambda e, h3=h, cc=c: e.dma_start(
                        out=st_ao.ap()[h3, :, cc * 128:(cc + 1) * 128], in_=otr[:, :]),
                        ("otr",), ("st_ao",), q="gpsimd")
        pop_scope()
        push_scope()
        wqt2 = salloc("wqt2", [128, 16, 512], BF16)
        DMA(lambda e: e.dma_start(out=xT[:, :, :],
                                  in_=st_ao.ap().rearrange("a p t -> p a t")),
            ("st_ao",), ("xT",))
        for mb in range(4):
            for r in range(4):
                DMA(lambda e, rr=r, m=mb: e.dma_start(
                    out=sA[:, :], in_=wo_in[(m * 4 + rr) * 128:(m * 4 + rr + 1) * 128, :]),
                    ("wo_in",), ("sA",))
                V(lambda e: e.tensor_copy(sBF[:, :, :].rearrange("p a b -> p (a b)"), sA[:, :]),
                  ("sA",), ("sBF",))
                for k in range(16):
                    b, bn = pt[k % 3], f"pt{k % 3}"
                    T(lambda e, kk=k, bb=b: e.transpose(bb[:, :], sBF[:, kk, :], idbf[:, :]),
                      ("sBF", "idbf"), (bn,))
                    V(lambda e, kk=k, rr=r, bb=b: e.tensor_copy(
                        wqt2[:, kk, rr * 128:(rr + 1) * 128], bb[:, :]), (bn,), ("wqt2",))
            for tt in range(NT):
                pbk, pbn = pb[tt % 2], f"pb{tt % 2}"
                for k in range(16):
                    T(lambda e, kk=k, t=tt, p=pbk: e.matmul(
                        p[:, :], xT[:, kk, t * 128:(t + 1) * 128], wqt2[:, kk, :],
                        start=(kk == 0), stop=(kk == 15)), ("wqt2", "xT"), (pbn,))
                V(lambda e, t=tt, m=mb, p=pbk: e.scalar_tensor_tensor(
                    x[:, t, m * 512:(m + 1) * 512], p[:, :], 1.0,
                    x[:, t, m * 512:(m + 1) * 512], ALU.mult, ALU.add), (pbn, "x"), ("x",))
        pop_scope()

    stagefns = [l0_ffn, mingru, l0_ffn, l1_ffn, l1_attn, l1_ffn]
    for i in range(min(upto, 6)):
        stagefns[i]()

    DMA(lambda e: e.dma_start(out=xout.rearrange("(t p) d -> p t d", p=128),
                              in_=x[:, :, :]), ("x",), ("xout_d",))
    _wait("sync", "ld", cnt["ld"])
    barrier()

    with nc.Block() as block:
        @block.sync
        def _(e):
            for f in ops["sync"]:
                f(e)

        @block.gpsimd
        def _(e):
            for f in ops["gpsimd"]:
                f(e)

        @block.tensor
        def _(e):
            for f in ops["tensor"]:
                f(e)

        @block.vector
        def _(e):
            for f in ops["vector"]:
                f(e)

        @block.scalar
        def _(e):
            for f in ops["scalar"]:
                f(e)

    for cm in reversed(top_cms):
        cm.__exit__(None, None, None)
    return nc


def _rope_tables(pos0, n, dim=128):
    inv = 1.0 / (10000.0 ** (np.arange(0, dim, 2, dtype=np.float32) / dim))
    t = np.arange(pos0, pos0 + n, dtype=np.float32)
    fr = np.outer(t, inv)
    return np.cos(fr).astype(np.float32), np.sin(fr).astype(np.float32)


def kernel(L0_attn_x, L0_attn_w_hg, L0_attn_w_out, L0_ffn_w_gate_up, L0_ffn_w_down,
           L1_attn_Wq, L1_attn_Wk, L1_attn_Wv, L1_attn_Wo, L1_ffn_weight, upto=6):
    wl = np.asarray(L1_ffn_weight)
    if wl.dtype == np.uint16:
        wl = wl.view(BF)
    wl = np.ascontiguousarray(wl)
    idbf = np.eye(128, dtype=BF)
    idf = np.eye(128, dtype=np.float32)
    maskT = np.triu(np.ones((128, 128), dtype=np.float32))
    common = dict(
        w_hg=np.ascontiguousarray(L0_attn_w_hg, np.float32),
        w_out=np.ascontiguousarray(L0_attn_w_out, np.float32),
        w_gu=np.ascontiguousarray(L0_ffn_w_gate_up, np.float32),
        w_dn=np.ascontiguousarray(L0_ffn_w_down, np.float32),
        wq_in=np.ascontiguousarray(L1_attn_Wq, np.float32),
        wk_in=np.ascontiguousarray(L1_attn_Wk, np.float32),
        wv_in=np.ascontiguousarray(L1_attn_Wv, np.float32),
        wo_in=np.ascontiguousarray(L1_attn_Wo, np.float32),
        wl_in=wl, idbf_in=idbf, idf_in=idf, maskT_in=maskT,
        sel_in=np.kron(np.eye(4, dtype=np.float32), np.ones((1, 128), np.float32)),
    )
    xf = np.ascontiguousarray(L0_attn_x, np.float32)
    in_maps = []
    for core in range(8):
        b, half = core // 2, core % 2
        cos, sin = _rope_tables(half * TOK, TOK)
        m = dict(common)
        m["xin"] = np.ascontiguousarray(xf[b, half * TOK:(half + 1) * TOK, :])
        m["cos_in"] = cos
        m["sin_in"] = sin
        m["role_in"] = np.full((128, 1), float(half), np.float32)
        in_maps.append(m)
    nc = build(upto)
    res = run_bass_kernel_spmd(nc, in_maps, core_ids=list(range(8)))
    out = np.empty((4, 2048, 2048), np.float32)
    for core in range(8):
        b, half = core // 2, core % 2
        out[b, half * TOK:(half + 1) * TOK, :] = res.results[core]["xout"]
    return out


# version 03
# speedup vs baseline: 1.8039x; 1.8039x over previous
import numpy as np
import ml_dtypes
import concourse.bass as bass
import concourse.mybir as mybir
from concourse.bass_utils import run_bass_kernel_spmd

F32 = mybir.dt.float32
BF16 = mybir.dt.bfloat16
I32 = mybir.dt.int32
ALU = mybir.AluOpType
ACT = mybir.ActivationFunctionType
AX = mybir.AxisListType.X
BF = ml_dtypes.bfloat16

TOK = 1024
NT = 8
D = 2048
DI = 3072
IFF = 5632
NO1 = 88
NI2 = 44
MAGIC = 12582912.0
MASK_HI = int(np.int32(np.uint32(0xFFF00000)))

ENG = ("sync", "gpsimd", "tensor", "vector", "scalar")
KEY = {"sync": "ld", "gpsimd": "gd", "tensor": "t", "vector": "v", "scalar": "s"}
INC = {"ld": 16, "gd": 16, "t": 1, "v": 1, "s": 1, "g": 1}


PHASES = 99


def build(upto=6):
    nc = bass.Bass("TRN2", target_bir_lowering=False)

    xin = nc.dram_tensor("xin", [TOK, D], F32, kind="ExternalInput").ap()
    w_hg = nc.dram_tensor("w_hg", [D, 2 * DI], F32, kind="ExternalInput").ap()
    w_out = nc.dram_tensor("w_out", [DI, D], F32, kind="ExternalInput").ap()
    w_gu = nc.dram_tensor("w_gu", [2 * IFF, D], F32, kind="ExternalInput").ap()
    w_dn = nc.dram_tensor("w_dn", [D, IFF], F32, kind="ExternalInput").ap()
    wq_in = nc.dram_tensor("wq_in", [D, D], F32, kind="ExternalInput").ap()
    wk_in = nc.dram_tensor("wk_in", [512, D], F32, kind="ExternalInput").ap()
    wv_in = nc.dram_tensor("wv_in", [512, D], F32, kind="ExternalInput").ap()
    wo_in = nc.dram_tensor("wo_in", [D, D], F32, kind="ExternalInput").ap()
    wl_in = nc.dram_tensor("wl_in", [D, D], BF16, kind="ExternalInput").ap()
    idbf_in = nc.dram_tensor("idbf_in", [128, 128], BF16, kind="ExternalInput").ap()
    idf_in = nc.dram_tensor("idf_in", [128, 128], F32, kind="ExternalInput").ap()
    maskT_in = nc.dram_tensor("maskT_in", [128, 128], F32, kind="ExternalInput").ap()
    cos_in = nc.dram_tensor("cos_in", [TOK, 64], F32, kind="ExternalInput").ap()
    sin_in = nc.dram_tensor("sin_in", [TOK, 64], F32, kind="ExternalInput").ap()
    role_in = nc.dram_tensor("role_in", [128, 1], F32, kind="ExternalInput").ap()
    sel_in = nc.dram_tensor("sel_in", [4, 512], F32, kind="ExternalInput").ap()
    xout = nc.dram_tensor("xout", [TOK, D], F32, kind="ExternalOutput").ap()

    st_w1 = nc.dram_tensor("st_w1", [NO1, 128, D], BF16)
    st_w2 = nc.dram_tensor("st_w2", [NI2, 128, D], BF16)
    st_wl = nc.dram_tensor("st_wl", [16, 128, D], BF16)
    st_ao = nc.dram_tensor("st_ao", [16, 128, TOK], BF16)
    sp_h = nc.dram_tensor("sp_h", [24, 128, TOK], BF16)
    sp_cp = nc.dram_tensor("sp_cp", [24, 128, TOK], BF16)
    ag1_in = nc.dram_tensor("ag1_in", [128, 24], F32)
    ag1_out = nc.dram_tensor("ag1_out", [256, 24], F32)
    ag2_in = nc.dram_tensor("ag2_in", [128, 516], F32)
    ag2_out = nc.dram_tensor("ag2_out", [256, 516], F32)
    ar_in = nc.dram_tensor("ar_in", [128, 1], F32)
    ar_out = nc.dram_tensor("ar_out", [128, 1], F32)

    ops = {e: [] for e in ENG}
    cnt = {k: 0 for k in INC}
    waited = {e: {} for e in ENG}
    last_w = {}
    readers = {}
    sem = {}

    def _wait(eng, sk, val):
        if val <= 0 or waited[eng].get(sk, 0) >= val:
            return
        waited[eng][sk] = val
        ops[eng].append(lambda e, s=sk, v=val: e.wait_ge(sem[s], v))

    def emit(eng, fn, ins=(), outs=(), semkey=None, inc=None):
        mykey = semkey or KEY[eng]
        myinc = inc if inc is not None else INC[mykey]
        for nm in ins:
            lw = last_w.get(nm)
            if lw is not None and (lw[0] != mykey or eng != "tensor"):
                _wait(eng, lw[0], lw[1])
        for nm in outs:
            lw = last_w.get(nm)
            if lw is not None and lw[0] != mykey:
                _wait(eng, lw[0], lw[1])
            for sk, v in readers.get(nm, {}).items():
                if sk != mykey:
                    _wait(eng, sk, v)
        cnt[mykey] += myinc
        myval = cnt[mykey]
        ops[eng].append(lambda e, f=fn, s=mykey, i=myinc: f(e).then_inc(sem[s], i))
        for nm in outs:
            last_w[nm] = (mykey, myval)
            readers.pop(nm, None)
        for nm in ins:
            readers.setdefault(nm, {})[mykey] = myval
        return myval

    def barrier():
        for e in ENG:
            for sk in INC:
                if sk != KEY[e]:
                    _wait(e, sk, cnt[sk])

    def T(fn, ins=(), outs=()):
        emit("tensor", fn, ins, outs)

    def V(fn, ins=(), outs=()):
        emit("vector", fn, ins, outs)

    def S(fn, ins=(), outs=()):
        emit("scalar", fn, ins, outs)

    def G(fn, ins=(), outs=()):
        emit("gpsimd", fn, ins, outs, semkey="g", inc=1)

    def DMA(fn, ins=(), outs=(), q="sync"):
        emit(q, fn, ins, outs)

    scopes = []
    top_cms = []

    _uid = [0]

    def salloc(name, shape, dt):
        _uid[0] += 1
        cm = nc.sbuf_tensor(f"{name}_{_uid[0]}", shape, dt)
        t = cm.__enter__()
        (scopes[-1] if scopes else top_cms).append(cm)
        return t

    def push_scope():
        scopes.append([])

    def pop_scope():
        barrier()
        for cm in reversed(scopes.pop()):
            cm.__exit__(None, None, None)

    x = salloc("x", [128, NT, D], F32)
    xT = salloc("xT", [128, 16, TOK], BF16)
    y = salloc("y", [128, D], F32)
    sA = salloc("sA", [128, D], F32)
    sBF = salloc("sBF", [128, 16, 128], BF16)
    stats = salloc("stats", [128, 16], F32)
    idbf = salloc("idbf", [128, 128], BF16)
    idf = salloc("idf", [128, 128], F32)
    maskT = salloc("maskT", [128, 128], F32)
    cosb = salloc("cosb", [128, NT, 64], F32)
    sinb = salloc("sinb", [128, NT, 64], F32)
    role = salloc("role", [128, 1], F32)
    onesf = salloc("onesf", [1, 128], F32)
    ceps = salloc("ceps", [128, 2], F32)
    hend = salloc("hend", [128, 24], F32)
    hendp = salloc("hendp", [128, 24], F32)
    selc = salloc("selc", [4, 512], F32)

    def palloc(name, shape, dt):
        cm = nc.psum_tensor(name, shape, dt)
        t = cm.__enter__()
        top_cms.append(cm)
        return t

    pb = [palloc(f"pb{i}", [128, 512], F32) for i in range(5)]
    pt = [palloc(f"pt{i}", [128, 128], BF16) for i in range(3)]

    for sk in INC:
        cm = nc.semaphore("sem_" + sk)
        sem[sk] = cm.__enter__()
        top_cms.append(cm)

    def ln_tile(tt, out_ap, out_name):
        sa = x[:, tt, :]
        V(lambda e, a=sa: e.reduce_sum(stats[:, 0:1], a, AX), ("x",), ("stats",))
        S(lambda e, a=sa: e.activation(y[:, :], a, ACT.Square, accum_out=stats[:, 1:2]),
          ("x",), ("y", "stats"))
        V(lambda e: e.tensor_scalar(stats[:, 0:1], stats[:, 0:1], -1.0 / D, None, ALU.mult),
          ("stats",), ("stats",))
        V(lambda e: e.tensor_scalar(stats[:, 1:2], stats[:, 1:2], 1.0 / D, None, ALU.mult),
          ("stats",), ("stats",))
        V(lambda e: e.tensor_tensor(stats[:, 2:3], stats[:, 0:1], stats[:, 0:1], ALU.mult),
          ("stats",), ("stats",))
        V(lambda e: e.tensor_tensor(stats[:, 1:2], stats[:, 1:2], stats[:, 2:3], ALU.subtract),
          ("stats",), ("stats",))
        S(lambda e: e.activation(stats[:, 3:4], stats[:, 1:2], ACT.Sqrt, ceps[:, 0:1], 1.0),
          ("stats", "ceps"), ("stats",))
        V(lambda e: e.reciprocal(stats[:, 3:4], stats[:, 3:4]), ("stats",), ("stats",))
        S(lambda e: e.mul(stats[:, 2:3], stats[:, 0:1], stats[:, 3:4]), ("stats",), ("stats",))
        S(lambda e, o=out_ap, a=sa: e.activation(o, a, ACT.Identity, stats[:, 2:3], stats[:, 3:4]),
          ("x", "stats"), (out_name,))

    def tp16(src_name, dst, dst_name, col0, src=None):
        s = sBF if src is None else src
        for k in range(16):
            b, bn = pt[k % 3], f"pt{k % 3}"
            T(lambda e, kk=k, bb=b, ss=s: e.transpose(bb[:, :], ss[:, kk, :], idbf[:, :]),
              (src_name, "idbf"), (bn,))
            V(lambda e, kk=k, bb=b, d=dst, c=col0: e.tensor_copy(d[:, kk, c:c + 128], bb[:, :]),
              (bn,), (dst_name,))

    def rne_clip(ap, name, qmax=127.0):
        V(lambda e, a=ap: e.tensor_scalar(a, a, MAGIC, None, ALU.add), (name,), (name,))
        V(lambda e, a=ap: e.tensor_scalar(a, a, -MAGIC, None, ALU.add), (name,), (name,))
        V(lambda e, a=ap, q=qmax: e.tensor_scalar(a, a, q, -q, ALU.min, ALU.max),
          (name,), (name,))

    def e4m3_round(fap, iap, nf, ni):
        V(lambda e: e.tensor_scalar(iap, fap.bitcast(I32), 20, None,
                                    ALU.logical_shift_right), (nf,), (ni,))
        V(lambda e: e.tensor_scalar(iap, iap, 1, None, ALU.bitwise_and), (ni,), (ni,))
        V(lambda e: e.tensor_tensor(iap, iap, fap.bitcast(I32), ALU.add), (nf, ni), (ni,))
        V(lambda e: e.tensor_scalar(iap, iap, 0x0007FFFF, None, ALU.add), (ni,), (ni,))
        V(lambda e: e.tensor_scalar(iap, iap, MASK_HI, None, ALU.bitwise_and), (ni,), (ni,))
        V(lambda e: e.tensor_copy(fap, iap.bitcast(F32)), (ni,), (nf,))

    DMA(lambda e: e.dma_start(out=idbf[:, :], in_=idbf_in), (), ("idbf",))
    DMA(lambda e: e.dma_start(out=idf[:, :], in_=idf_in), (), ("idf",))
    DMA(lambda e: e.dma_start(out=maskT[:, :], in_=maskT_in), (), ("maskT",))
    DMA(lambda e: e.dma_start(out=cosb[:, :, :],
                              in_=cos_in.rearrange("(t p) c -> p t c", p=128)), (), ("cosb",))
    DMA(lambda e: e.dma_start(out=sinb[:, :, :],
                              in_=sin_in.rearrange("(t p) c -> p t c", p=128)), (), ("sinb",))
    DMA(lambda e: e.dma_start(out=role[:, :], in_=role_in), (), ("role",))
    DMA(lambda e: e.dma_start(out=x[:, :, :],
                              in_=xin.rearrange("(t p) d -> p t d", p=128)), (), ("x",))
    V(lambda e: e.memset(onesf[:, :], 1.0), (), ("onesf",))
    DMA(lambda e: e.dma_start(out=selc[:, :], in_=sel_in), (), ("selc",))
    V(lambda e: e.memset(ceps[:, 0:1], 1e-5), (), ("ceps",))

    push_scope()
    wrow = salloc("wrow", [128, IFF], F32)
    wqd = salloc("wqd", [128, IFF], BF16)
    sbT = salloc("sbT", [128, NI2, 128], BF16)
    wli = salloc("wli", [128, D], I32)
    wlrow = salloc("wlrow", [128, D], BF16)

    def wq_tile(src, sname, ncols, bi, ot):
        b, bn = wrow, "wrow"
        DMA(lambda e, o=ot, n=ncols, bb=b: e.dma_start(
            out=bb[:, 0:n], in_=src[o * 128:(o + 1) * 128, 0:n]),
            (sname,), (bn, bn + "L", bn + "H"))
        V(lambda e, n=ncols, bb=b: e.tensor_reduce(stats[:, 4:5], bb[:, 0:n], AX, ALU.max,
                                                   apply_absolute_value=True),
          (bn, bn + "L", bn + "H"), ("stats",))
        V(lambda e: e.tensor_scalar(stats[:, 5:6], stats[:, 4:5], 1.0 / 127.0, 1e-8,
                                    ALU.mult, ALU.max), ("stats",), ("stats",))
        V(lambda e: e.reciprocal(stats[:, 6:7], stats[:, 5:6]), ("stats",), ("stats",))
        nh = ncols // 2
        for EE, c0, c1, sfx in ((V, 0, nh, "L"), (V, nh, ncols, "H")):
            hn, wn = bn + sfx, "wqd" + sfx
            EE(lambda e, bb=b, a=c0, z=c1: e.tensor_scalar(bb[:, a:z], bb[:, a:z], stats[:, 6:7],
                                                           None, ALU.mult), (hn, "stats"), (hn,))
            EE(lambda e, bb=b, a=c0, z=c1: e.tensor_scalar(bb[:, a:z], bb[:, a:z], MAGIC, None,
                                                           ALU.add), (hn,), (hn,))
            EE(lambda e, bb=b, a=c0, z=c1: e.tensor_scalar(bb[:, a:z], bb[:, a:z], -MAGIC, None,
                                                           ALU.add), (hn,), (hn,))
            EE(lambda e, bb=b, a=c0, z=c1: e.tensor_scalar(bb[:, a:z], bb[:, a:z], 127.0, -127.0,
                                                           ALU.min, ALU.max), (hn,), (hn,))
            EE(lambda e, bb=b, a=c0, z=c1: e.tensor_scalar(wqd[:, a:z], bb[:, a:z], stats[:, 5:6],
                                                           None, ALU.mult), (hn, "stats"), (wn,))

    def stash_tp(nk, store_fn):
        for k in range(nk):
            b, bn = pt[k % 3], f"pt{k % 3}"
            half = "wqdL" if k < nk // 2 else "wqdH"
            T(lambda e, kk=k, bb=b: e.transpose(bb[:, :], wqd[:, kk * 128:(kk + 1) * 128], idbf[:, :]),
              (half, "idbf"), (bn,))
            S(lambda e, kk=k, bb=b: e.copy(sbT[:, kk, :], bb[:, :]), (bn,), ("sbT",))
        store_fn()

    for ot in range(NO1):
        wq_tile(w_gu, "w_gu", D, ot % 2, ot)
        stash_tp(16, lambda o=ot: DMA(lambda e: e.dma_start(
            out=st_w1.ap()[o].rearrange("p (a b) -> p a b", a=16), in_=sbT[:, 0:16, :]),
            ("sbT",), ("st_w1",), q="gpsimd"))
    for ot in range(16):
        wq_tile(w_dn, "w_dn", IFF, ot % 2, ot)

        def store_w2(o=ot):
            for h in range(2):
                DMA(lambda e, hh=h: e.dma_start(
                    out=st_w2.ap()[hh * 22:(hh + 1) * 22, :, o * 128:(o + 1) * 128]
                        .rearrange("a p b -> p a b"),
                    in_=sbT[:, hh * 22:(hh + 1) * 22, :]), ("sbT",), ("st_w2",), q="gpsimd")
        stash_tp(NI2, store_w2)

    V(lambda e: e.memset(stats[:, 8:9], 0.0), (), ("stats",))
    for ot in range(16):
        DMA(lambda e, o=ot: e.dma_start(
            out=wlrow[:, :], in_=wl_in[o * 128:(o + 1) * 128, :]),
            ("wl_in",), ("wlrow",))
        V(lambda e: e.tensor_reduce(stats[:, 9:10], wlrow[:, :],
                                    AX, ALU.max, apply_absolute_value=True),
          ("wlrow",), ("stats",))
        V(lambda e: e.tensor_tensor(stats[:, 8:9], stats[:, 8:9], stats[:, 9:10], ALU.max),
          ("stats",), ("stats",))
    T(lambda e: e.transpose(pb[4][0:1, 0:128], stats[:, 8:9], idf[:, :]), ("stats", "idf"), ("pb4",))
    V(lambda e: e.reduce_max(stats[0:1, 9:10], pb[4][0:1, 0:128], AX), ("pb4",), ("stats",))
    T(lambda e: e.matmul(pb[4][:, 0:1], onesf[0:1, :], stats[0:1, 9:10], start=True, stop=True),
      ("onesf", "stats"), ("pb4",))
    V(lambda e: e.tensor_scalar(stats[:, 12:13], pb[4][:, 0:1], 1.0 / 448.0, 1e-12,
                                ALU.mult, ALU.max), ("pb4",), ("stats",))
    V(lambda e: e.reciprocal(stats[:, 13:14], stats[:, 12:13]), ("stats",), ("stats",))
    for ot in range(16):
        DMA(lambda e, o=ot: e.dma_start(
            out=wlrow[:, :], in_=wl_in[o * 128:(o + 1) * 128, :]),
            ("wl_in",), ("wlrow",))
        V(lambda e: e.tensor_scalar(sA[:, :], wlrow[:, :],
                                    stats[:, 13:14], None, ALU.mult),
          ("wlrow", "stats"), ("sA",))
        e4m3_round(sA[:, :], wli[:, :], "sA", "wli")
        V(lambda e: e.tensor_copy(wqd[:, 0:D], sA[:, :]), ("sA",), ("wqd", "wqdL", "wqdH"))
        stash_tp(16, lambda o=ot: DMA(lambda e: e.dma_start(
            out=st_wl.ap()[:, :, o * 128:(o + 1) * 128].rearrange("a p b -> p a b"),
            in_=sbT[:, 0:16, :]), ("sbT",), ("st_wl",), q="gpsimd"))
    pop_scope()

    def l0_ffn():
        push_scope()
        hsb = salloc("hsb", [128, NI2, 512], BF16)
        for half in range(2 if PHASES >= 5 else 1):
            tts = list(range(half * 4, half * 4 + 4))
            push_scope()
            w1g2 = [salloc("w1g", [128, 16, 128], BF16) for _ in range(2)]
            w1u2 = [salloc("w1u", [128, 16, 128], BF16) for _ in range(2)]
            sB = salloc("sB", [128, 512], F32)
            sC = salloc("sC", [128, 512], F32)
            sCg = salloc("sCg", [128, 512], F32)
            hax = salloc("hax", [128, 4], F32)
            hrecT = salloc("hrecT", [4, 128], F32)
            hsclT = salloc("hsclT", [4, 128], F32)
            recb = salloc("recb", [128, 512], F32)
            sclb = salloc("sclb", [128, 512], F32)
            for tt in tts:
                ln_tile(tt, y[:, :], "y")
                V(lambda e: e.tensor_reduce(stats[:, 4:5], y[:, :], AX, ALU.max,
                                            apply_absolute_value=True), ("y",), ("stats",))
                V(lambda e: e.tensor_scalar(stats[:, 5:6], stats[:, 4:5], 1.0 / 127.0, 1e-8,
                                            ALU.mult, ALU.max), ("stats",), ("stats",))
                V(lambda e: e.reciprocal(stats[:, 6:7], stats[:, 5:6]), ("stats",), ("stats",))
                V(lambda e: e.tensor_scalar(sA[:, :], y[:, :], stats[:, 6:7], None, ALU.mult),
                  ("y", "stats"), ("sA",))
                rne_clip(sA[:, :], "sA")
                V(lambda e: e.tensor_scalar(sBF[:, :, :].rearrange("p a b -> p (a b)"),
                                            sA[:, :], stats[:, 5:6], None, ALU.mult),
                  ("sA", "stats"), ("sBF",))
                tp16("sBF", xT, "xT", (tt - half * 4) * 128)
            V(lambda e: e.memset(hax[:, :], 0.0), (), ("hax",))
            for mp in range(NI2 if PHASES >= 2 else 0):
                bg, bgn = w1g2[mp % 2], f"w1g{mp % 2}"
                bu, bun = w1u2[mp % 2], f"w1u{mp % 2}"
                DMA(lambda e, m=mp, b=bg: e.dma_start(
                    out=b[:, :, :], in_=st_w1.ap()[m].rearrange("p (a b) -> p a b", a=16)),
                    ("st_w1",), (bgn,))
                DMA(lambda e, m=mp, b=bu: e.dma_start(
                    out=b[:, :, :], in_=st_w1.ap()[m + NI2].rearrange("p (a b) -> p a b", a=16)),
                    ("st_w1",), (bun,))
                for k in range(16):
                    T(lambda e, kk=k, b=bg: e.matmul(
                        pb[0][:, :], b[:, kk, :], xT[:, kk, 0:512],
                        start=(kk == 0), stop=(kk == 15)), (bgn, "xT"), ("pb0",))
                for k in range(16):
                    T(lambda e, kk=k, b=bu: e.matmul(
                        pb[1][:, :], b[:, kk, :], xT[:, kk, 0:512],
                        start=(kk == 0), stop=(kk == 15)), (bun, "xT"), ("pb1",))
                S(lambda e: e.activation(sB[:, :], pb[0][:, :], ACT.Silu), ("pb0",), ("sB",))
                V(lambda e, m=mp: e.tensor_tensor(hsb[:, m, :], sB[:, :], pb[1][:, :], ALU.mult),
                  ("sB", "pb1"), ("hsb", "hsb0", "hsb1"))
                for c in range(4):
                    b, bn = pt[c % 3], f"pt{c % 3}"
                    T(lambda e, m=mp, cc=c, bb=b: e.transpose(
                        bb[:, :], hsb[:, m, cc * 128:(cc + 1) * 128], idbf[:, :]),
                        ("hsb", "idbf"), (bn,))
                    V(lambda e, bb=b: e.tensor_reduce(stats[:, 7:8], bb[:, :], AX, ALU.max,
                                                      apply_absolute_value=True),
                      (bn,), ("stats",))
                    V(lambda e, cc=c: e.tensor_tensor(hax[:, cc:cc + 1], hax[:, cc:cc + 1],
                                                      stats[:, 7:8], ALU.max),
                      ("stats", "hax"), ("hax",))
            V(lambda e: e.tensor_scalar(hax[:, 0:4], hax[:, 0:4], 1.0 / 127.0, 1e-8,
                                        ALU.mult, ALU.max), ("hax",), ("hax",))
            T(lambda e: e.transpose(pb[4][0:4, 0:128], hax[:, 0:4], idf[:, :]), ("hax", "idf"), ("pb4",))
            V(lambda e: e.tensor_copy(hsclT[0:4, :], pb[4][0:4, 0:128]), ("pb4",), ("hsclT",))
            V(lambda e: e.reciprocal(hrecT[0:4, :], hsclT[0:4, :]), ("hsclT",), ("hrecT",))
            for c in range(4):
                T(lambda e, cc=c: e.matmul(pb[4][:, cc * 128:(cc + 1) * 128],
                                           selc[0:4, cc * 128:(cc + 1) * 128],
                                           hrecT[0:4, :], start=True, stop=True),
                  ("selc", "hrecT"), ("pb4",))
            V(lambda e: e.tensor_copy(recb[:, :], pb[4][:, 0:512]), ("pb4",), ("recb",))
            for c in range(4):
                T(lambda e, cc=c: e.matmul(pb[4][:, cc * 128:(cc + 1) * 128],
                                           selc[0:4, cc * 128:(cc + 1) * 128],
                                           hsclT[0:4, :], start=True, stop=True),
                  ("selc", "hsclT"), ("pb4",))
            V(lambda e: e.tensor_copy(sclb[:, :], pb[4][:, 0:512]), ("pb4",), ("sclb",))
            for mp in range(NI2 if PHASES >= 3 else 0):
                EE = V
                tb = sC if mp % 2 == 0 else sCg
                tn = "sC" if mp % 2 == 0 else "sCg"
                hn = f"hsb{mp % 2}"
                EE(lambda e, m=mp, t=tb: e.tensor_copy(t[:, :], hsb[:, m, :]), (hn, "hsb"), (tn,))
                EE(lambda e, t=tb: e.tensor_tensor(t[:, :], t[:, :], recb[:, :], ALU.mult),
                   (tn, "recb"), (tn,))
                EE(lambda e, t=tb: e.tensor_scalar(t[:, :], t[:, :], MAGIC, None, ALU.add),
                   (tn,), (tn,))
                EE(lambda e, t=tb: e.tensor_scalar(t[:, :], t[:, :], -MAGIC, None, ALU.add),
                   (tn,), (tn,))
                EE(lambda e, t=tb: e.tensor_scalar(t[:, :], t[:, :], 127.0, -127.0,
                                                   ALU.min, ALU.max), (tn,), (tn,))
                EE(lambda e, t=tb: e.tensor_tensor(t[:, :], t[:, :], sclb[:, :], ALU.mult),
                   (tn, "sclb"), (tn,))
                EE(lambda e, m=mp, t=tb: e.tensor_copy(hsb[:, m, :], t[:, :]), (tn,), (hn,))
            pop_scope()
            push_scope()
            w2b = salloc("w2b", [128, NI2, 256], BF16)
            for nch in range(8 if PHASES >= 4 else 0):
                b, bn = w2b, "w2b"
                for h2 in range(2):
                    DMA(lambda e, n=nch, hh=h2, bb=b: e.dma_start(
                        out=bb[:, hh * 22:(hh + 1) * 22, :],
                        in_=st_w2.ap()[hh * 22:(hh + 1) * 22, :, n * 256:(n + 1) * 256]
                            .rearrange("a p b -> p a b")), ("st_w2",), (bn,))
                for i, tt in enumerate(tts):
                    pbk, pbn = pb[i % 2], f"pb{i % 2}"
                    for k in range(NI2):
                        T(lambda e, kk=k, lt=tt - half * 4, bb=b, p=pbk: e.matmul(
                            p[:, 0:256], hsb[:, kk, lt * 128:(lt + 1) * 128], bb[:, kk, :],
                            start=(kk == 0), stop=(kk == NI2 - 1)), (bn, "hsb0", "hsb1"), (pbn,))
                    V(lambda e, t=tt, n=nch, p=pbk: e.scalar_tensor_tensor(
                        x[:, t, n * 256:(n + 1) * 256], p[:, 0:256], 0.5,
                        x[:, t, n * 256:(n + 1) * 256], ALU.mult, ALU.add),
                      (pbn, "x"), ("x",))
            pop_scope()
        pop_scope()

    def mingru():
        push_scope()
        whf2 = [salloc("whf", [128, 16, 128], F32) for _ in range(2)]
        whb = [salloc(f"whb{i}", [128, 16, 128], BF16) for i in range(2)]
        wgb = [salloc(f"wgb{i}", [128, 16, 128], BF16) for i in range(2)]
        ct = salloc("ct", [128, TOK], F32)
        zt = salloc("zt", [128, TOK], F32)
        gt = salloc("gt", [128, TOK], F32)
        ht = salloc("ht", [128, TOK], F32)
        cp = salloc("cp", [128, TOK], F32)
        mki = salloc("mki", [128, TOK], I32)
        hbf = salloc("hbf", [128, TOK], BF16)
        cpb = salloc("cpb", [128, TOK], BF16)
        for tt in range(NT):
            ln_tile(tt, sBF[:, :, :].rearrange("p a b -> p (a b)"), "sBF")
            tp16("sBF", xT, "xT", tt * 128)
        for j in range(24):
            bh, bhn = whb[j % 2], f"whb{j % 2}"
            bg, bgn = wgb[j % 2], f"wgb{j % 2}"
            wf, wfn = whf2[j % 2], f"whf{j % 2}"
            DMA(lambda e, jj=j, w=wf: e.dma_start(
                out=w[:, :, :],
                in_=w_hg[:, jj * 128:(jj + 1) * 128].rearrange("(a p) c -> p a c", p=128)),
                ("w_hg",), (wfn,))
            V(lambda e, b=bh, w=wf: e.tensor_copy(
                b[:, :, :].rearrange("p a b -> p (a b)"),
                w[:, :, :].rearrange("p a b -> p (a b)")), (wfn,), (bhn,))
            wf2, wfn2 = whf2[(j + 1) % 2], f"whf{(j + 1) % 2}"
            DMA(lambda e, jj=j, w=wf2: e.dma_start(
                out=w[:, :, :],
                in_=w_hg[:, DI + jj * 128:DI + (jj + 1) * 128]
                    .rearrange("(a p) c -> p a c", p=128)), ("w_hg",), (wfn2,))
            V(lambda e, b=bg, w=wf2: e.tensor_copy(
                b[:, :, :].rearrange("p a b -> p (a b)"),
                w[:, :, :].rearrange("p a b -> p (a b)")), (wfn2,), (bgn,))
            for hf in range(2):
                for k in range(16):
                    T(lambda e, kk=k, h=hf, b=bh: e.matmul(
                        pb[h][:, :], b[:, kk, :], xT[:, kk, h * 512:(h + 1) * 512],
                        start=(kk == 0), stop=(kk == 15)), (bhn, "xT"), (f"pb{hf}",))
                for k in range(16):
                    T(lambda e, kk=k, h=hf, b=bg: e.matmul(
                        pb[2 + h][:, :], b[:, kk, :], xT[:, kk, h * 512:(h + 1) * 512],
                        start=(kk == 0), stop=(kk == 15)), (bgn, "xT"), (f"pb{2 + hf}",))
            for hf in range(2):
                sl = slice(hf * 512, (hf + 1) * 512)
                S(lambda e, h=hf, s=sl: e.activation(ct[:, s], pb[2 + h][:, :], ACT.Sigmoid,
                                                     0.0, -1.0), (f"pb{2 + hf}",), ("ct",))
                S(lambda e, h=hf, s=sl: e.activation(zt[:, s], pb[2 + h][:, :], ACT.Sigmoid),
                  (f"pb{2 + hf}",), ("zt",))
                S(lambda e, h=hf, s=sl: e.activation(gt[:, s], pb[h][:, :], ACT.Sigmoid),
                  (f"pb{hf}",), ("gt",))
                V(lambda e, h=hf, s=sl: e.tensor_scalar(ht[:, s], pb[h][:, :], 0.5, None,
                                                        ALU.add), (f"pb{hf}",), ("ht",))
                V(lambda e, h=hf, s=sl: e.tensor_scalar(mki[:, s], pb[h][:, :], 0.0, None,
                                                        ALU.is_ge), (f"pb{hf}",), ("mki",))
            V(lambda e: e.copy_predicated(gt[:, :], mki[:, :], ht[:, :]),
              ("mki", "ht", "gt"), ("gt",))
            V(lambda e: e.tensor_tensor(gt[:, :], zt[:, :], gt[:, :], ALU.mult),
              ("zt", "gt"), ("gt",))
            V(lambda e: e.tensor_tensor_scan(ht[:, 0:512], ct[:, 0:512], gt[:, 0:512],
                                             0.0, ALU.mult, ALU.add), ("ct", "gt"), ("ht",))
            V(lambda e: e.tensor_tensor_scan(ht[:, 512:1024], ct[:, 512:1024], gt[:, 512:1024],
                                             ht[:, 511:512], ALU.mult, ALU.add),
              ("ct", "gt", "ht"), ("ht",))
            V(lambda e: e.tensor_tensor_scan(cp[:, 0:512], ct[:, 0:512], gt[:, 0:512],
                                             1.0, ALU.mult, ALU.bypass), ("ct", "gt"), ("cp",))
            V(lambda e: e.tensor_tensor_scan(cp[:, 512:1024], ct[:, 512:1024], gt[:, 512:1024],
                                             cp[:, 511:512], ALU.mult, ALU.bypass),
              ("ct", "gt", "cp"), ("cp",))
            V(lambda e, jj=j: e.tensor_copy(hend[:, jj:jj + 1], ht[:, 1023:1024]),
              ("ht",), ("hend",))
            S(lambda e: e.copy(hbf[:, :], ht[:, :]), ("ht",), ("hbf",))
            S(lambda e: e.copy(cpb[:, :], cp[:, :]), ("cp",), ("cpb",))
            DMA(lambda e, jj=j: e.dma_start(out=sp_h.ap()[jj], in_=hbf[:, :]),
                ("hbf",), ("sp_h",), q="gpsimd")
            DMA(lambda e, jj=j: e.dma_start(out=sp_cp.ap()[jj], in_=cpb[:, :]),
                ("cpb",), ("sp_cp",), q="gpsimd")
        DMA(lambda e: e.dma_start(out=ag1_in.ap(), in_=hend[:, :]), ("hend",), ("ag1_in",),
            q="gpsimd")
        G(lambda e: e.collective_compute("AllGather", ALU.bypass,
                                         replica_groups=[[0, 1], [2, 3], [4, 5], [6, 7]],
                                         ins=[ag1_in.ap()], outs=[ag1_out.ap()]),
          ("ag1_in",), ("ag1_out",))
        DMA(lambda e: e.dma_start(out=hendp[:, :], in_=ag1_out.ap()[0:128, :]),
            ("ag1_out",), ("hendp",))
        V(lambda e: e.tensor_scalar(hendp[:, :], hendp[:, :], role[:, 0:1], None, ALU.mult),
          ("hendp", "role"), ("hendp",))
        pop_scope()
        push_scope()
        hfb = salloc("hfb", [128, TOK], BF16)
        cpb2 = salloc("cpb2", [128, TOK], BF16)
        sB2 = salloc("sB2", [128, TOK], F32)
        sC2 = salloc("sC2", [128, TOK], F32)
        for k in range(24):
            DMA(lambda e, kk=k: e.dma_start(out=hfb[:, :], in_=sp_h.ap()[kk]),
                ("sp_h",), ("hfb",), q="gpsimd")
            DMA(lambda e, kk=k: e.dma_start(out=cpb2[:, :], in_=sp_cp.ap()[kk]),
                ("sp_cp",), ("cpb2",), q="gpsimd")
            V(lambda e, kk=k: e.tensor_scalar(sB2[:, :], cpb2[:, :], hendp[:, kk:kk + 1],
                                              None, ALU.mult), ("cpb2", "hendp"), ("sB2",))
            V(lambda e: e.tensor_copy(sC2[:, :], hfb[:, :]), ("hfb",), ("sC2",))
            V(lambda e: e.tensor_tensor(sC2[:, :], sC2[:, :], sB2[:, :], ALU.add),
              ("sB2", "sC2"), ("sC2",))
            V(lambda e: e.tensor_copy(hfb[:, :], sC2[:, :]), ("sC2",), ("hfb",))
            DMA(lambda e, kk=k: e.dma_start(out=sp_h.ap()[kk], in_=hfb[:, :]),
                ("hfb",), ("sp_h",), q="gpsimd")
        pop_scope()
        push_scope()
        hfx = salloc("hfx", [128, 12, TOK], BF16)
        wob = salloc("wob", [128, 12, 512], BF16)
        wcf = salloc("wcf", [128, 512], F32)
        for nch in range(4):
            for kh in range(2):
                for k in range(12):
                    DMA(lambda e, kk=k, k0=kh * 12: e.dma_start(
                        out=hfx[:, kk, :], in_=sp_h.ap()[k0 + kk]),
                        ("sp_h",), ("hfx",), q="gpsimd")
                    DMA(lambda e, kk=k, k0=kh * 12, n=nch: e.dma_start(
                        out=wcf[:, :],
                        in_=w_out[(k0 + kk) * 128:(k0 + kk + 1) * 128,
                                  n * 512:(n + 1) * 512]), ("w_out",), ("wcf",))
                    V(lambda e, kk=k: e.tensor_copy(wob[:, kk, :], wcf[:, :]),
                      ("wcf",), ("wob",))
                for tt in range(NT):
                    pbk, pbn = pb[tt % 2], f"pb{tt % 2}"
                    for k in range(12):
                        T(lambda e, kk=k, t=tt, p=pbk: e.matmul(
                            p[:, :], hfx[:, kk, t * 128:(t + 1) * 128], wob[:, kk, :],
                            start=(kk == 0), stop=(kk == 11)), ("hfx", "wob"), (pbn,))
                    V(lambda e, t=tt, n=nch, p=pbk: e.scalar_tensor_tensor(
                        x[:, t, n * 512:(n + 1) * 512], p[:, :], 1.0,
                        x[:, t, n * 512:(n + 1) * 512], ALU.mult, ALU.add), (pbn, "x"), ("x",))
        pop_scope()

    def l1_ffn():
        push_scope()
        yall = salloc("yall", [128, NT, D], BF16)
        l1b2 = [salloc("l1b", [128, 16, 512], BF16) for _ in range(2)]
        sI = salloc("sI", [128, D], I32)
        obf = salloc("obf", [128, 512], BF16)
        sB3 = salloc("sB3", [128, 512], F32)
        V(lambda e: e.memset(stats[:, 8:9], 0.0), (), ("stats",))
        for tt in range(NT):
            ln_tile(tt, y[:, :], "y")
            V(lambda e: e.tensor_reduce(stats[:, 9:10], y[:, :], AX, ALU.max,
                                        apply_absolute_value=True), ("y",), ("stats",))
            V(lambda e: e.tensor_tensor(stats[:, 8:9], stats[:, 8:9], stats[:, 9:10], ALU.max),
              ("stats",), ("stats",))
            S(lambda e, t=tt: e.copy(yall[:, t, :], y[:, :]), ("y",), ("yall",))
        T(lambda e: e.transpose(pb[4][0:1, 0:128], stats[:, 8:9], idf[:, :]), ("stats", "idf"), ("pb4",))
        V(lambda e: e.reduce_max(stats[0:1, 9:10], pb[4][0:1, 0:128], AX), ("pb4",), ("stats",))
        T(lambda e: e.matmul(pb[4][:, 0:1], onesf[0:1, :], stats[0:1, 9:10],
                             start=True, stop=True), ("onesf", "stats"), ("pb4",))
        V(lambda e: e.tensor_copy(stats[:, 10:11], pb[4][:, 0:1]), ("pb4",), ("stats",))
        DMA(lambda e: e.dma_start(out=ar_in.ap(), in_=stats[:, 10:11]), ("stats",), ("ar_in",),
            q="gpsimd")
        G(lambda e: e.collective_compute("AllReduce", ALU.max,
                                         replica_groups=[list(range(8))],
                                         ins=[ar_in.ap()], outs=[ar_out.ap()]),
          ("ar_in",), ("ar_out",))
        DMA(lambda e: e.dma_start(out=stats[:, 10:11], in_=ar_out.ap()), ("ar_out",), ("stats",))
        V(lambda e: e.tensor_scalar(stats[:, 4:5], stats[:, 10:11], 1.0 / 448.0, 1e-12,
                                    ALU.mult, ALU.max), ("stats",), ("stats",))
        V(lambda e: e.reciprocal(stats[:, 5:6], stats[:, 4:5]), ("stats",), ("stats",))
        V(lambda e: e.tensor_tensor(stats[:, 6:7], stats[:, 4:5], stats[:, 12:13], ALU.mult),
          ("stats",), ("stats",))
        for tt in range(NT):
            V(lambda e, t=tt: e.tensor_scalar(sA[:, :], yall[:, t, :], stats[:, 5:6],
                                              None, ALU.mult), ("yall", "stats"), ("sA",))
            e4m3_round(sA[:, :], sI[:, :], "sA", "sI")
            V(lambda e: e.tensor_copy(sBF[:, :, :].rearrange("p a b -> p (a b)"), sA[:, :]),
              ("sA",), ("sBF",))
            tp16("sBF", xT, "xT", tt * 128)
        for nch in range(4):
            b, bn = l1b2[nch % 2], f"l1b{nch % 2}"
            DMA(lambda e, n=nch, bb=b: e.dma_start(
                out=bb[:, :, :],
                in_=st_wl.ap()[:, :, n * 512:(n + 1) * 512].rearrange("a p b -> p a b")),
                ("st_wl",), (bn,))
            for tt in range(NT):
                pbk, pbn = pb[tt % 2], f"pb{tt % 2}"
                for k in range(16):
                    T(lambda e, kk=k, t=tt, bb=b, p=pbk: e.matmul(
                        p[:, :], xT[:, kk, t * 128:(t + 1) * 128], bb[:, kk, :],
                        start=(kk == 0), stop=(kk == 15)), (bn, "xT"), (pbn,))
                V(lambda e, p=pbk: e.tensor_scalar(obf[:, :], p[:, :], stats[:, 6:7],
                                                   None, ALU.mult), (pbn, "stats"), ("obf",))
                V(lambda e: e.tensor_scalar(sB3[:, :], obf[:, :], 0.5, None, ALU.mult),
                  ("obf",), ("sB3",))
                V(lambda e, t=tt, n=nch: e.tensor_tensor(
                    x[:, t, n * 512:(n + 1) * 512], x[:, t, n * 512:(n + 1) * 512],
                    sB3[:, :], ALU.add), ("sB3", "x"), ("x",))
        pop_scope()

    def l1_attn():
        push_scope()
        kfT = salloc("kfT", [128, 4, TOK], BF16)
        ktok = salloc("ktok", [128, NT, 512], BF16)
        vaug = salloc("vaug", [128, NT, 516], BF16)
        wqt = salloc("wqt", [128, 16, 512], BF16)
        qtile = salloc("qtile", [128, 512], F32)
        mki2 = salloc("mki2", [128, 512], I32)
        for tt in range(NT):
            ln_tile(tt, sBF[:, :, :].rearrange("p a b -> p (a b)"), "sBF")
            tp16("sBF", xT, "xT", tt * 128)

        def load_wqt(src, sname, mb, nrows):
            for r in range(nrows):
                DMA(lambda e, rr=r, m=mb: e.dma_start(
                    out=sA[:, :], in_=src[(m * 4 + rr) * 128:(m * 4 + rr + 1) * 128, :]),
                    (sname,), ("sA",))
                V(lambda e: e.tensor_copy(sBF[:, :, :].rearrange("p a b -> p (a b)"), sA[:, :]),
                  ("sA",), ("sBF",))
                for k in range(16):
                    b, bn = pt[k % 3], f"pt{k % 3}"
                    T(lambda e, kk=k, bb=b: e.transpose(bb[:, :], sBF[:, kk, :], idbf[:, :]),
                      ("sBF", "idbf"), (bn,))
                    V(lambda e, kk=k, rr=r, bb=b: e.tensor_copy(
                        wqt[:, kk, rr * 128:(rr + 1) * 128], bb[:, :]), (bn,), ("wqt",))

        def rope_elu(nheads, tt):
            for hh in range(nheads):
                a = qtile[:, hh * 128:hh * 128 + 64]
                bq = qtile[:, hh * 128 + 64:hh * 128 + 128]
                V(lambda e, A=a: e.tensor_tensor(y[:, 0:64], A, cosb[:, tt, :], ALU.mult),
                  ("qtile", "cosb"), ("y",))
                V(lambda e, B=bq: e.tensor_tensor(y[:, 64:128], B, sinb[:, tt, :], ALU.mult),
                  ("qtile", "sinb"), ("y",))
                V(lambda e, B=bq: e.tensor_tensor(y[:, 128:192], B, cosb[:, tt, :], ALU.mult),
                  ("qtile", "cosb"), ("y",))
                V(lambda e, A=a: e.tensor_tensor(y[:, 192:256], A, sinb[:, tt, :], ALU.mult),
                  ("qtile", "sinb"), ("y",))
                V(lambda e, A=a: e.tensor_tensor(A, y[:, 0:64], y[:, 64:128], ALU.subtract),
                  ("y",), ("qtile",))
                V(lambda e, B=bq: e.tensor_tensor(B, y[:, 128:192], y[:, 192:256], ALU.add),
                  ("y",), ("qtile",))
            n = nheads * 128
            S(lambda e, m=n: e.activation(y[:, 0:m], qtile[:, 0:m], ACT.Exp),
              ("qtile",), ("y",))
            V(lambda e, m=n: e.tensor_scalar(mki2[:, 0:m], qtile[:, 0:m], 0.0, None, ALU.is_ge),
              ("qtile",), ("mki2",))
            V(lambda e, m=n: e.tensor_scalar(qtile[:, 0:m], qtile[:, 0:m], 1.0, None, ALU.add),
              ("qtile",), ("qtile",))
            V(lambda e, m=n: e.copy_predicated(y[:, 0:m], mki2[:, 0:m], qtile[:, 0:m]),
              ("mki2", "qtile", "y"), ("y",))
            V(lambda e, m=n: e.tensor_copy(
                sBF[:, :, :].rearrange("p a b -> p (a b)")[:, 0:m], y[:, 0:m]),
              ("y",), ("sBF",))

        load_wqt(wk_in, "wk_in", 0, 4)
        for tt in range(NT):
            for k in range(16):
                T(lambda e, kk=k, t=tt: e.matmul(pb[0][:, :], xT[:, kk, t * 128:(t + 1) * 128],
                                                 wqt[:, kk, :], start=(kk == 0), stop=(kk == 15)),
                  ("wqt", "xT"), ("pb0",))
            V(lambda e: e.tensor_copy(qtile[:, :], pb[0][:, :]), ("pb0",), ("qtile",))
            rope_elu(4, tt)
            V(lambda e, t=tt: e.tensor_copy(
                ktok[:, t, :], sBF[:, :, :].rearrange("p a b -> p (a b)")[:, 0:512]),
              ("sBF",), ("ktok",))
            for g in range(4):
                b, bn = pt[g % 3], f"pt{g % 3}"
                T(lambda e, gg=g, bb=b: e.transpose(bb[:, :], sBF[:, gg, :], idbf[:, :]),
                  ("sBF", "idbf"), (bn,))
                V(lambda e, gg=g, t=tt, bb=b: e.tensor_copy(
                    kfT[:, gg, t * 128:(t + 1) * 128], bb[:, :]), (bn,), ("kfT",))
        load_wqt(wv_in, "wv_in", 0, 4)
        for g in range(4):
            V(lambda e, gg=g: e.memset(vaug[:, :, gg * 129 + 128:gg * 129 + 129], 1.0),
              (), ("vaug",))
        for tt in range(NT):
            for k in range(16):
                T(lambda e, kk=k, t=tt: e.matmul(pb[0][:, :], xT[:, kk, t * 128:(t + 1) * 128],
                                                 wqt[:, kk, :], start=(kk == 0), stop=(kk == 15)),
                  ("wqt", "xT"), ("pb0",))
            for g in range(4):
                V(lambda e, gg=g, t=tt: e.tensor_copy(
                    vaug[:, t, gg * 129:gg * 129 + 128], pb[0][:, gg * 128:(gg + 1) * 128]),
                  ("pb0",), ("vaug",))
        snaps = salloc("snaps", [128, 36, 129], BF16)
        stacc = salloc("stacc", [128, 129], F32)
        stfin = salloc("stfin", [128, 516], F32)
        stp = salloc("stp", [128, 516], F32)
        stpb = salloc("stpb", [128, 516], BF16)
        for g in range(4):
            V(lambda e: e.memset(stacc[:, :], 0.0), (), ("stacc",))
            for c in range(NT):
                T(lambda e, gg=g, cc=c: e.matmul(
                    pb[2][:, 0:129], ktok[:, cc, gg * 128:(gg + 1) * 128],
                    vaug[:, cc, gg * 129:(gg + 1) * 129], start=True, stop=True),
                    ("ktok", "vaug"), ("pb2",))
                V(lambda e: e.tensor_tensor(stacc[:, :], stacc[:, :], pb[2][:, 0:129], ALU.add),
                  ("pb2", "stacc"), ("stacc",))
                V(lambda e, gg=g, cc=c: e.tensor_copy(snaps[:, gg * 9 + cc + 1, :], stacc[:, :]),
                  ("stacc",), ("snaps",))
            V(lambda e, gg=g: e.tensor_copy(stfin[:, gg * 129:(gg + 1) * 129], stacc[:, :]),
              ("stacc",), ("stfin",))
        DMA(lambda e: e.dma_start(out=ag2_in.ap(), in_=stfin[:, :]), ("stfin",), ("ag2_in",),
            q="gpsimd")
        G(lambda e: e.collective_compute("AllGather", ALU.bypass,
                                         replica_groups=[[0, 1], [2, 3], [4, 5], [6, 7]],
                                         ins=[ag2_in.ap()], outs=[ag2_out.ap()]),
          ("ag2_in",), ("ag2_out",))
        DMA(lambda e: e.dma_start(out=stp[:, :], in_=ag2_out.ap()[0:128, :]),
            ("ag2_out",), ("stp",))
        V(lambda e: e.tensor_scalar(stp[:, :], stp[:, :], role[:, 0:1], None, ALU.mult),
          ("stp", "role"), ("stp",))
        V(lambda e: e.tensor_copy(stpb[:, :], stp[:, :]), ("stp",), ("stpb",))
        for g in range(4):
            V(lambda e, gg=g: e.tensor_copy(snaps[:, gg * 9, :], stpb[:, gg * 129:(gg + 1) * 129]),
              ("stpb",), ("snaps",))
            for c in range(1, 9):
                V(lambda e, gg=g, cc=c: e.tensor_tensor(
                    snaps[:, gg * 9 + cc, :], snaps[:, gg * 9 + cc, :],
                    stpb[:, gg * 129:(gg + 1) * 129], ALU.add), ("stpb", "snaps"), ("snaps",))
        qfT = salloc("qfT", [128, 4, TOK], BF16)
        scb = salloc("scb", [128, 128], BF16)
        otr = salloc("otr", [128, 128], BF16)
        for g in range(4):
            load_wqt(wq_in, "wq_in", g, 4)
            for tt in range(NT):
                for k in range(16):
                    T(lambda e, kk=k, t=tt: e.matmul(
                        pb[0][:, :], xT[:, kk, t * 128:(t + 1) * 128], wqt[:, kk, :],
                        start=(kk == 0), stop=(kk == 15)), ("wqt", "xT"), ("pb0",))
                V(lambda e: e.tensor_copy(qtile[:, :], pb[0][:, :]), ("pb0",), ("qtile",))
                rope_elu(4, tt)
                for hh in range(4):
                    b, bn = pt[hh % 3], f"pt{hh % 3}"
                    T(lambda e, h2=hh, bb=b: e.transpose(bb[:, :], sBF[:, h2, :], idbf[:, :]),
                      ("sBF", "idbf"), (bn,))
                    V(lambda e, h2=hh, t=tt, bb=b: e.tensor_copy(
                        qfT[:, h2, t * 128:(t + 1) * 128], bb[:, :]), (bn,), ("qfT",))
            for hh in range(4):
                h = g * 4 + hh
                for c in range(NT):
                    T(lambda e, gg=g, h2=hh, cc=c: e.matmul(
                        pb[2][:, 0:128], kfT[:, gg, cc * 128:(cc + 1) * 128],
                        qfT[:, h2, cc * 128:(cc + 1) * 128], start=True, stop=True),
                        ("kfT", "qfT"), ("pb2",))
                    V(lambda e: e.tensor_tensor(scb[:, :], pb[2][:, 0:128], maskT[:, :],
                                                ALU.mult), ("pb2", "maskT"), ("scb",))
                    T(lambda e, gg=g, cc=c: e.matmul(
                        pb[3][:, 0:129], scb[:, :], vaug[:, cc, gg * 129:(gg + 1) * 129],
                        start=True, stop=False), ("scb", "vaug"), ("pb3",))
                    T(lambda e, gg=g, h2=hh, cc=c: e.matmul(
                        pb[3][:, 0:129], qfT[:, h2, cc * 128:(cc + 1) * 128],
                        snaps[:, gg * 9 + cc, :], start=False, stop=True),
                        ("qfT", "snaps"), ("pb3",))
                    V(lambda e: e.tensor_scalar(stats[:, 4:5], pb[3][:, 128:129], 1e-6, None,
                                                ALU.add), ("pb3",), ("stats",))
                    V(lambda e: e.reciprocal(stats[:, 5:6], stats[:, 4:5]), ("stats",), ("stats",))
                    V(lambda e: e.tensor_scalar(scb[:, :], pb[3][:, 0:128], stats[:, 5:6],
                                                None, ALU.mult), ("pb3", "stats"), ("scb",))
                    T(lambda e: e.transpose(pt[2][:, :], scb[:, :], idbf[:, :]),
                      ("scb", "idbf"), ("pt2",))
                    S(lambda e: e.copy(otr[:, :], pt[2][:, :]), ("pt2",), ("otr",))
                    DMA(lambda e, h3=h, cc=c: e.dma_start(
                        out=st_ao.ap()[h3, :, cc * 128:(cc + 1) * 128], in_=otr[:, :]),
                        ("otr",), ("st_ao",), q="gpsimd")
        pop_scope()
        push_scope()
        wqt2 = salloc("wqt2", [128, 16, 512], BF16)
        DMA(lambda e: e.dma_start(out=xT[:, :, :],
                                  in_=st_ao.ap().rearrange("a p t -> p a t")),
            ("st_ao",), ("xT",))
        for mb in range(4):
            for r in range(4):
                DMA(lambda e, rr=r, m=mb: e.dma_start(
                    out=sA[:, :], in_=wo_in[(m * 4 + rr) * 128:(m * 4 + rr + 1) * 128, :]),
                    ("wo_in",), ("sA",))
                V(lambda e: e.tensor_copy(sBF[:, :, :].rearrange("p a b -> p (a b)"), sA[:, :]),
                  ("sA",), ("sBF",))
                for k in range(16):
                    b, bn = pt[k % 3], f"pt{k % 3}"
                    T(lambda e, kk=k, bb=b: e.transpose(bb[:, :], sBF[:, kk, :], idbf[:, :]),
                      ("sBF", "idbf"), (bn,))
                    V(lambda e, kk=k, rr=r, bb=b: e.tensor_copy(
                        wqt2[:, kk, rr * 128:(rr + 1) * 128], bb[:, :]), (bn,), ("wqt2",))
            for tt in range(NT):
                pbk, pbn = pb[tt % 2], f"pb{tt % 2}"
                for k in range(16):
                    T(lambda e, kk=k, t=tt, p=pbk: e.matmul(
                        p[:, :], xT[:, kk, t * 128:(t + 1) * 128], wqt2[:, kk, :],
                        start=(kk == 0), stop=(kk == 15)), ("wqt2", "xT"), (pbn,))
                V(lambda e, t=tt, m=mb, p=pbk: e.scalar_tensor_tensor(
                    x[:, t, m * 512:(m + 1) * 512], p[:, :], 1.0,
                    x[:, t, m * 512:(m + 1) * 512], ALU.mult, ALU.add), (pbn, "x"), ("x",))
        pop_scope()

    stagefns = [l0_ffn, mingru, l0_ffn, l1_ffn, l1_attn, l1_ffn]
    for i in range(min(upto, 6)):
        stagefns[i]()

    DMA(lambda e: e.dma_start(out=xout.rearrange("(t p) d -> p t d", p=128),
                              in_=x[:, :, :]), ("x",), ("xout_d",))
    _wait("sync", "ld", cnt["ld"])
    barrier()

    with nc.Block() as block:
        @block.sync
        def _(e):
            for f in ops["sync"]:
                f(e)

        @block.gpsimd
        def _(e):
            for f in ops["gpsimd"]:
                f(e)

        @block.tensor
        def _(e):
            for f in ops["tensor"]:
                f(e)

        @block.vector
        def _(e):
            for f in ops["vector"]:
                f(e)

        @block.scalar
        def _(e):
            for f in ops["scalar"]:
                f(e)

    for cm in reversed(top_cms):
        cm.__exit__(None, None, None)
    return nc


def _rope_tables(pos0, n, dim=128):
    inv = 1.0 / (10000.0 ** (np.arange(0, dim, 2, dtype=np.float32) / dim))
    t = np.arange(pos0, pos0 + n, dtype=np.float32)
    fr = np.outer(t, inv)
    return np.cos(fr).astype(np.float32), np.sin(fr).astype(np.float32)


def kernel(L0_attn_x, L0_attn_w_hg, L0_attn_w_out, L0_ffn_w_gate_up, L0_ffn_w_down,
           L1_attn_Wq, L1_attn_Wk, L1_attn_Wv, L1_attn_Wo, L1_ffn_weight, upto=6):
    wl = np.asarray(L1_ffn_weight)
    if wl.dtype == np.uint16:
        wl = wl.view(BF)
    wl = np.ascontiguousarray(wl)
    idbf = np.eye(128, dtype=BF)
    idf = np.eye(128, dtype=np.float32)
    maskT = np.triu(np.ones((128, 128), dtype=np.float32))
    common = dict(
        w_hg=np.ascontiguousarray(L0_attn_w_hg, np.float32),
        w_out=np.ascontiguousarray(L0_attn_w_out, np.float32),
        w_gu=np.ascontiguousarray(L0_ffn_w_gate_up, np.float32),
        w_dn=np.ascontiguousarray(L0_ffn_w_down, np.float32),
        wq_in=np.ascontiguousarray(L1_attn_Wq, np.float32),
        wk_in=np.ascontiguousarray(L1_attn_Wk, np.float32),
        wv_in=np.ascontiguousarray(L1_attn_Wv, np.float32),
        wo_in=np.ascontiguousarray(L1_attn_Wo, np.float32),
        wl_in=wl, idbf_in=idbf, idf_in=idf, maskT_in=maskT,
        sel_in=np.kron(np.eye(4, dtype=np.float32), np.ones((1, 128), np.float32)),
    )
    xf = np.ascontiguousarray(L0_attn_x, np.float32)
    in_maps = []
    for core in range(8):
        b, half = core // 2, core % 2
        cos, sin = _rope_tables(half * TOK, TOK)
        m = dict(common)
        m["xin"] = np.ascontiguousarray(xf[b, half * TOK:(half + 1) * TOK, :])
        m["cos_in"] = cos
        m["sin_in"] = sin
        m["role_in"] = np.full((128, 1), float(half), np.float32)
        in_maps.append(m)
    nc = build(upto)
    res = run_bass_kernel_spmd(nc, in_maps, core_ids=list(range(8)))
    out = np.empty((4, 2048, 2048), np.float32)
    for core in range(8):
        b, half = core // 2, core % 2
        out[b, half * TOK:(half + 1) * TOK, :] = res.results[core]["xout"]
    return out


# version 04
# speedup vs baseline: 1.9324x; 1.0713x over previous
import numpy as np
import ml_dtypes
import concourse.bass as bass
import concourse.mybir as mybir
from concourse.bass_utils import run_bass_kernel_spmd

F32 = mybir.dt.float32
BF16 = mybir.dt.bfloat16
I32 = mybir.dt.int32
ALU = mybir.AluOpType
ACT = mybir.ActivationFunctionType
AX = mybir.AxisListType.X
BF = ml_dtypes.bfloat16

TOK = 1024
NT = 8
D = 2048
DI = 3072
IFF = 5632
NO1 = 88
NI2 = 44
MAGIC = 12582912.0
MASK_HI = int(np.int32(np.uint32(0xFFF00000)))

ENG = ("sync", "gpsimd", "tensor", "vector", "scalar")
KEY = {"sync": "ld", "gpsimd": "gd", "tensor": "t", "vector": "v", "scalar": "s"}
INC = {"ld": 16, "gd": 16, "t": 1, "v": 1, "s": 1, "g": 1}


PHASES = 99


def build(upto=6):
    nc = bass.Bass("TRN2", target_bir_lowering=False)

    xin = nc.dram_tensor("xin", [TOK, D], F32, kind="ExternalInput").ap()
    w_hg = nc.dram_tensor("w_hg", [D, 2 * DI], F32, kind="ExternalInput").ap()
    w_out = nc.dram_tensor("w_out", [DI, D], F32, kind="ExternalInput").ap()
    w_gu = nc.dram_tensor("w_gu", [2 * IFF, D], F32, kind="ExternalInput").ap()
    w_dn = nc.dram_tensor("w_dn", [D, IFF], F32, kind="ExternalInput").ap()
    wq_in = nc.dram_tensor("wq_in", [D, D], F32, kind="ExternalInput").ap()
    wk_in = nc.dram_tensor("wk_in", [512, D], F32, kind="ExternalInput").ap()
    wv_in = nc.dram_tensor("wv_in", [512, D], F32, kind="ExternalInput").ap()
    wo_in = nc.dram_tensor("wo_in", [D, D], F32, kind="ExternalInput").ap()
    wl_in = nc.dram_tensor("wl_in", [D, D], BF16, kind="ExternalInput").ap()
    idbf_in = nc.dram_tensor("idbf_in", [128, 128], BF16, kind="ExternalInput").ap()
    idf_in = nc.dram_tensor("idf_in", [128, 128], F32, kind="ExternalInput").ap()
    maskT_in = nc.dram_tensor("maskT_in", [128, 128], F32, kind="ExternalInput").ap()
    cos_in = nc.dram_tensor("cos_in", [TOK, 64], F32, kind="ExternalInput").ap()
    sin_in = nc.dram_tensor("sin_in", [TOK, 64], F32, kind="ExternalInput").ap()
    role_in = nc.dram_tensor("role_in", [128, 1], F32, kind="ExternalInput").ap()
    sel_in = nc.dram_tensor("sel_in", [4, 512], F32, kind="ExternalInput").ap()
    xout = nc.dram_tensor("xout", [TOK, D], F32, kind="ExternalOutput").ap()

    st_w1 = nc.dram_tensor("st_w1", [NO1, 128, D], BF16)
    st_w2 = nc.dram_tensor("st_w2", [NI2, 128, D], BF16)
    st_wl = nc.dram_tensor("st_wl", [16, 128, D], BF16)
    st_ao = nc.dram_tensor("st_ao", [16, 128, TOK], BF16)
    sp_h = nc.dram_tensor("sp_h", [24, 128, TOK], BF16)
    sp_cp = nc.dram_tensor("sp_cp", [24, 128, TOK], BF16)
    ag1_in = nc.dram_tensor("ag1_in", [128, 24], F32)
    ag1_out = nc.dram_tensor("ag1_out", [256, 24], F32)
    ag2_in = nc.dram_tensor("ag2_in", [128, 516], F32)
    ag2_out = nc.dram_tensor("ag2_out", [256, 516], F32)
    ar_in = nc.dram_tensor("ar_in", [128, 1], F32)
    ar_out = nc.dram_tensor("ar_out", [128, 1], F32)

    ops = {e: [] for e in ENG}
    cnt = {k: 0 for k in INC}
    waited = {e: {} for e in ENG}
    last_w = {}
    readers = {}
    sem = {}

    def _wait(eng, sk, val):
        if val <= 0 or waited[eng].get(sk, 0) >= val:
            return
        waited[eng][sk] = val
        ops[eng].append(lambda e, s=sk, v=val: e.wait_ge(sem[s], v))

    def emit(eng, fn, ins=(), outs=(), semkey=None, inc=None):
        mykey = semkey or KEY[eng]
        myinc = inc if inc is not None else INC[mykey]
        for nm in ins:
            lw = last_w.get(nm)
            if lw is not None and (lw[0] != mykey or eng != "tensor"):
                _wait(eng, lw[0], lw[1])
        for nm in outs:
            lw = last_w.get(nm)
            if lw is not None and lw[0] != mykey:
                _wait(eng, lw[0], lw[1])
            for sk, v in readers.get(nm, {}).items():
                if sk != mykey:
                    _wait(eng, sk, v)
        cnt[mykey] += myinc
        myval = cnt[mykey]
        ops[eng].append(lambda e, f=fn, s=mykey, i=myinc: f(e).then_inc(sem[s], i))
        for nm in outs:
            last_w[nm] = (mykey, myval)
            readers.pop(nm, None)
        for nm in ins:
            readers.setdefault(nm, {})[mykey] = myval
        return myval

    def barrier():
        for e in ENG:
            for sk in INC:
                if sk != KEY[e]:
                    _wait(e, sk, cnt[sk])

    def T(fn, ins=(), outs=()):
        emit("tensor", fn, ins, outs)

    def V(fn, ins=(), outs=()):
        emit("vector", fn, ins, outs)

    def S(fn, ins=(), outs=()):
        emit("scalar", fn, ins, outs)

    def G(fn, ins=(), outs=()):
        emit("gpsimd", fn, ins, outs, semkey="g", inc=1)

    def DMA(fn, ins=(), outs=(), q="sync"):
        emit(q, fn, ins, outs)

    scopes = []
    top_cms = []

    _uid = [0]

    def salloc(name, shape, dt):
        _uid[0] += 1
        cm = nc.sbuf_tensor(f"{name}_{_uid[0]}", shape, dt)
        t = cm.__enter__()
        (scopes[-1] if scopes else top_cms).append(cm)
        return t

    def push_scope():
        scopes.append([])

    def pop_scope():
        barrier()
        for cm in reversed(scopes.pop()):
            cm.__exit__(None, None, None)

    x = salloc("x", [128, NT, D], F32)
    xT = salloc("xT", [128, 16, TOK], BF16)
    y = salloc("y", [128, D], F32)
    sA = salloc("sA", [128, D], F32)
    sBF = salloc("sBF", [128, 16, 128], BF16)
    stats = salloc("stats", [128, 16], F32)
    idbf = salloc("idbf", [128, 128], BF16)
    idf = salloc("idf", [128, 128], F32)
    maskT = salloc("maskT", [128, 128], F32)
    cosb = salloc("cosb", [128, NT, 64], F32)
    sinb = salloc("sinb", [128, NT, 64], F32)
    role = salloc("role", [128, 1], F32)
    onesf = salloc("onesf", [1, 128], F32)
    ceps = salloc("ceps", [128, 2], F32)
    hend = salloc("hend", [128, 24], F32)
    hendp = salloc("hendp", [128, 24], F32)
    selc = salloc("selc", [4, 512], F32)

    def palloc(name, shape, dt):
        cm = nc.psum_tensor(name, shape, dt)
        t = cm.__enter__()
        top_cms.append(cm)
        return t

    pb = [palloc(f"pb{i}", [128, 512], F32) for i in range(5)]
    pt = [palloc(f"pt{i}", [128, 128], BF16) for i in range(3)]

    for sk in INC:
        cm = nc.semaphore("sem_" + sk)
        sem[sk] = cm.__enter__()
        top_cms.append(cm)

    def ln_tile(tt, out_ap, out_name):
        sa = x[:, tt, :]
        V(lambda e, a=sa: e.reduce_sum(stats[:, 0:1], a, AX), ("x",), ("stats",))
        S(lambda e, a=sa: e.activation(y[:, :], a, ACT.Square, accum_out=stats[:, 1:2]),
          ("x",), ("y", "stats"))
        V(lambda e: e.tensor_scalar(stats[:, 0:1], stats[:, 0:1], -1.0 / D, None, ALU.mult),
          ("stats",), ("stats",))
        V(lambda e: e.tensor_scalar(stats[:, 1:2], stats[:, 1:2], 1.0 / D, None, ALU.mult),
          ("stats",), ("stats",))
        V(lambda e: e.tensor_tensor(stats[:, 2:3], stats[:, 0:1], stats[:, 0:1], ALU.mult),
          ("stats",), ("stats",))
        V(lambda e: e.tensor_tensor(stats[:, 1:2], stats[:, 1:2], stats[:, 2:3], ALU.subtract),
          ("stats",), ("stats",))
        S(lambda e: e.activation(stats[:, 3:4], stats[:, 1:2], ACT.Sqrt, ceps[:, 0:1], 1.0),
          ("stats", "ceps"), ("stats",))
        V(lambda e: e.reciprocal(stats[:, 3:4], stats[:, 3:4]), ("stats",), ("stats",))
        S(lambda e: e.mul(stats[:, 2:3], stats[:, 0:1], stats[:, 3:4]), ("stats",), ("stats",))
        S(lambda e, o=out_ap, a=sa: e.activation(o, a, ACT.Identity, stats[:, 2:3], stats[:, 3:4]),
          ("x", "stats"), (out_name,))

    def tp16(src_name, dst, dst_name, col0, src=None):
        s = sBF if src is None else src
        for k in range(16):
            b, bn = pt[k % 3], f"pt{k % 3}"
            T(lambda e, kk=k, bb=b, ss=s: e.transpose(bb[:, :], ss[:, kk, :], idbf[:, :]),
              (src_name, "idbf"), (bn,))
            V(lambda e, kk=k, bb=b, d=dst, c=col0: e.tensor_copy(d[:, kk, c:c + 128], bb[:, :]),
              (bn,), (dst_name,))

    def rne_clip(ap, name, qmax=127.0):
        V(lambda e, a=ap: e.tensor_scalar(a, a, MAGIC, None, ALU.add), (name,), (name,))
        V(lambda e, a=ap: e.tensor_scalar(a, a, -MAGIC, None, ALU.add), (name,), (name,))
        V(lambda e, a=ap, q=qmax: e.tensor_scalar(a, a, q, -q, ALU.min, ALU.max),
          (name,), (name,))

    def e4m3_round(fap, iap, nf, ni):
        V(lambda e: e.tensor_scalar(iap, fap.bitcast(I32), 20, None,
                                    ALU.logical_shift_right), (nf,), (ni,))
        V(lambda e: e.tensor_scalar(iap, iap, 1, None, ALU.bitwise_and), (ni,), (ni,))
        V(lambda e: e.tensor_tensor(iap, iap, fap.bitcast(I32), ALU.add), (nf, ni), (ni,))
        V(lambda e: e.tensor_scalar(iap, iap, 0x0007FFFF, None, ALU.add), (ni,), (ni,))
        V(lambda e: e.tensor_scalar(iap, iap, MASK_HI, None, ALU.bitwise_and), (ni,), (ni,))
        V(lambda e: e.tensor_copy(fap, iap.bitcast(F32)), (ni,), (nf,))

    DMA(lambda e: e.dma_start(out=idbf[:, :], in_=idbf_in), (), ("idbf",))
    DMA(lambda e: e.dma_start(out=idf[:, :], in_=idf_in), (), ("idf",))
    DMA(lambda e: e.dma_start(out=maskT[:, :], in_=maskT_in), (), ("maskT",))
    DMA(lambda e: e.dma_start(out=cosb[:, :, :],
                              in_=cos_in.rearrange("(t p) c -> p t c", p=128)), (), ("cosb",))
    DMA(lambda e: e.dma_start(out=sinb[:, :, :],
                              in_=sin_in.rearrange("(t p) c -> p t c", p=128)), (), ("sinb",))
    DMA(lambda e: e.dma_start(out=role[:, :], in_=role_in), (), ("role",))
    DMA(lambda e: e.dma_start(out=x[:, :, :],
                              in_=xin.rearrange("(t p) d -> p t d", p=128)), (), ("x",))
    V(lambda e: e.memset(onesf[:, :], 1.0), (), ("onesf",))
    DMA(lambda e: e.dma_start(out=selc[:, :], in_=sel_in), (), ("selc",))
    V(lambda e: e.memset(ceps[:, 0:1], 1e-5), (), ("ceps",))

    push_scope()
    wrow = salloc("wrow", [128, IFF], F32)
    wqd = salloc("wqd", [128, IFF], BF16)
    sbT = salloc("sbT", [128, NI2, 128], BF16)
    wli = salloc("wli", [128, D], I32)
    wlrow = salloc("wlrow", [128, D], BF16)

    def wq_tile(src, sname, ncols, bi, ot):
        b, bn = wrow, "wrow"
        DMA(lambda e, o=ot, n=ncols, bb=b: e.dma_start(
            out=bb[:, 0:n], in_=src[o * 128:(o + 1) * 128, 0:n]),
            (sname,), (bn, bn + "L", bn + "H"))
        V(lambda e, n=ncols, bb=b: e.tensor_reduce(stats[:, 4:5], bb[:, 0:n], AX, ALU.max,
                                                   apply_absolute_value=True),
          (bn, bn + "L", bn + "H"), ("stats",))
        V(lambda e: e.tensor_scalar(stats[:, 5:6], stats[:, 4:5], 1.0 / 127.0, 1e-8,
                                    ALU.mult, ALU.max), ("stats",), ("stats",))
        V(lambda e: e.reciprocal(stats[:, 6:7], stats[:, 5:6]), ("stats",), ("stats",))
        nh = ncols // 2
        for EE, c0, c1, sfx in ((V, 0, nh, "L"), (V, nh, ncols, "H")):
            hn, wn = bn + sfx, "wqd" + sfx
            EE(lambda e, bb=b, a=c0, z=c1: e.tensor_scalar(bb[:, a:z], bb[:, a:z], stats[:, 6:7],
                                                           MAGIC, ALU.mult, ALU.add),
               (hn, "stats"), (hn,))
            EE(lambda e, bb=b, a=c0, z=c1: e.tensor_scalar(bb[:, a:z], bb[:, a:z], -MAGIC, 127.0,
                                                           ALU.add, ALU.min), (hn,), (hn,))
            EE(lambda e, bb=b, a=c0, z=c1: e.tensor_scalar(wqd[:, a:z], bb[:, a:z], -127.0,
                                                           stats[:, 5:6], ALU.max, ALU.mult),
               (hn, "stats"), (wn,))

    def stash_tp(nk, store_fn):
        for k in range(nk):
            b, bn = pt[k % 3], f"pt{k % 3}"
            half = "wqdL" if k < nk // 2 else "wqdH"
            T(lambda e, kk=k, bb=b: e.transpose(bb[:, :], wqd[:, kk * 128:(kk + 1) * 128], idbf[:, :]),
              (half, "idbf"), (bn,))
            S(lambda e, kk=k, bb=b: e.copy(sbT[:, kk, :], bb[:, :]), (bn,), ("sbT",))
        store_fn()

    for ot in range(NO1):
        wq_tile(w_gu, "w_gu", D, ot % 2, ot)
        stash_tp(16, lambda o=ot: DMA(lambda e: e.dma_start(
            out=st_w1.ap()[o].rearrange("p (a b) -> p a b", a=16), in_=sbT[:, 0:16, :]),
            ("sbT",), (f"st_w1_{o}",), q="gpsimd"))
    for ot in range(16):
        wq_tile(w_dn, "w_dn", IFF, ot % 2, ot)

        def store_w2(o=ot):
            for h in range(2):
                DMA(lambda e, hh=h: e.dma_start(
                    out=st_w2.ap()[hh * 22:(hh + 1) * 22, :, o * 128:(o + 1) * 128]
                        .rearrange("a p b -> p a b"),
                    in_=sbT[:, hh * 22:(hh + 1) * 22, :]), ("sbT",), (f"st_w2_{o}",), q="gpsimd")
        stash_tp(NI2, store_w2)

    V(lambda e: e.memset(stats[:, 8:9], 0.0), (), ("stats",))
    for ot in range(16):
        DMA(lambda e, o=ot: e.dma_start(
            out=wlrow[:, :], in_=wl_in[o * 128:(o + 1) * 128, :]),
            ("wl_in",), ("wlrow",))
        V(lambda e: e.tensor_reduce(stats[:, 9:10], wlrow[:, :],
                                    AX, ALU.max, apply_absolute_value=True),
          ("wlrow",), ("stats",))
        V(lambda e: e.tensor_tensor(stats[:, 8:9], stats[:, 8:9], stats[:, 9:10], ALU.max),
          ("stats",), ("stats",))
    T(lambda e: e.transpose(pb[4][0:1, 0:128], stats[:, 8:9], idf[:, :]), ("stats", "idf"), ("pb4",))
    V(lambda e: e.reduce_max(stats[0:1, 9:10], pb[4][0:1, 0:128], AX), ("pb4",), ("stats",))
    T(lambda e: e.matmul(pb[4][:, 0:1], onesf[0:1, :], stats[0:1, 9:10], start=True, stop=True),
      ("onesf", "stats"), ("pb4",))
    V(lambda e: e.tensor_scalar(stats[:, 12:13], pb[4][:, 0:1], 1.0 / 448.0, 1e-12,
                                ALU.mult, ALU.max), ("pb4",), ("stats",))
    V(lambda e: e.reciprocal(stats[:, 13:14], stats[:, 12:13]), ("stats",), ("stats",))
    for ot in range(16):
        DMA(lambda e, o=ot: e.dma_start(
            out=wlrow[:, :], in_=wl_in[o * 128:(o + 1) * 128, :]),
            ("wl_in",), ("wlrow",))
        V(lambda e: e.tensor_scalar(sA[:, :], wlrow[:, :],
                                    stats[:, 13:14], None, ALU.mult),
          ("wlrow", "stats"), ("sA",))
        e4m3_round(sA[:, :], wli[:, :], "sA", "wli")
        V(lambda e: e.tensor_copy(wqd[:, 0:D], sA[:, :]), ("sA",), ("wqd", "wqdL", "wqdH"))
        stash_tp(16, lambda o=ot: DMA(lambda e: e.dma_start(
            out=st_wl.ap()[:, :, o * 128:(o + 1) * 128].rearrange("a p b -> p a b"),
            in_=sbT[:, 0:16, :]), ("sbT",), (f"st_wl_{o}",), q="gpsimd"))
    pop_scope()

    def l0_ffn():
        push_scope()
        hsb = salloc("hsb", [128, NI2, 512], BF16)
        for half in range(2 if PHASES >= 5 else 1):
            tts = list(range(half * 4, half * 4 + 4))
            push_scope()
            w1g2 = [salloc("w1g", [128, 16, 128], BF16) for _ in range(2)]
            w1u2 = [salloc("w1u", [128, 16, 128], BF16) for _ in range(2)]
            sB = salloc("sB", [128, 512], F32)
            sC = salloc("sC", [128, 512], F32)
            sCg = salloc("sCg", [128, 512], F32)
            hax = salloc("hax", [128, 4], F32)
            hrecT = salloc("hrecT", [4, 128], F32)
            hsclT = salloc("hsclT", [4, 128], F32)
            recb = salloc("recb", [128, 512], F32)
            sclb = salloc("sclb", [128, 512], F32)
            for tt in tts:
                ln_tile(tt, y[:, :], "y")
                V(lambda e: e.tensor_reduce(stats[:, 4:5], y[:, :], AX, ALU.max,
                                            apply_absolute_value=True), ("y",), ("stats",))
                V(lambda e: e.tensor_scalar(stats[:, 5:6], stats[:, 4:5], 1.0 / 127.0, 1e-8,
                                            ALU.mult, ALU.max), ("stats",), ("stats",))
                V(lambda e: e.reciprocal(stats[:, 6:7], stats[:, 5:6]), ("stats",), ("stats",))
                V(lambda e: e.tensor_scalar(sA[:, :], y[:, :], stats[:, 6:7], MAGIC,
                                            ALU.mult, ALU.add), ("y", "stats"), ("sA",))
                V(lambda e: e.tensor_scalar(sA[:, :], sA[:, :], -MAGIC, 127.0,
                                            ALU.add, ALU.min), ("sA",), ("sA",))
                V(lambda e: e.tensor_scalar(sBF[:, :, :].rearrange("p a b -> p (a b)"),
                                            sA[:, :], -127.0, stats[:, 5:6],
                                            ALU.max, ALU.mult), ("sA", "stats"), ("sBF",))
                tp16("sBF", xT, "xT", (tt - half * 4) * 128)
            V(lambda e: e.memset(hax[:, :], 0.0), (), ("hax",))
            for mp in range(NI2 if PHASES >= 2 else 0):
                bg, bgn = w1g2[mp % 2], f"w1g{mp % 2}"
                bu, bun = w1u2[mp % 2], f"w1u{mp % 2}"
                DMA(lambda e, m=mp, b=bg: e.dma_start(
                    out=b[:, :, :], in_=st_w1.ap()[m].rearrange("p (a b) -> p a b", a=16)),
                    (f"st_w1_{mp}",), (bgn,))
                DMA(lambda e, m=mp, b=bu: e.dma_start(
                    out=b[:, :, :], in_=st_w1.ap()[m + NI2].rearrange("p (a b) -> p a b", a=16)),
                    (f"st_w1_{mp + NI2}",), (bun,))
                for k in range(16):
                    T(lambda e, kk=k, b=bg: e.matmul(
                        pb[0][:, :], b[:, kk, :], xT[:, kk, 0:512],
                        start=(kk == 0), stop=(kk == 15)), (bgn, "xT"), ("pb0",))
                for k in range(16):
                    T(lambda e, kk=k, b=bu: e.matmul(
                        pb[1][:, :], b[:, kk, :], xT[:, kk, 0:512],
                        start=(kk == 0), stop=(kk == 15)), (bun, "xT"), ("pb1",))
                S(lambda e: e.activation(sB[:, :], pb[0][:, :], ACT.Silu), ("pb0",), ("sB",))
                V(lambda e, m=mp: e.tensor_tensor(hsb[:, m, :], sB[:, :], pb[1][:, :], ALU.mult),
                  ("sB", "pb1"), ("hsb", "hsb0", "hsb1"))
                for c in range(4):
                    b, bn = pt[c % 3], f"pt{c % 3}"
                    T(lambda e, m=mp, cc=c, bb=b: e.transpose(
                        bb[:, :], hsb[:, m, cc * 128:(cc + 1) * 128], idbf[:, :]),
                        ("hsb", "idbf"), (bn,))
                    V(lambda e, bb=b: e.tensor_reduce(stats[:, 7:8], bb[:, :], AX, ALU.max,
                                                      apply_absolute_value=True),
                      (bn,), ("stats",))
                    V(lambda e, cc=c: e.tensor_tensor(hax[:, cc:cc + 1], hax[:, cc:cc + 1],
                                                      stats[:, 7:8], ALU.max),
                      ("stats", "hax"), ("hax",))
            V(lambda e: e.tensor_scalar(hax[:, 0:4], hax[:, 0:4], 1.0 / 127.0, 1e-8,
                                        ALU.mult, ALU.max), ("hax",), ("hax",))
            T(lambda e: e.transpose(pb[4][0:4, 0:128], hax[:, 0:4], idf[:, :]), ("hax", "idf"), ("pb4",))
            V(lambda e: e.tensor_copy(hsclT[0:4, :], pb[4][0:4, 0:128]), ("pb4",), ("hsclT",))
            V(lambda e: e.reciprocal(hrecT[0:4, :], hsclT[0:4, :]), ("hsclT",), ("hrecT",))
            for c in range(4):
                T(lambda e, cc=c: e.matmul(pb[4][:, cc * 128:(cc + 1) * 128],
                                           selc[0:4, cc * 128:(cc + 1) * 128],
                                           hrecT[0:4, :], start=True, stop=True),
                  ("selc", "hrecT"), ("pb4",))
            V(lambda e: e.tensor_copy(recb[:, :], pb[4][:, 0:512]), ("pb4",), ("recb",))
            for c in range(4):
                T(lambda e, cc=c: e.matmul(pb[4][:, cc * 128:(cc + 1) * 128],
                                           selc[0:4, cc * 128:(cc + 1) * 128],
                                           hsclT[0:4, :], start=True, stop=True),
                  ("selc", "hsclT"), ("pb4",))
            V(lambda e: e.tensor_copy(sclb[:, :], pb[4][:, 0:512]), ("pb4",), ("sclb",))
            for mp in range(NI2 if PHASES >= 3 else 0):
                EE = V
                tb = sC if mp % 2 == 0 else sCg
                tn = "sC" if mp % 2 == 0 else "sCg"
                hn = f"hsb{mp % 2}"
                S(lambda e, m=mp, t=tb: e.copy(t[:, :], hsb[:, m, :]), (hn, "hsb"), (tn,))
                EE(lambda e, t=tb: e.tensor_tensor(t[:, :], t[:, :], recb[:, :], ALU.mult),
                   (tn, "recb"), (tn,))
                EE(lambda e, t=tb: e.tensor_scalar(t[:, :], t[:, :], MAGIC, -MAGIC,
                                                   ALU.add, ALU.add), (tn,), (tn,))
                EE(lambda e, t=tb: e.tensor_scalar(t[:, :], t[:, :], 127.0, -127.0,
                                                   ALU.min, ALU.max), (tn,), (tn,))
                EE(lambda e, m=mp, t=tb: e.tensor_tensor(hsb[:, m, :], t[:, :], sclb[:, :],
                                                         ALU.mult), (tn, "sclb"), (hn,))
            pop_scope()
            push_scope()
            w2b = salloc("w2b", [128, NI2, 256], BF16)
            for nch in range(8 if PHASES >= 4 else 0):
                b, bn = w2b, "w2b"
                for h2 in range(2):
                    DMA(lambda e, n=nch, hh=h2, bb=b: e.dma_start(
                        out=bb[:, hh * 22:(hh + 1) * 22, :],
                        in_=st_w2.ap()[hh * 22:(hh + 1) * 22, :, n * 256:(n + 1) * 256]
                            .rearrange("a p b -> p a b")),
                        (f"st_w2_{2 * nch}", f"st_w2_{2 * nch + 1}"), (bn,))
                for i, tt in enumerate(tts):
                    pbk, pbn = pb[i % 2], f"pb{i % 2}"
                    for k in range(NI2):
                        T(lambda e, kk=k, lt=tt - half * 4, bb=b, p=pbk: e.matmul(
                            p[:, 0:256], hsb[:, kk, lt * 128:(lt + 1) * 128], bb[:, kk, :],
                            start=(kk == 0), stop=(kk == NI2 - 1)), (bn, "hsb0", "hsb1"), (pbn,))
                    V(lambda e, t=tt, n=nch, p=pbk: e.scalar_tensor_tensor(
                        x[:, t, n * 256:(n + 1) * 256], p[:, 0:256], 0.5,
                        x[:, t, n * 256:(n + 1) * 256], ALU.mult, ALU.add),
                      (pbn, "x"), ("x",))
            pop_scope()
        pop_scope()

    def mingru():
        push_scope()
        whf2 = [salloc("whf", [128, 16, 128], F32) for _ in range(2)]
        whb = [salloc(f"whb{i}", [128, 16, 128], BF16) for i in range(2)]
        wgb = [salloc(f"wgb{i}", [128, 16, 128], BF16) for i in range(2)]
        ct = salloc("ct", [128, TOK], F32)
        zt = salloc("zt", [128, TOK], F32)
        gt = salloc("gt", [128, TOK], F32)
        ht = salloc("ht", [128, TOK], F32)
        cp = salloc("cp", [128, TOK], F32)
        mki = salloc("mki", [128, TOK], I32)
        hbf = salloc("hbf", [128, TOK], BF16)
        cpb = salloc("cpb", [128, TOK], BF16)
        for tt in range(NT):
            ln_tile(tt, sBF[:, :, :].rearrange("p a b -> p (a b)"), "sBF")
            tp16("sBF", xT, "xT", tt * 128)
        for j in range(24):
            bh, bhn = whb[j % 2], f"whb{j % 2}"
            bg, bgn = wgb[j % 2], f"wgb{j % 2}"
            wf, wfn = whf2[j % 2], f"whf{j % 2}"
            DMA(lambda e, jj=j, w=wf: e.dma_start(
                out=w[:, :, :],
                in_=w_hg[:, jj * 128:(jj + 1) * 128].rearrange("(a p) c -> p a c", p=128)),
                ("w_hg",), (wfn,))
            V(lambda e, b=bh, w=wf: e.tensor_copy(
                b[:, :, :].rearrange("p a b -> p (a b)"),
                w[:, :, :].rearrange("p a b -> p (a b)")), (wfn,), (bhn,))
            wf2, wfn2 = whf2[(j + 1) % 2], f"whf{(j + 1) % 2}"
            DMA(lambda e, jj=j, w=wf2: e.dma_start(
                out=w[:, :, :],
                in_=w_hg[:, DI + jj * 128:DI + (jj + 1) * 128]
                    .rearrange("(a p) c -> p a c", p=128)), ("w_hg",), (wfn2,))
            V(lambda e, b=bg, w=wf2: e.tensor_copy(
                b[:, :, :].rearrange("p a b -> p (a b)"),
                w[:, :, :].rearrange("p a b -> p (a b)")), (wfn2,), (bgn,))
            for hf in range(2):
                for k in range(16):
                    T(lambda e, kk=k, h=hf, b=bh: e.matmul(
                        pb[h][:, :], b[:, kk, :], xT[:, kk, h * 512:(h + 1) * 512],
                        start=(kk == 0), stop=(kk == 15)), (bhn, "xT"), (f"pb{hf}",))
                for k in range(16):
                    T(lambda e, kk=k, h=hf, b=bg: e.matmul(
                        pb[2 + h][:, :], b[:, kk, :], xT[:, kk, h * 512:(h + 1) * 512],
                        start=(kk == 0), stop=(kk == 15)), (bgn, "xT"), (f"pb{2 + hf}",))
            for hf in range(2):
                sl = slice(hf * 512, (hf + 1) * 512)
                S(lambda e, h=hf, s=sl: e.activation(ct[:, s], pb[2 + h][:, :], ACT.Sigmoid,
                                                     0.0, -1.0), (f"pb{2 + hf}",), ("ct",))
                S(lambda e, h=hf, s=sl: e.activation(zt[:, s], pb[2 + h][:, :], ACT.Sigmoid),
                  (f"pb{2 + hf}",), ("zt",))
                S(lambda e, h=hf, s=sl: e.activation(gt[:, s], pb[h][:, :], ACT.Sigmoid),
                  (f"pb{hf}",), ("gt",))
                V(lambda e, h=hf, s=sl: e.tensor_scalar(ht[:, s], pb[h][:, :], 0.5, None,
                                                        ALU.add), (f"pb{hf}",), ("ht",))
                V(lambda e, h=hf, s=sl: e.tensor_scalar(mki[:, s], pb[h][:, :], 0.0, None,
                                                        ALU.is_ge), (f"pb{hf}",), ("mki",))
            V(lambda e: e.copy_predicated(gt[:, :], mki[:, :], ht[:, :]),
              ("mki", "ht", "gt"), ("gt",))
            V(lambda e: e.tensor_tensor(gt[:, :], zt[:, :], gt[:, :], ALU.mult),
              ("zt", "gt"), ("gt",))
            V(lambda e: e.tensor_tensor_scan(ht[:, 0:512], ct[:, 0:512], gt[:, 0:512],
                                             0.0, ALU.mult, ALU.add), ("ct", "gt"), ("ht",))
            V(lambda e: e.tensor_tensor_scan(ht[:, 512:1024], ct[:, 512:1024], gt[:, 512:1024],
                                             ht[:, 511:512], ALU.mult, ALU.add),
              ("ct", "gt", "ht"), ("ht",))
            V(lambda e: e.tensor_tensor_scan(cp[:, 0:512], ct[:, 0:512], gt[:, 0:512],
                                             1.0, ALU.mult, ALU.bypass), ("ct", "gt"), ("cp",))
            V(lambda e: e.tensor_tensor_scan(cp[:, 512:1024], ct[:, 512:1024], gt[:, 512:1024],
                                             cp[:, 511:512], ALU.mult, ALU.bypass),
              ("ct", "gt", "cp"), ("cp",))
            V(lambda e, jj=j: e.tensor_copy(hend[:, jj:jj + 1], ht[:, 1023:1024]),
              ("ht",), ("hend",))
            S(lambda e: e.copy(hbf[:, :], ht[:, :]), ("ht",), ("hbf",))
            S(lambda e: e.copy(cpb[:, :], cp[:, :]), ("cp",), ("cpb",))
            DMA(lambda e, jj=j: e.dma_start(out=sp_h.ap()[jj], in_=hbf[:, :]),
                ("hbf",), ("sp_h",), q="gpsimd")
            DMA(lambda e, jj=j: e.dma_start(out=sp_cp.ap()[jj], in_=cpb[:, :]),
                ("cpb",), ("sp_cp",), q="gpsimd")
        DMA(lambda e: e.dma_start(out=ag1_in.ap(), in_=hend[:, :]), ("hend",), ("ag1_in",),
            q="gpsimd")
        G(lambda e: e.collective_compute("AllGather", ALU.bypass,
                                         replica_groups=[[0, 1], [2, 3], [4, 5], [6, 7]],
                                         ins=[ag1_in.ap()], outs=[ag1_out.ap()]),
          ("ag1_in",), ("ag1_out",))
        DMA(lambda e: e.dma_start(out=hendp[:, :], in_=ag1_out.ap()[0:128, :]),
            ("ag1_out",), ("hendp",))
        V(lambda e: e.tensor_scalar(hendp[:, :], hendp[:, :], role[:, 0:1], None, ALU.mult),
          ("hendp", "role"), ("hendp",))
        pop_scope()
        push_scope()
        hfb = salloc("hfb", [128, TOK], BF16)
        cpb2 = salloc("cpb2", [128, TOK], BF16)
        sB2 = salloc("sB2", [128, TOK], F32)
        sC2 = salloc("sC2", [128, TOK], F32)
        for k in range(24):
            DMA(lambda e, kk=k: e.dma_start(out=hfb[:, :], in_=sp_h.ap()[kk]),
                ("sp_h",), ("hfb",), q="gpsimd")
            DMA(lambda e, kk=k: e.dma_start(out=cpb2[:, :], in_=sp_cp.ap()[kk]),
                ("sp_cp",), ("cpb2",), q="gpsimd")
            V(lambda e, kk=k: e.tensor_scalar(sB2[:, :], cpb2[:, :], hendp[:, kk:kk + 1],
                                              None, ALU.mult), ("cpb2", "hendp"), ("sB2",))
            V(lambda e: e.tensor_copy(sC2[:, :], hfb[:, :]), ("hfb",), ("sC2",))
            V(lambda e: e.tensor_tensor(sC2[:, :], sC2[:, :], sB2[:, :], ALU.add),
              ("sB2", "sC2"), ("sC2",))
            V(lambda e: e.tensor_copy(hfb[:, :], sC2[:, :]), ("sC2",), ("hfb",))
            DMA(lambda e, kk=k: e.dma_start(out=sp_h.ap()[kk], in_=hfb[:, :]),
                ("hfb",), ("sp_h",), q="gpsimd")
        pop_scope()
        push_scope()
        hfx = salloc("hfx", [128, 12, TOK], BF16)
        wob = salloc("wob", [128, 12, 512], BF16)
        wcf = salloc("wcf", [128, 512], F32)
        for nch in range(4):
            for kh in range(2):
                for k in range(12):
                    DMA(lambda e, kk=k, k0=kh * 12: e.dma_start(
                        out=hfx[:, kk, :], in_=sp_h.ap()[k0 + kk]),
                        ("sp_h",), ("hfx",), q="gpsimd")
                    DMA(lambda e, kk=k, k0=kh * 12, n=nch: e.dma_start(
                        out=wcf[:, :],
                        in_=w_out[(k0 + kk) * 128:(k0 + kk + 1) * 128,
                                  n * 512:(n + 1) * 512]), ("w_out",), ("wcf",))
                    V(lambda e, kk=k: e.tensor_copy(wob[:, kk, :], wcf[:, :]),
                      ("wcf",), ("wob",))
                for tt in range(NT):
                    pbk, pbn = pb[tt % 2], f"pb{tt % 2}"
                    for k in range(12):
                        T(lambda e, kk=k, t=tt, p=pbk: e.matmul(
                            p[:, :], hfx[:, kk, t * 128:(t + 1) * 128], wob[:, kk, :],
                            start=(kk == 0), stop=(kk == 11)), ("hfx", "wob"), (pbn,))
                    V(lambda e, t=tt, n=nch, p=pbk: e.scalar_tensor_tensor(
                        x[:, t, n * 512:(n + 1) * 512], p[:, :], 1.0,
                        x[:, t, n * 512:(n + 1) * 512], ALU.mult, ALU.add), (pbn, "x"), ("x",))
        pop_scope()

    def l1_ffn():
        push_scope()
        yall = salloc("yall", [128, NT, D], BF16)
        l1b2 = [salloc("l1b", [128, 16, 512], BF16) for _ in range(2)]
        sI = salloc("sI", [128, D], I32)
        obf = salloc("obf", [128, 512], BF16)
        sB3 = salloc("sB3", [128, 512], F32)
        V(lambda e: e.memset(stats[:, 8:9], 0.0), (), ("stats",))
        for tt in range(NT):
            ln_tile(tt, y[:, :], "y")
            V(lambda e: e.tensor_reduce(stats[:, 9:10], y[:, :], AX, ALU.max,
                                        apply_absolute_value=True), ("y",), ("stats",))
            V(lambda e: e.tensor_tensor(stats[:, 8:9], stats[:, 8:9], stats[:, 9:10], ALU.max),
              ("stats",), ("stats",))
            S(lambda e, t=tt: e.copy(yall[:, t, :], y[:, :]), ("y",), ("yall",))
        T(lambda e: e.transpose(pb[4][0:1, 0:128], stats[:, 8:9], idf[:, :]), ("stats", "idf"), ("pb4",))
        V(lambda e: e.reduce_max(stats[0:1, 9:10], pb[4][0:1, 0:128], AX), ("pb4",), ("stats",))
        T(lambda e: e.matmul(pb[4][:, 0:1], onesf[0:1, :], stats[0:1, 9:10],
                             start=True, stop=True), ("onesf", "stats"), ("pb4",))
        V(lambda e: e.tensor_copy(stats[:, 10:11], pb[4][:, 0:1]), ("pb4",), ("stats",))
        DMA(lambda e: e.dma_start(out=ar_in.ap(), in_=stats[:, 10:11]), ("stats",), ("ar_in",),
            q="gpsimd")
        G(lambda e: e.collective_compute("AllReduce", ALU.max,
                                         replica_groups=[list(range(8))],
                                         ins=[ar_in.ap()], outs=[ar_out.ap()]),
          ("ar_in",), ("ar_out",))
        DMA(lambda e: e.dma_start(out=stats[:, 10:11], in_=ar_out.ap()), ("ar_out",), ("stats",))
        V(lambda e: e.tensor_scalar(stats[:, 4:5], stats[:, 10:11], 1.0 / 448.0, 1e-12,
                                    ALU.mult, ALU.max), ("stats",), ("stats",))
        V(lambda e: e.reciprocal(stats[:, 5:6], stats[:, 4:5]), ("stats",), ("stats",))
        V(lambda e: e.tensor_tensor(stats[:, 6:7], stats[:, 4:5], stats[:, 12:13], ALU.mult),
          ("stats",), ("stats",))
        for tt in range(NT):
            V(lambda e, t=tt: e.tensor_scalar(sA[:, :], yall[:, t, :], stats[:, 5:6],
                                              None, ALU.mult), ("yall", "stats"), ("sA",))
            e4m3_round(sA[:, :], sI[:, :], "sA", "sI")
            V(lambda e: e.tensor_copy(sBF[:, :, :].rearrange("p a b -> p (a b)"), sA[:, :]),
              ("sA",), ("sBF",))
            tp16("sBF", xT, "xT", tt * 128)
        for nch in range(4):
            b, bn = l1b2[nch % 2], f"l1b{nch % 2}"
            DMA(lambda e, n=nch, bb=b: e.dma_start(
                out=bb[:, :, :],
                in_=st_wl.ap()[:, :, n * 512:(n + 1) * 512].rearrange("a p b -> p a b")),
                tuple(f"st_wl_{4 * nch + i}" for i in range(4)), (bn,))
            for tt in range(NT):
                pbk, pbn = pb[tt % 2], f"pb{tt % 2}"
                for k in range(16):
                    T(lambda e, kk=k, t=tt, bb=b, p=pbk: e.matmul(
                        p[:, :], xT[:, kk, t * 128:(t + 1) * 128], bb[:, kk, :],
                        start=(kk == 0), stop=(kk == 15)), (bn, "xT"), (pbn,))
                V(lambda e, p=pbk: e.tensor_scalar(obf[:, :], p[:, :], stats[:, 6:7],
                                                   None, ALU.mult), (pbn, "stats"), ("obf",))
                V(lambda e: e.tensor_scalar(sB3[:, :], obf[:, :], 0.5, None, ALU.mult),
                  ("obf",), ("sB3",))
                V(lambda e, t=tt, n=nch: e.tensor_tensor(
                    x[:, t, n * 512:(n + 1) * 512], x[:, t, n * 512:(n + 1) * 512],
                    sB3[:, :], ALU.add), ("sB3", "x"), ("x",))
        pop_scope()

    def l1_attn():
        push_scope()
        kfT = salloc("kfT", [128, 4, TOK], BF16)
        ktok = salloc("ktok", [128, NT, 512], BF16)
        vaug = salloc("vaug", [128, NT, 516], BF16)
        wqt = salloc("wqt", [128, 16, 512], BF16)
        qtile = salloc("qtile", [128, 512], F32)
        mki2 = salloc("mki2", [128, 512], I32)
        for tt in range(NT):
            ln_tile(tt, sBF[:, :, :].rearrange("p a b -> p (a b)"), "sBF")
            tp16("sBF", xT, "xT", tt * 128)

        def load_wqt(src, sname, mb, nrows):
            for r in range(nrows):
                DMA(lambda e, rr=r, m=mb: e.dma_start(
                    out=sA[:, :], in_=src[(m * 4 + rr) * 128:(m * 4 + rr + 1) * 128, :]),
                    (sname,), ("sA",))
                V(lambda e: e.tensor_copy(sBF[:, :, :].rearrange("p a b -> p (a b)"), sA[:, :]),
                  ("sA",), ("sBF",))
                for k in range(16):
                    b, bn = pt[k % 3], f"pt{k % 3}"
                    T(lambda e, kk=k, bb=b: e.transpose(bb[:, :], sBF[:, kk, :], idbf[:, :]),
                      ("sBF", "idbf"), (bn,))
                    V(lambda e, kk=k, rr=r, bb=b: e.tensor_copy(
                        wqt[:, kk, rr * 128:(rr + 1) * 128], bb[:, :]), (bn,), ("wqt",))

        def rope_elu(nheads, tt):
            for hh in range(nheads):
                a = qtile[:, hh * 128:hh * 128 + 64]
                bq = qtile[:, hh * 128 + 64:hh * 128 + 128]
                V(lambda e, A=a: e.tensor_tensor(y[:, 0:64], A, cosb[:, tt, :], ALU.mult),
                  ("qtile", "cosb"), ("y",))
                V(lambda e, B=bq: e.tensor_tensor(y[:, 64:128], B, sinb[:, tt, :], ALU.mult),
                  ("qtile", "sinb"), ("y",))
                V(lambda e, B=bq: e.tensor_tensor(y[:, 128:192], B, cosb[:, tt, :], ALU.mult),
                  ("qtile", "cosb"), ("y",))
                V(lambda e, A=a: e.tensor_tensor(y[:, 192:256], A, sinb[:, tt, :], ALU.mult),
                  ("qtile", "sinb"), ("y",))
                V(lambda e, A=a: e.tensor_tensor(A, y[:, 0:64], y[:, 64:128], ALU.subtract),
                  ("y",), ("qtile",))
                V(lambda e, B=bq: e.tensor_tensor(B, y[:, 128:192], y[:, 192:256], ALU.add),
                  ("y",), ("qtile",))
            n = nheads * 128
            S(lambda e, m=n: e.activation(y[:, 0:m], qtile[:, 0:m], ACT.Exp),
              ("qtile",), ("y",))
            V(lambda e, m=n: e.tensor_scalar(mki2[:, 0:m], qtile[:, 0:m], 0.0, None, ALU.is_ge),
              ("qtile",), ("mki2",))
            V(lambda e, m=n: e.tensor_scalar(qtile[:, 0:m], qtile[:, 0:m], 1.0, None, ALU.add),
              ("qtile",), ("qtile",))
            V(lambda e, m=n: e.copy_predicated(y[:, 0:m], mki2[:, 0:m], qtile[:, 0:m]),
              ("mki2", "qtile", "y"), ("y",))
            V(lambda e, m=n: e.tensor_copy(
                sBF[:, :, :].rearrange("p a b -> p (a b)")[:, 0:m], y[:, 0:m]),
              ("y",), ("sBF",))

        load_wqt(wk_in, "wk_in", 0, 4)
        for tt in range(NT):
            for k in range(16):
                T(lambda e, kk=k, t=tt: e.matmul(pb[0][:, :], xT[:, kk, t * 128:(t + 1) * 128],
                                                 wqt[:, kk, :], start=(kk == 0), stop=(kk == 15)),
                  ("wqt", "xT"), ("pb0",))
            V(lambda e: e.tensor_copy(qtile[:, :], pb[0][:, :]), ("pb0",), ("qtile",))
            rope_elu(4, tt)
            V(lambda e, t=tt: e.tensor_copy(
                ktok[:, t, :], sBF[:, :, :].rearrange("p a b -> p (a b)")[:, 0:512]),
              ("sBF",), ("ktok",))
            for g in range(4):
                b, bn = pt[g % 3], f"pt{g % 3}"
                T(lambda e, gg=g, bb=b: e.transpose(bb[:, :], sBF[:, gg, :], idbf[:, :]),
                  ("sBF", "idbf"), (bn,))
                V(lambda e, gg=g, t=tt, bb=b: e.tensor_copy(
                    kfT[:, gg, t * 128:(t + 1) * 128], bb[:, :]), (bn,), ("kfT",))
        load_wqt(wv_in, "wv_in", 0, 4)
        for g in range(4):
            V(lambda e, gg=g: e.memset(vaug[:, :, gg * 129 + 128:gg * 129 + 129], 1.0),
              (), ("vaug",))
        for tt in range(NT):
            for k in range(16):
                T(lambda e, kk=k, t=tt: e.matmul(pb[0][:, :], xT[:, kk, t * 128:(t + 1) * 128],
                                                 wqt[:, kk, :], start=(kk == 0), stop=(kk == 15)),
                  ("wqt", "xT"), ("pb0",))
            for g in range(4):
                V(lambda e, gg=g, t=tt: e.tensor_copy(
                    vaug[:, t, gg * 129:gg * 129 + 128], pb[0][:, gg * 128:(gg + 1) * 128]),
                  ("pb0",), ("vaug",))
        snaps = salloc("snaps", [128, 36, 129], BF16)
        stacc = salloc("stacc", [128, 129], F32)
        stfin = salloc("stfin", [128, 516], F32)
        stp = salloc("stp", [128, 516], F32)
        stpb = salloc("stpb", [128, 516], BF16)
        for g in range(4):
            V(lambda e: e.memset(stacc[:, :], 0.0), (), ("stacc",))
            for c in range(NT):
                T(lambda e, gg=g, cc=c: e.matmul(
                    pb[2][:, 0:129], ktok[:, cc, gg * 128:(gg + 1) * 128],
                    vaug[:, cc, gg * 129:(gg + 1) * 129], start=True, stop=True),
                    ("ktok", "vaug"), ("pb2",))
                V(lambda e: e.tensor_tensor(stacc[:, :], stacc[:, :], pb[2][:, 0:129], ALU.add),
                  ("pb2", "stacc"), ("stacc",))
                V(lambda e, gg=g, cc=c: e.tensor_copy(snaps[:, gg * 9 + cc + 1, :], stacc[:, :]),
                  ("stacc",), ("snaps",))
            V(lambda e, gg=g: e.tensor_copy(stfin[:, gg * 129:(gg + 1) * 129], stacc[:, :]),
              ("stacc",), ("stfin",))
        DMA(lambda e: e.dma_start(out=ag2_in.ap(), in_=stfin[:, :]), ("stfin",), ("ag2_in",),
            q="gpsimd")
        G(lambda e: e.collective_compute("AllGather", ALU.bypass,
                                         replica_groups=[[0, 1], [2, 3], [4, 5], [6, 7]],
                                         ins=[ag2_in.ap()], outs=[ag2_out.ap()]),
          ("ag2_in",), ("ag2_out",))
        DMA(lambda e: e.dma_start(out=stp[:, :], in_=ag2_out.ap()[0:128, :]),
            ("ag2_out",), ("stp",))
        V(lambda e: e.tensor_scalar(stp[:, :], stp[:, :], role[:, 0:1], None, ALU.mult),
          ("stp", "role"), ("stp",))
        V(lambda e: e.tensor_copy(stpb[:, :], stp[:, :]), ("stp",), ("stpb",))
        for g in range(4):
            V(lambda e, gg=g: e.tensor_copy(snaps[:, gg * 9, :], stpb[:, gg * 129:(gg + 1) * 129]),
              ("stpb",), ("snaps",))
            for c in range(1, 9):
                V(lambda e, gg=g, cc=c: e.tensor_tensor(
                    snaps[:, gg * 9 + cc, :], snaps[:, gg * 9 + cc, :],
                    stpb[:, gg * 129:(gg + 1) * 129], ALU.add), ("stpb", "snaps"), ("snaps",))
        qfT = salloc("qfT", [128, 4, TOK], BF16)
        scb = salloc("scb", [128, 128], BF16)
        otr = salloc("otr", [128, 128], BF16)
        for g in range(4):
            load_wqt(wq_in, "wq_in", g, 4)
            for tt in range(NT):
                for k in range(16):
                    T(lambda e, kk=k, t=tt: e.matmul(
                        pb[0][:, :], xT[:, kk, t * 128:(t + 1) * 128], wqt[:, kk, :],
                        start=(kk == 0), stop=(kk == 15)), ("wqt", "xT"), ("pb0",))
                V(lambda e: e.tensor_copy(qtile[:, :], pb[0][:, :]), ("pb0",), ("qtile",))
                rope_elu(4, tt)
                for hh in range(4):
                    b, bn = pt[hh % 3], f"pt{hh % 3}"
                    T(lambda e, h2=hh, bb=b: e.transpose(bb[:, :], sBF[:, h2, :], idbf[:, :]),
                      ("sBF", "idbf"), (bn,))
                    V(lambda e, h2=hh, t=tt, bb=b: e.tensor_copy(
                        qfT[:, h2, t * 128:(t + 1) * 128], bb[:, :]), (bn,), ("qfT",))
            for hh in range(4):
                h = g * 4 + hh
                for c in range(NT):
                    T(lambda e, gg=g, h2=hh, cc=c: e.matmul(
                        pb[2][:, 0:128], kfT[:, gg, cc * 128:(cc + 1) * 128],
                        qfT[:, h2, cc * 128:(cc + 1) * 128], start=True, stop=True),
                        ("kfT", "qfT"), ("pb2",))
                    V(lambda e: e.tensor_tensor(scb[:, :], pb[2][:, 0:128], maskT[:, :],
                                                ALU.mult), ("pb2", "maskT"), ("scb",))
                    T(lambda e, gg=g, cc=c: e.matmul(
                        pb[3][:, 0:129], scb[:, :], vaug[:, cc, gg * 129:(gg + 1) * 129],
                        start=True, stop=False), ("scb", "vaug"), ("pb3",))
                    T(lambda e, gg=g, h2=hh, cc=c: e.matmul(
                        pb[3][:, 0:129], qfT[:, h2, cc * 128:(cc + 1) * 128],
                        snaps[:, gg * 9 + cc, :], start=False, stop=True),
                        ("qfT", "snaps"), ("pb3",))
                    V(lambda e: e.tensor_scalar(stats[:, 4:5], pb[3][:, 128:129], 1e-6, None,
                                                ALU.add), ("pb3",), ("stats",))
                    V(lambda e: e.reciprocal(stats[:, 5:6], stats[:, 4:5]), ("stats",), ("stats",))
                    V(lambda e: e.tensor_scalar(scb[:, :], pb[3][:, 0:128], stats[:, 5:6],
                                                None, ALU.mult), ("pb3", "stats"), ("scb",))
                    T(lambda e: e.transpose(pt[2][:, :], scb[:, :], idbf[:, :]),
                      ("scb", "idbf"), ("pt2",))
                    S(lambda e: e.copy(otr[:, :], pt[2][:, :]), ("pt2",), ("otr",))
                    DMA(lambda e, h3=h, cc=c: e.dma_start(
                        out=st_ao.ap()[h3, :, cc * 128:(cc + 1) * 128], in_=otr[:, :]),
                        ("otr",), ("st_ao",), q="gpsimd")
        pop_scope()
        push_scope()
        wqt2 = salloc("wqt2", [128, 16, 512], BF16)
        DMA(lambda e: e.dma_start(out=xT[:, :, :],
                                  in_=st_ao.ap().rearrange("a p t -> p a t")),
            ("st_ao",), ("xT",))
        for mb in range(4):
            for r in range(4):
                DMA(lambda e, rr=r, m=mb: e.dma_start(
                    out=sA[:, :], in_=wo_in[(m * 4 + rr) * 128:(m * 4 + rr + 1) * 128, :]),
                    ("wo_in",), ("sA",))
                V(lambda e: e.tensor_copy(sBF[:, :, :].rearrange("p a b -> p (a b)"), sA[:, :]),
                  ("sA",), ("sBF",))
                for k in range(16):
                    b, bn = pt[k % 3], f"pt{k % 3}"
                    T(lambda e, kk=k, bb=b: e.transpose(bb[:, :], sBF[:, kk, :], idbf[:, :]),
                      ("sBF", "idbf"), (bn,))
                    V(lambda e, kk=k, rr=r, bb=b: e.tensor_copy(
                        wqt2[:, kk, rr * 128:(rr + 1) * 128], bb[:, :]), (bn,), ("wqt2",))
            for tt in range(NT):
                pbk, pbn = pb[tt % 2], f"pb{tt % 2}"
                for k in range(16):
                    T(lambda e, kk=k, t=tt, p=pbk: e.matmul(
                        p[:, :], xT[:, kk, t * 128:(t + 1) * 128], wqt2[:, kk, :],
                        start=(kk == 0), stop=(kk == 15)), ("wqt2", "xT"), (pbn,))
                V(lambda e, t=tt, m=mb, p=pbk: e.scalar_tensor_tensor(
                    x[:, t, m * 512:(m + 1) * 512], p[:, :], 1.0,
                    x[:, t, m * 512:(m + 1) * 512], ALU.mult, ALU.add), (pbn, "x"), ("x",))
        pop_scope()

    stagefns = [l0_ffn, mingru, l0_ffn, l1_ffn, l1_attn, l1_ffn]
    for i in range(min(upto, 6)):
        stagefns[i]()

    DMA(lambda e: e.dma_start(out=xout.rearrange("(t p) d -> p t d", p=128),
                              in_=x[:, :, :]), ("x",), ("xout_d",))
    _wait("sync", "ld", cnt["ld"])
    barrier()

    with nc.Block() as block:
        @block.sync
        def _(e):
            for f in ops["sync"]:
                f(e)

        @block.gpsimd
        def _(e):
            for f in ops["gpsimd"]:
                f(e)

        @block.tensor
        def _(e):
            for f in ops["tensor"]:
                f(e)

        @block.vector
        def _(e):
            for f in ops["vector"]:
                f(e)

        @block.scalar
        def _(e):
            for f in ops["scalar"]:
                f(e)

    for cm in reversed(top_cms):
        cm.__exit__(None, None, None)
    return nc


def _rope_tables(pos0, n, dim=128):
    inv = 1.0 / (10000.0 ** (np.arange(0, dim, 2, dtype=np.float32) / dim))
    t = np.arange(pos0, pos0 + n, dtype=np.float32)
    fr = np.outer(t, inv)
    return np.cos(fr).astype(np.float32), np.sin(fr).astype(np.float32)


def kernel(L0_attn_x, L0_attn_w_hg, L0_attn_w_out, L0_ffn_w_gate_up, L0_ffn_w_down,
           L1_attn_Wq, L1_attn_Wk, L1_attn_Wv, L1_attn_Wo, L1_ffn_weight, upto=6):
    wl = np.asarray(L1_ffn_weight)
    if wl.dtype == np.uint16:
        wl = wl.view(BF)
    wl = np.ascontiguousarray(wl)
    idbf = np.eye(128, dtype=BF)
    idf = np.eye(128, dtype=np.float32)
    maskT = np.triu(np.ones((128, 128), dtype=np.float32))
    common = dict(
        w_hg=np.ascontiguousarray(L0_attn_w_hg, np.float32),
        w_out=np.ascontiguousarray(L0_attn_w_out, np.float32),
        w_gu=np.ascontiguousarray(L0_ffn_w_gate_up, np.float32),
        w_dn=np.ascontiguousarray(L0_ffn_w_down, np.float32),
        wq_in=np.ascontiguousarray(L1_attn_Wq, np.float32),
        wk_in=np.ascontiguousarray(L1_attn_Wk, np.float32),
        wv_in=np.ascontiguousarray(L1_attn_Wv, np.float32),
        wo_in=np.ascontiguousarray(L1_attn_Wo, np.float32),
        wl_in=wl, idbf_in=idbf, idf_in=idf, maskT_in=maskT,
        sel_in=np.kron(np.eye(4, dtype=np.float32), np.ones((1, 128), np.float32)),
    )
    xf = np.ascontiguousarray(L0_attn_x, np.float32)
    in_maps = []
    for core in range(8):
        b, half = core // 2, core % 2
        cos, sin = _rope_tables(half * TOK, TOK)
        m = dict(common)
        m["xin"] = np.ascontiguousarray(xf[b, half * TOK:(half + 1) * TOK, :])
        m["cos_in"] = cos
        m["sin_in"] = sin
        m["role_in"] = np.full((128, 1), float(half), np.float32)
        in_maps.append(m)
    nc = build(upto)
    res = run_bass_kernel_spmd(nc, in_maps, core_ids=list(range(8)))
    out = np.empty((4, 2048, 2048), np.float32)
    for core in range(8):
        b, half = core // 2, core % 2
        out[b, half * TOK:(half + 1) * TOK, :] = res.results[core]["xout"]
    return out


# version 05
# speedup vs baseline: 1.9982x; 1.0340x over previous
import numpy as np
import ml_dtypes
import concourse.bass as bass
import concourse.mybir as mybir
from concourse.bass_utils import run_bass_kernel_spmd

F32 = mybir.dt.float32
BF16 = mybir.dt.bfloat16
I32 = mybir.dt.int32
ALU = mybir.AluOpType
ACT = mybir.ActivationFunctionType
AX = mybir.AxisListType.X
BF = ml_dtypes.bfloat16

TOK = 1024
NT = 8
D = 2048
DI = 3072
IFF = 5632
NO1 = 88
NI2 = 44
MAGIC = 12582912.0
MASK_HI = int(np.int32(np.uint32(0xFFF00000)))

ENG = ("sync", "gpsimd", "tensor", "vector", "scalar")
KEY = {"sync": "ld", "gpsimd": "gd", "tensor": "t", "vector": "v", "scalar": "s"}
INC = {"ld": 16, "gd": 16, "t": 1, "v": 1, "s": 1, "g": 1}


PHASES = 99


def build(upto=6):
    nc = bass.Bass("TRN2", target_bir_lowering=False)

    xin = nc.dram_tensor("xin", [TOK, D], F32, kind="ExternalInput").ap()
    w_hg = nc.dram_tensor("w_hg", [D, 2 * DI], F32, kind="ExternalInput").ap()
    w_out = nc.dram_tensor("w_out", [DI, D], F32, kind="ExternalInput").ap()
    w_gu = nc.dram_tensor("w_gu", [2 * IFF, D], F32, kind="ExternalInput").ap()
    w_dn = nc.dram_tensor("w_dn", [D, IFF], F32, kind="ExternalInput").ap()
    wq_in = nc.dram_tensor("wq_in", [D, D], F32, kind="ExternalInput").ap()
    wk_in = nc.dram_tensor("wk_in", [512, D], F32, kind="ExternalInput").ap()
    wv_in = nc.dram_tensor("wv_in", [512, D], F32, kind="ExternalInput").ap()
    wo_in = nc.dram_tensor("wo_in", [D, D], F32, kind="ExternalInput").ap()
    wl_in = nc.dram_tensor("wl_in", [D, D], BF16, kind="ExternalInput").ap()
    idbf_in = nc.dram_tensor("idbf_in", [128, 128], BF16, kind="ExternalInput").ap()
    idf_in = nc.dram_tensor("idf_in", [128, 128], F32, kind="ExternalInput").ap()
    maskT_in = nc.dram_tensor("maskT_in", [128, 128], F32, kind="ExternalInput").ap()
    cos_in = nc.dram_tensor("cos_in", [TOK, 64], F32, kind="ExternalInput").ap()
    sin_in = nc.dram_tensor("sin_in", [TOK, 64], F32, kind="ExternalInput").ap()
    role_in = nc.dram_tensor("role_in", [128, 1], F32, kind="ExternalInput").ap()
    sel_in = nc.dram_tensor("sel_in", [4, 512], F32, kind="ExternalInput").ap()
    xout = nc.dram_tensor("xout", [TOK, D], F32, kind="ExternalOutput").ap()

    st_w1 = nc.dram_tensor("st_w1", [NO1, 128, D], BF16)
    st_w2 = nc.dram_tensor("st_w2", [NI2, 128, D], BF16)
    st_wl = nc.dram_tensor("st_wl", [16, 128, D], BF16)
    st_ao = nc.dram_tensor("st_ao", [16, 128, TOK], BF16)
    sp_h = nc.dram_tensor("sp_h", [24, 128, TOK], BF16)
    sp_cp = nc.dram_tensor("sp_cp", [24, 128, TOK], BF16)
    ag1_in = nc.dram_tensor("ag1_in", [128, 24], F32)
    ag1_out = nc.dram_tensor("ag1_out", [256, 24], F32)
    ag2_in = nc.dram_tensor("ag2_in", [128, 516], F32)
    ag2_out = nc.dram_tensor("ag2_out", [256, 516], F32)
    ar_in = nc.dram_tensor("ar_in", [128, 1], F32)
    ar_out = nc.dram_tensor("ar_out", [128, 1], F32)

    ops = {e: [] for e in ENG}
    cnt = {k: 0 for k in INC}
    waited = {e: {} for e in ENG}
    last_w = {}
    readers = {}
    sem = {}

    def _wait(eng, sk, val):
        if val <= 0 or waited[eng].get(sk, 0) >= val:
            return
        waited[eng][sk] = val
        ops[eng].append(lambda e, s=sk, v=val: e.wait_ge(sem[s], v))

    def emit(eng, fn, ins=(), outs=(), semkey=None, inc=None):
        mykey = semkey or KEY[eng]
        myinc = inc if inc is not None else INC[mykey]
        for nm in ins:
            lw = last_w.get(nm)
            if lw is not None and (lw[0] != mykey or eng != "tensor"):
                _wait(eng, lw[0], lw[1])
        for nm in outs:
            lw = last_w.get(nm)
            if lw is not None and lw[0] != mykey:
                _wait(eng, lw[0], lw[1])
            for sk, v in readers.get(nm, {}).items():
                if sk != mykey:
                    _wait(eng, sk, v)
        cnt[mykey] += myinc
        myval = cnt[mykey]
        ops[eng].append(lambda e, f=fn, s=mykey, i=myinc: f(e).then_inc(sem[s], i))
        for nm in outs:
            last_w[nm] = (mykey, myval)
            readers.pop(nm, None)
        for nm in ins:
            readers.setdefault(nm, {})[mykey] = myval
        return myval

    def barrier():
        for e in ENG:
            for sk in INC:
                if sk != KEY[e]:
                    _wait(e, sk, cnt[sk])

    def T(fn, ins=(), outs=()):
        emit("tensor", fn, ins, outs)

    def V(fn, ins=(), outs=()):
        emit("vector", fn, ins, outs)

    def S(fn, ins=(), outs=()):
        emit("scalar", fn, ins, outs)

    def G(fn, ins=(), outs=()):
        emit("gpsimd", fn, ins, outs, semkey="g", inc=1)

    def DMA(fn, ins=(), outs=(), q="sync"):
        emit(q, fn, ins, outs)

    scopes = []
    top_cms = []

    _uid = [0]

    def salloc(name, shape, dt):
        _uid[0] += 1
        cm = nc.sbuf_tensor(f"{name}_{_uid[0]}", shape, dt)
        t = cm.__enter__()
        (scopes[-1] if scopes else top_cms).append(cm)
        return t

    def push_scope():
        scopes.append([])

    def pop_scope():
        barrier()
        for cm in reversed(scopes.pop()):
            cm.__exit__(None, None, None)

    x = salloc("x", [128, NT, D], F32)
    xT = salloc("xT", [128, 16, TOK], BF16)
    y = salloc("y", [128, D], F32)
    sA = salloc("sA", [128, D], F32)
    sBF = salloc("sBF", [128, 16, 128], BF16)
    stats = salloc("stats", [128, 16], F32)
    idbf = salloc("idbf", [128, 128], BF16)
    idf = salloc("idf", [128, 128], F32)
    maskT = salloc("maskT", [128, 128], F32)
    cosb = salloc("cosb", [128, NT, 64], F32)
    sinb = salloc("sinb", [128, NT, 64], F32)
    role = salloc("role", [128, 1], F32)
    onesf = salloc("onesf", [1, 128], F32)
    ceps = salloc("ceps", [128, 2], F32)
    hend = salloc("hend", [128, 24], F32)
    hendp = salloc("hendp", [128, 24], F32)
    selc = salloc("selc", [4, 512], F32)

    def palloc(name, shape, dt):
        cm = nc.psum_tensor(name, shape, dt)
        t = cm.__enter__()
        top_cms.append(cm)
        return t

    pb = [palloc(f"pb{i}", [128, 512], F32) for i in range(5)]
    pt = [palloc(f"pt{i}", [128, 128], BF16) for i in range(3)]

    for sk in INC:
        cm = nc.semaphore("sem_" + sk)
        sem[sk] = cm.__enter__()
        top_cms.append(cm)

    def ln_tile(tt, out_ap, out_name):
        sa = x[:, tt, :]
        V(lambda e, a=sa: e.reduce_sum(stats[:, 0:1], a, AX), ("x",), ("stats",))
        S(lambda e, a=sa: e.activation(y[:, :], a, ACT.Square, accum_out=stats[:, 1:2]),
          ("x",), ("y", "stats"))
        V(lambda e: e.tensor_scalar(stats[:, 0:1], stats[:, 0:1], -1.0 / D, None, ALU.mult),
          ("stats",), ("stats",))
        V(lambda e: e.tensor_scalar(stats[:, 1:2], stats[:, 1:2], 1.0 / D, None, ALU.mult),
          ("stats",), ("stats",))
        V(lambda e: e.tensor_tensor(stats[:, 2:3], stats[:, 0:1], stats[:, 0:1], ALU.mult),
          ("stats",), ("stats",))
        V(lambda e: e.tensor_tensor(stats[:, 1:2], stats[:, 1:2], stats[:, 2:3], ALU.subtract),
          ("stats",), ("stats",))
        S(lambda e: e.activation(stats[:, 3:4], stats[:, 1:2], ACT.Sqrt, ceps[:, 0:1], 1.0),
          ("stats", "ceps"), ("stats",))
        V(lambda e: e.reciprocal(stats[:, 3:4], stats[:, 3:4]), ("stats",), ("stats",))
        S(lambda e: e.mul(stats[:, 2:3], stats[:, 0:1], stats[:, 3:4]), ("stats",), ("stats",))
        S(lambda e, o=out_ap, a=sa: e.activation(o, a, ACT.Identity, stats[:, 2:3], stats[:, 3:4]),
          ("x", "stats"), (out_name,))

    def tp16(src_name, dst, dst_name, col0, src=None):
        s = sBF if src is None else src
        for k in range(16):
            b, bn = pt[k % 3], f"pt{k % 3}"
            T(lambda e, kk=k, bb=b, ss=s: e.transpose(bb[:, :], ss[:, kk, :], idbf[:, :]),
              (src_name, "idbf"), (bn,))
            V(lambda e, kk=k, bb=b, d=dst, c=col0: e.tensor_copy(d[:, kk, c:c + 128], bb[:, :]),
              (bn,), (dst_name,))

    def rne_clip(ap, name, qmax=127.0):
        V(lambda e, a=ap: e.tensor_scalar(a, a, MAGIC, None, ALU.add), (name,), (name,))
        V(lambda e, a=ap: e.tensor_scalar(a, a, -MAGIC, None, ALU.add), (name,), (name,))
        V(lambda e, a=ap, q=qmax: e.tensor_scalar(a, a, q, -q, ALU.min, ALU.max),
          (name,), (name,))

    def e4m3_round(fap, iap, nf, ni):
        V(lambda e: e.tensor_scalar(iap, fap.bitcast(I32), 20, None,
                                    ALU.logical_shift_right), (nf,), (ni,))
        V(lambda e: e.tensor_scalar(iap, iap, 1, None, ALU.bitwise_and), (ni,), (ni,))
        V(lambda e: e.tensor_tensor(iap, iap, fap.bitcast(I32), ALU.add), (nf, ni), (ni,))
        V(lambda e: e.tensor_scalar(iap, iap, 0x0007FFFF, None, ALU.add), (ni,), (ni,))
        V(lambda e: e.tensor_scalar(iap, iap, MASK_HI, None, ALU.bitwise_and), (ni,), (ni,))
        V(lambda e: e.tensor_copy(fap, iap.bitcast(F32)), (ni,), (nf,))

    DMA(lambda e: e.dma_start(out=idbf[:, :], in_=idbf_in), (), ("idbf",))
    DMA(lambda e: e.dma_start(out=idf[:, :], in_=idf_in), (), ("idf",))
    DMA(lambda e: e.dma_start(out=maskT[:, :], in_=maskT_in), (), ("maskT",))
    DMA(lambda e: e.dma_start(out=cosb[:, :, :],
                              in_=cos_in.rearrange("(t p) c -> p t c", p=128)), (), ("cosb",))
    DMA(lambda e: e.dma_start(out=sinb[:, :, :],
                              in_=sin_in.rearrange("(t p) c -> p t c", p=128)), (), ("sinb",))
    DMA(lambda e: e.dma_start(out=role[:, :], in_=role_in), (), ("role",))
    DMA(lambda e: e.dma_start(out=x[:, :, :],
                              in_=xin.rearrange("(t p) d -> p t d", p=128)), (), ("x",))
    V(lambda e: e.memset(onesf[:, :], 1.0), (), ("onesf",))
    DMA(lambda e: e.dma_start(out=selc[:, :], in_=sel_in), (), ("selc",))
    V(lambda e: e.memset(ceps[:, 0:1], 1e-5), (), ("ceps",))

    push_scope()
    wrow = salloc("wrow", [128, IFF], F32)
    wqd = salloc("wqd", [128, IFF], BF16)
    sbT = salloc("sbT", [128, NI2, 128], BF16)
    wli = salloc("wli", [128, D], I32)
    wlrow = salloc("wlrow", [128, D], BF16)

    def wq_tile(src, sname, ncols, bi, ot):
        b, bn = wrow, "wrow"
        DMA(lambda e, o=ot, n=ncols, bb=b: e.dma_start(
            out=bb[:, 0:n], in_=src[o * 128:(o + 1) * 128, 0:n]),
            (sname,), (bn, bn + "L", bn + "H"))
        V(lambda e, n=ncols, bb=b: e.tensor_reduce(stats[:, 4:5], bb[:, 0:n], AX, ALU.max,
                                                   apply_absolute_value=True),
          (bn, bn + "L", bn + "H"), ("stats",))
        V(lambda e: e.tensor_scalar(stats[:, 5:6], stats[:, 4:5], 1.0 / 127.0, 1e-8,
                                    ALU.mult, ALU.max), ("stats",), ("stats",))
        V(lambda e: e.reciprocal(stats[:, 6:7], stats[:, 5:6]), ("stats",), ("stats",))
        nh = ncols // 2
        for EE, c0, c1, sfx in ((V, 0, nh, "L"), (V, nh, ncols, "H")):
            hn, wn = bn + sfx, "wqd" + sfx
            EE(lambda e, bb=b, a=c0, z=c1: e.tensor_scalar(bb[:, a:z], bb[:, a:z], stats[:, 6:7],
                                                           MAGIC, ALU.mult, ALU.add),
               (hn, "stats"), (hn,))
            EE(lambda e, bb=b, a=c0, z=c1: e.tensor_scalar(bb[:, a:z], bb[:, a:z], -MAGIC, 127.0,
                                                           ALU.add, ALU.min), (hn,), (hn,))
            EE(lambda e, bb=b, a=c0, z=c1: e.tensor_scalar(wqd[:, a:z], bb[:, a:z], -127.0,
                                                           stats[:, 5:6], ALU.max, ALU.mult),
               (hn, "stats"), (wn,))

    def stash_tp(nk, store_fn):
        for k in range(nk):
            b, bn = pt[k % 3], f"pt{k % 3}"
            half = "wqdL" if k < nk // 2 else "wqdH"
            T(lambda e, kk=k, bb=b: e.transpose(bb[:, :], wqd[:, kk * 128:(kk + 1) * 128], idbf[:, :]),
              (half, "idbf"), (bn,))
            S(lambda e, kk=k, bb=b: e.copy(sbT[:, kk, :], bb[:, :]), (bn,), ("sbT",))
        store_fn()

    for ot in range(NO1):
        wq_tile(w_gu, "w_gu", D, ot % 2, ot)
        stash_tp(16, lambda o=ot: DMA(lambda e: e.dma_start(
            out=st_w1.ap()[o].rearrange("p (a b) -> p a b", a=16), in_=sbT[:, 0:16, :]),
            ("sbT",), (f"st_w1_{o}",), q="gpsimd"))
    for ot in range(16):
        wq_tile(w_dn, "w_dn", IFF, ot % 2, ot)

        def store_w2(o=ot):
            for h in range(2):
                DMA(lambda e, hh=h: e.dma_start(
                    out=st_w2.ap()[hh * 22:(hh + 1) * 22, :, o * 128:(o + 1) * 128]
                        .rearrange("a p b -> p a b"),
                    in_=sbT[:, hh * 22:(hh + 1) * 22, :]), ("sbT",), (f"st_w2_{o}",), q="gpsimd")
        stash_tp(NI2, store_w2)

    V(lambda e: e.memset(stats[:, 8:9], 0.0), (), ("stats",))
    for ot in range(16):
        DMA(lambda e, o=ot: e.dma_start(
            out=wlrow[:, :], in_=wl_in[o * 128:(o + 1) * 128, :]),
            ("wl_in",), ("wlrow",))
        V(lambda e: e.tensor_reduce(stats[:, 9:10], wlrow[:, :],
                                    AX, ALU.max, apply_absolute_value=True),
          ("wlrow",), ("stats",))
        V(lambda e: e.tensor_tensor(stats[:, 8:9], stats[:, 8:9], stats[:, 9:10], ALU.max),
          ("stats",), ("stats",))
    T(lambda e: e.transpose(pb[4][0:1, 0:128], stats[:, 8:9], idf[:, :]), ("stats", "idf"), ("pb4",))
    V(lambda e: e.reduce_max(stats[0:1, 9:10], pb[4][0:1, 0:128], AX), ("pb4",), ("stats",))
    T(lambda e: e.matmul(pb[4][:, 0:1], onesf[0:1, :], stats[0:1, 9:10], start=True, stop=True),
      ("onesf", "stats"), ("pb4",))
    V(lambda e: e.tensor_scalar(stats[:, 12:13], pb[4][:, 0:1], 1.0 / 448.0, 1e-12,
                                ALU.mult, ALU.max), ("pb4",), ("stats",))
    V(lambda e: e.reciprocal(stats[:, 13:14], stats[:, 12:13]), ("stats",), ("stats",))
    for ot in range(16):
        DMA(lambda e, o=ot: e.dma_start(
            out=wlrow[:, :], in_=wl_in[o * 128:(o + 1) * 128, :]),
            ("wl_in",), ("wlrow",))
        V(lambda e: e.tensor_scalar(sA[:, :], wlrow[:, :],
                                    stats[:, 13:14], None, ALU.mult),
          ("wlrow", "stats"), ("sA",))
        e4m3_round(sA[:, :], wli[:, :], "sA", "wli")
        V(lambda e: e.tensor_copy(wqd[:, 0:D], sA[:, :]), ("sA",), ("wqd", "wqdL", "wqdH"))
        stash_tp(16, lambda o=ot: DMA(lambda e: e.dma_start(
            out=st_wl.ap()[:, :, o * 128:(o + 1) * 128].rearrange("a p b -> p a b"),
            in_=sbT[:, 0:16, :]), ("sbT",), (f"st_wl_{o}",), q="gpsimd"))
    pop_scope()

    def l0_ffn():
        push_scope()
        hsb = salloc("hsb", [128, NI2, 512], BF16)
        for half in range(2 if PHASES >= 5 else 1):
            tts = list(range(half * 4, half * 4 + 4))
            push_scope()
            w1g2 = [salloc("w1g", [128, 16, 128], BF16) for _ in range(2)]
            w1u2 = [salloc("w1u", [128, 16, 128], BF16) for _ in range(2)]
            sB = salloc("sB", [128, 512], F32)
            sC = salloc("sC", [128, 512], F32)
            sCg = salloc("sCg", [128, 512], F32)
            hax = salloc("hax", [128, 4], F32)
            hrecT = salloc("hrecT", [4, 128], F32)
            hsclT = salloc("hsclT", [4, 128], F32)
            recb = salloc("recb", [128, 512], F32)
            sclb = salloc("sclb", [128, 512], F32)
            for tt in tts:
                ln_tile(tt, y[:, :], "y")
                V(lambda e: e.tensor_reduce(stats[:, 4:5], y[:, :], AX, ALU.max,
                                            apply_absolute_value=True), ("y",), ("stats",))
                V(lambda e: e.tensor_scalar(stats[:, 5:6], stats[:, 4:5], 1.0 / 127.0, 1e-8,
                                            ALU.mult, ALU.max), ("stats",), ("stats",))
                V(lambda e: e.reciprocal(stats[:, 6:7], stats[:, 5:6]), ("stats",), ("stats",))
                V(lambda e: e.tensor_scalar(sA[:, :], y[:, :], stats[:, 6:7], MAGIC,
                                            ALU.mult, ALU.add), ("y", "stats"), ("sA",))
                V(lambda e: e.tensor_scalar(sA[:, :], sA[:, :], -MAGIC, 127.0,
                                            ALU.add, ALU.min), ("sA",), ("sA",))
                V(lambda e: e.tensor_scalar(sBF[:, :, :].rearrange("p a b -> p (a b)"),
                                            sA[:, :], -127.0, stats[:, 5:6],
                                            ALU.max, ALU.mult), ("sA", "stats"), ("sBF",))
                tp16("sBF", xT, "xT", (tt - half * 4) * 128)
            V(lambda e: e.memset(hax[:, :], 0.0), (), ("hax",))
            for mp in range(NI2 if PHASES >= 2 else 0):
                bg, bgn = w1g2[mp % 2], f"w1g{mp % 2}"
                bu, bun = w1u2[mp % 2], f"w1u{mp % 2}"
                DMA(lambda e, m=mp, b=bg: e.dma_start(
                    out=b[:, :, :], in_=st_w1.ap()[m].rearrange("p (a b) -> p a b", a=16)),
                    (f"st_w1_{mp}",), (bgn,))
                DMA(lambda e, m=mp, b=bu: e.dma_start(
                    out=b[:, :, :], in_=st_w1.ap()[m + NI2].rearrange("p (a b) -> p a b", a=16)),
                    (f"st_w1_{mp + NI2}",), (bun,))
                for k in range(16):
                    T(lambda e, kk=k, b=bg: e.matmul(
                        pb[0][:, :], b[:, kk, :], xT[:, kk, 0:512],
                        start=(kk == 0), stop=(kk == 15)), (bgn, "xT"), ("pb0",))
                for k in range(16):
                    T(lambda e, kk=k, b=bu: e.matmul(
                        pb[1][:, :], b[:, kk, :], xT[:, kk, 0:512],
                        start=(kk == 0), stop=(kk == 15)), (bun, "xT"), ("pb1",))
                S(lambda e: e.activation(sB[:, :], pb[0][:, :], ACT.Silu), ("pb0",), ("sB",))
                V(lambda e, m=mp: e.tensor_tensor(hsb[:, m, :], sB[:, :], pb[1][:, :], ALU.mult),
                  ("sB", "pb1"), ("hsb", "hsb0", "hsb1"))
                for c in range(4):
                    b, bn = pt[c % 3], f"pt{c % 3}"
                    T(lambda e, m=mp, cc=c, bb=b: e.transpose(
                        bb[:, :], hsb[:, m, cc * 128:(cc + 1) * 128], idbf[:, :]),
                        ("hsb", "idbf"), (bn,))
                    V(lambda e, bb=b: e.tensor_reduce(stats[:, 7:8], bb[:, :], AX, ALU.max,
                                                      apply_absolute_value=True),
                      (bn,), ("stats",))
                    V(lambda e, cc=c: e.tensor_tensor(hax[:, cc:cc + 1], hax[:, cc:cc + 1],
                                                      stats[:, 7:8], ALU.max),
                      ("stats", "hax"), ("hax",))
            V(lambda e: e.tensor_scalar(hax[:, 0:4], hax[:, 0:4], 1.0 / 127.0, 1e-8,
                                        ALU.mult, ALU.max), ("hax",), ("hax",))
            T(lambda e: e.transpose(pb[4][0:4, 0:128], hax[:, 0:4], idf[:, :]), ("hax", "idf"), ("pb4",))
            V(lambda e: e.tensor_copy(hsclT[0:4, :], pb[4][0:4, 0:128]), ("pb4",), ("hsclT",))
            V(lambda e: e.reciprocal(hrecT[0:4, :], hsclT[0:4, :]), ("hsclT",), ("hrecT",))
            for c in range(4):
                T(lambda e, cc=c: e.matmul(pb[4][:, cc * 128:(cc + 1) * 128],
                                           selc[0:4, cc * 128:(cc + 1) * 128],
                                           hrecT[0:4, :], start=True, stop=True),
                  ("selc", "hrecT"), ("pb4",))
            V(lambda e: e.tensor_copy(recb[:, :], pb[4][:, 0:512]), ("pb4",), ("recb",))
            for c in range(4):
                T(lambda e, cc=c: e.matmul(pb[4][:, cc * 128:(cc + 1) * 128],
                                           selc[0:4, cc * 128:(cc + 1) * 128],
                                           hsclT[0:4, :], start=True, stop=True),
                  ("selc", "hsclT"), ("pb4",))
            V(lambda e: e.tensor_copy(sclb[:, :], pb[4][:, 0:512]), ("pb4",), ("sclb",))
            for mp in range(NI2 if PHASES >= 3 else 0):
                EE = V
                tb = sC if mp % 2 == 0 else sCg
                tn = "sC" if mp % 2 == 0 else "sCg"
                hn = f"hsb{mp % 2}"
                S(lambda e, m=mp, t=tb: e.copy(t[:, :], hsb[:, m, :]), (hn, "hsb"), (tn,))
                EE(lambda e, t=tb: e.tensor_tensor(t[:, :], t[:, :], recb[:, :], ALU.mult),
                   (tn, "recb"), (tn,))
                EE(lambda e, t=tb: e.tensor_scalar(t[:, :], t[:, :], MAGIC, -MAGIC,
                                                   ALU.add, ALU.add), (tn,), (tn,))
                EE(lambda e, t=tb: e.tensor_scalar(t[:, :], t[:, :], 127.0, -127.0,
                                                   ALU.min, ALU.max), (tn,), (tn,))
                EE(lambda e, m=mp, t=tb: e.tensor_tensor(hsb[:, m, :], t[:, :], sclb[:, :],
                                                         ALU.mult), (tn, "sclb"), (hn,))
            pop_scope()
            push_scope()
            w2b = salloc("w2b", [128, NI2, 256], BF16)
            for nch in range(8 if PHASES >= 4 else 0):
                b, bn = w2b, "w2b"
                for h2 in range(2):
                    DMA(lambda e, n=nch, hh=h2, bb=b: e.dma_start(
                        out=bb[:, hh * 22:(hh + 1) * 22, :],
                        in_=st_w2.ap()[hh * 22:(hh + 1) * 22, :, n * 256:(n + 1) * 256]
                            .rearrange("a p b -> p a b")),
                        (f"st_w2_{2 * nch}", f"st_w2_{2 * nch + 1}"), (bn,))
                for i, tt in enumerate(tts):
                    pbk, pbn = pb[i % 2], f"pb{i % 2}"
                    for k in range(NI2):
                        T(lambda e, kk=k, lt=tt - half * 4, bb=b, p=pbk: e.matmul(
                            p[:, 0:256], hsb[:, kk, lt * 128:(lt + 1) * 128], bb[:, kk, :],
                            start=(kk == 0), stop=(kk == NI2 - 1)), (bn, "hsb0", "hsb1"), (pbn,))
                    V(lambda e, t=tt, n=nch, p=pbk: e.scalar_tensor_tensor(
                        x[:, t, n * 256:(n + 1) * 256], p[:, 0:256], 0.5,
                        x[:, t, n * 256:(n + 1) * 256], ALU.mult, ALU.add),
                      (pbn, "x"), ("x",))
            pop_scope()
        pop_scope()

    def mingru():
        push_scope()
        whf2 = [salloc("whf", [128, 16, 128], F32) for _ in range(2)]
        whb = [salloc(f"whb{i}", [128, 16, 128], BF16) for i in range(2)]
        wgb = [salloc(f"wgb{i}", [128, 16, 128], BF16) for i in range(2)]
        ct = salloc("ct", [128, TOK], F32)
        zt = salloc("zt", [128, TOK], F32)
        gt = salloc("gt", [128, TOK], F32)
        ht = salloc("ht", [128, TOK], F32)
        cp = salloc("cp", [128, TOK], F32)
        mki = salloc("mki", [128, TOK], I32)
        hbf = salloc("hbf", [128, TOK], BF16)
        cpb = salloc("cpb", [128, TOK], BF16)
        for tt in range(NT):
            ln_tile(tt, sBF[:, :, :].rearrange("p a b -> p (a b)"), "sBF")
            tp16("sBF", xT, "xT", tt * 128)
        for j in range(24):
            bh, bhn = whb[j % 2], f"whb{j % 2}"
            bg, bgn = wgb[j % 2], f"wgb{j % 2}"
            wf, wfn = whf2[j % 2], f"whf{j % 2}"
            DMA(lambda e, jj=j, w=wf: e.dma_start(
                out=w[:, :, :],
                in_=w_hg[:, jj * 128:(jj + 1) * 128].rearrange("(a p) c -> p a c", p=128)),
                ("w_hg",), (wfn,))
            V(lambda e, b=bh, w=wf: e.tensor_copy(
                b[:, :, :].rearrange("p a b -> p (a b)"),
                w[:, :, :].rearrange("p a b -> p (a b)")), (wfn,), (bhn,))
            wf2, wfn2 = whf2[(j + 1) % 2], f"whf{(j + 1) % 2}"
            DMA(lambda e, jj=j, w=wf2: e.dma_start(
                out=w[:, :, :],
                in_=w_hg[:, DI + jj * 128:DI + (jj + 1) * 128]
                    .rearrange("(a p) c -> p a c", p=128)), ("w_hg",), (wfn2,))
            V(lambda e, b=bg, w=wf2: e.tensor_copy(
                b[:, :, :].rearrange("p a b -> p (a b)"),
                w[:, :, :].rearrange("p a b -> p (a b)")), (wfn2,), (bgn,))
            for hf in range(2):
                for k in range(16):
                    T(lambda e, kk=k, h=hf, b=bh: e.matmul(
                        pb[h][:, :], b[:, kk, :], xT[:, kk, h * 512:(h + 1) * 512],
                        start=(kk == 0), stop=(kk == 15)), (bhn, "xT"), (f"pb{hf}",))
                for k in range(16):
                    T(lambda e, kk=k, h=hf, b=bg: e.matmul(
                        pb[2 + h][:, :], b[:, kk, :], xT[:, kk, h * 512:(h + 1) * 512],
                        start=(kk == 0), stop=(kk == 15)), (bgn, "xT"), (f"pb{2 + hf}",))
            for hf in range(2):
                sl = slice(hf * 512, (hf + 1) * 512)
                S(lambda e, h=hf, s=sl: e.activation(ct[:, s], pb[2 + h][:, :], ACT.Sigmoid,
                                                     0.0, -1.0), (f"pb{2 + hf}",), ("ct",))
                S(lambda e, h=hf, s=sl: e.activation(zt[:, s], pb[2 + h][:, :], ACT.Sigmoid),
                  (f"pb{2 + hf}",), ("zt",))
                S(lambda e, h=hf, s=sl: e.activation(gt[:, s], pb[h][:, :], ACT.Sigmoid),
                  (f"pb{hf}",), ("gt",))
                V(lambda e, h=hf, s=sl: e.tensor_scalar(ht[:, s], pb[h][:, :], 0.5, None,
                                                        ALU.add), (f"pb{hf}",), ("ht",))
                V(lambda e, h=hf, s=sl: e.tensor_scalar(mki[:, s], pb[h][:, :], 0.0, None,
                                                        ALU.is_ge), (f"pb{hf}",), ("mki",))
            V(lambda e: e.copy_predicated(gt[:, :], mki[:, :], ht[:, :]),
              ("mki", "ht", "gt"), ("gt",))
            V(lambda e: e.tensor_tensor(gt[:, :], zt[:, :], gt[:, :], ALU.mult),
              ("zt", "gt"), ("gt",))
            V(lambda e: e.tensor_tensor_scan(ht[:, 0:512], ct[:, 0:512], gt[:, 0:512],
                                             0.0, ALU.mult, ALU.add), ("ct", "gt"), ("ht",))
            V(lambda e: e.tensor_tensor_scan(ht[:, 512:1024], ct[:, 512:1024], gt[:, 512:1024],
                                             ht[:, 511:512], ALU.mult, ALU.add),
              ("ct", "gt", "ht"), ("ht",))
            V(lambda e: e.tensor_tensor_scan(cp[:, 0:512], ct[:, 0:512], gt[:, 0:512],
                                             1.0, ALU.mult, ALU.bypass), ("ct", "gt"), ("cp",))
            V(lambda e: e.tensor_tensor_scan(cp[:, 512:1024], ct[:, 512:1024], gt[:, 512:1024],
                                             cp[:, 511:512], ALU.mult, ALU.bypass),
              ("ct", "gt", "cp"), ("cp",))
            V(lambda e, jj=j: e.tensor_copy(hend[:, jj:jj + 1], ht[:, 1023:1024]),
              ("ht",), ("hend",))
            S(lambda e: e.copy(hbf[:, :], ht[:, :]), ("ht",), ("hbf",))
            S(lambda e: e.copy(cpb[:, :], cp[:, :]), ("cp",), ("cpb",))
            DMA(lambda e, jj=j: e.dma_start(out=sp_h.ap()[jj], in_=hbf[:, :]),
                ("hbf",), ("sp_h",), q="gpsimd")
            DMA(lambda e, jj=j: e.dma_start(out=sp_cp.ap()[jj], in_=cpb[:, :]),
                ("cpb",), ("sp_cp",), q="gpsimd")
        DMA(lambda e: e.dma_start(out=ag1_in.ap(), in_=hend[:, :]), ("hend",), ("ag1_in",),
            q="gpsimd")
        G(lambda e: e.collective_compute("AllGather", ALU.bypass,
                                         replica_groups=[[0, 1], [2, 3], [4, 5], [6, 7]],
                                         ins=[ag1_in.ap()], outs=[ag1_out.ap()]),
          ("ag1_in",), ("ag1_out",))
        DMA(lambda e: e.dma_start(out=hendp[:, :], in_=ag1_out.ap()[0:128, :]),
            ("ag1_out",), ("hendp",))
        V(lambda e: e.tensor_scalar(hendp[:, :], hendp[:, :], role[:, 0:1], None, ALU.mult),
          ("hendp", "role"), ("hendp",))
        pop_scope()
        push_scope()
        hfb = salloc("hfb", [128, TOK], BF16)
        cpb2 = salloc("cpb2", [128, TOK], BF16)
        sB2 = salloc("sB2", [128, TOK], F32)
        sC2 = salloc("sC2", [128, TOK], F32)
        for k in range(24):
            DMA(lambda e, kk=k: e.dma_start(out=hfb[:, :], in_=sp_h.ap()[kk]),
                ("sp_h",), ("hfb",), q="gpsimd")
            DMA(lambda e, kk=k: e.dma_start(out=cpb2[:, :], in_=sp_cp.ap()[kk]),
                ("sp_cp",), ("cpb2",), q="gpsimd")
            V(lambda e, kk=k: e.tensor_scalar(sB2[:, :], cpb2[:, :], hendp[:, kk:kk + 1],
                                              None, ALU.mult), ("cpb2", "hendp"), ("sB2",))
            V(lambda e: e.tensor_copy(sC2[:, :], hfb[:, :]), ("hfb",), ("sC2",))
            V(lambda e: e.tensor_tensor(sC2[:, :], sC2[:, :], sB2[:, :], ALU.add),
              ("sB2", "sC2"), ("sC2",))
            V(lambda e: e.tensor_copy(hfb[:, :], sC2[:, :]), ("sC2",), ("hfb",))
            DMA(lambda e, kk=k: e.dma_start(out=sp_h.ap()[kk], in_=hfb[:, :]),
                ("hfb",), ("sp_h",), q="gpsimd")
        pop_scope()
        push_scope()
        hfx1 = salloc("hfx", [128, 12, TOK], BF16)
        wob2 = [salloc("wob", [128, 12, 512], BF16) for _ in range(2)]
        wof = salloc("wof", [128, 12, 512], F32)
        for nch in range(4):
            for kh in range(2):
                par = (nch * 2 + kh) % 2
                hfx, hfn = hfx1, "hfx"
                wob, won = wob2[par], f"wob{par}"
                for k in range(12):
                    DMA(lambda e, kk=k, k0=kh * 12, h=hfx: e.dma_start(
                        out=h[:, kk, :], in_=sp_h.ap()[k0 + kk]),
                        ("sp_h",), (hfn,), q="gpsimd")
                DMA(lambda e, k0=kh * 12, n=nch: e.dma_start(
                    out=wof[:, :, :],
                    in_=w_out[k0 * 128:(k0 + 12) * 128, n * 512:(n + 1) * 512]
                        .rearrange("(a p) c -> p a c", p=128)), ("w_out",), ("wof",))
                S(lambda e, w=wob: e.copy(w[:, :, :].rearrange("p a b -> p (a b)"),
                                          wof[:, :, :].rearrange("p a b -> p (a b)")),
                  ("wof",), (won,))
                for tt in range(NT):
                    pbk, pbn = pb[tt % 2], f"pb{tt % 2}"
                    for k in range(12):
                        T(lambda e, kk=k, t=tt, p=pbk, h=hfx, w=wob: e.matmul(
                            p[:, :], h[:, kk, t * 128:(t + 1) * 128], w[:, kk, :],
                            start=(kk == 0), stop=(kk == 11)), (hfn, won), (pbn,))
                    V(lambda e, t=tt, n=nch, p=pbk: e.scalar_tensor_tensor(
                        x[:, t, n * 512:(n + 1) * 512], p[:, :], 1.0,
                        x[:, t, n * 512:(n + 1) * 512], ALU.mult, ALU.add), (pbn, "x"), ("x",))
        pop_scope()

    def l1_ffn():
        push_scope()
        yall = salloc("yall", [128, NT, D], BF16)
        l1b2 = [salloc("l1b", [128, 16, 512], BF16) for _ in range(2)]
        sI = salloc("sI", [128, D], I32)
        obf = salloc("obf", [128, 512], BF16)
        sB3 = salloc("sB3", [128, 512], F32)
        V(lambda e: e.memset(stats[:, 8:9], 0.0), (), ("stats",))
        for tt in range(NT):
            ln_tile(tt, y[:, :], "y")
            V(lambda e: e.tensor_reduce(stats[:, 9:10], y[:, :], AX, ALU.max,
                                        apply_absolute_value=True), ("y",), ("stats",))
            V(lambda e: e.tensor_tensor(stats[:, 8:9], stats[:, 8:9], stats[:, 9:10], ALU.max),
              ("stats",), ("stats",))
            S(lambda e, t=tt: e.copy(yall[:, t, :], y[:, :]), ("y",), ("yall",))
        T(lambda e: e.transpose(pb[4][0:1, 0:128], stats[:, 8:9], idf[:, :]), ("stats", "idf"), ("pb4",))
        V(lambda e: e.reduce_max(stats[0:1, 9:10], pb[4][0:1, 0:128], AX), ("pb4",), ("stats",))
        T(lambda e: e.matmul(pb[4][:, 0:1], onesf[0:1, :], stats[0:1, 9:10],
                             start=True, stop=True), ("onesf", "stats"), ("pb4",))
        V(lambda e: e.tensor_copy(stats[:, 10:11], pb[4][:, 0:1]), ("pb4",), ("stats",))
        DMA(lambda e: e.dma_start(out=ar_in.ap(), in_=stats[:, 10:11]), ("stats",), ("ar_in",),
            q="gpsimd")
        G(lambda e: e.collective_compute("AllReduce", ALU.max,
                                         replica_groups=[list(range(8))],
                                         ins=[ar_in.ap()], outs=[ar_out.ap()]),
          ("ar_in",), ("ar_out",))
        DMA(lambda e: e.dma_start(out=stats[:, 10:11], in_=ar_out.ap()), ("ar_out",), ("stats",))
        V(lambda e: e.tensor_scalar(stats[:, 4:5], stats[:, 10:11], 1.0 / 448.0, 1e-12,
                                    ALU.mult, ALU.max), ("stats",), ("stats",))
        V(lambda e: e.reciprocal(stats[:, 5:6], stats[:, 4:5]), ("stats",), ("stats",))
        V(lambda e: e.tensor_tensor(stats[:, 6:7], stats[:, 4:5], stats[:, 12:13], ALU.mult),
          ("stats",), ("stats",))
        for tt in range(NT):
            V(lambda e, t=tt: e.tensor_scalar(sA[:, :], yall[:, t, :], stats[:, 5:6],
                                              None, ALU.mult), ("yall", "stats"), ("sA",))
            e4m3_round(sA[:, :], sI[:, :], "sA", "sI")
            V(lambda e: e.tensor_copy(sBF[:, :, :].rearrange("p a b -> p (a b)"), sA[:, :]),
              ("sA",), ("sBF",))
            tp16("sBF", xT, "xT", tt * 128)
        for nch in range(4):
            b, bn = l1b2[nch % 2], f"l1b{nch % 2}"
            DMA(lambda e, n=nch, bb=b: e.dma_start(
                out=bb[:, :, :],
                in_=st_wl.ap()[:, :, n * 512:(n + 1) * 512].rearrange("a p b -> p a b")),
                tuple(f"st_wl_{4 * nch + i}" for i in range(4)), (bn,))
            for tt in range(NT):
                pbk, pbn = pb[tt % 2], f"pb{tt % 2}"
                for k in range(16):
                    T(lambda e, kk=k, t=tt, bb=b, p=pbk: e.matmul(
                        p[:, :], xT[:, kk, t * 128:(t + 1) * 128], bb[:, kk, :],
                        start=(kk == 0), stop=(kk == 15)), (bn, "xT"), (pbn,))
                V(lambda e, p=pbk: e.tensor_scalar(obf[:, :], p[:, :], stats[:, 6:7],
                                                   None, ALU.mult), (pbn, "stats"), ("obf",))
                V(lambda e: e.tensor_scalar(sB3[:, :], obf[:, :], 0.5, None, ALU.mult),
                  ("obf",), ("sB3",))
                V(lambda e, t=tt, n=nch: e.tensor_tensor(
                    x[:, t, n * 512:(n + 1) * 512], x[:, t, n * 512:(n + 1) * 512],
                    sB3[:, :], ALU.add), ("sB3", "x"), ("x",))
        pop_scope()

    def l1_attn():
        push_scope()
        kfT = salloc("kfT", [128, 4, TOK], BF16)
        ktok = salloc("ktok", [128, NT, 512], BF16)
        vaug = salloc("vaug", [128, NT, 516], BF16)
        wqt = salloc("wqt", [128, 16, 512], BF16)
        qtile = salloc("qtile", [128, 512], F32)
        mki2 = salloc("mki2", [128, 512], I32)
        for tt in range(NT):
            ln_tile(tt, sBF[:, :, :].rearrange("p a b -> p (a b)"), "sBF")
            tp16("sBF", xT, "xT", tt * 128)

        def load_wqt(src, sname, mb, nrows):
            for r in range(nrows):
                DMA(lambda e, rr=r, m=mb: e.dma_start(
                    out=sA[:, :], in_=src[(m * 4 + rr) * 128:(m * 4 + rr + 1) * 128, :]),
                    (sname,), ("sA",))
                V(lambda e: e.tensor_copy(sBF[:, :, :].rearrange("p a b -> p (a b)"), sA[:, :]),
                  ("sA",), ("sBF",))
                for k in range(16):
                    b, bn = pt[k % 3], f"pt{k % 3}"
                    T(lambda e, kk=k, bb=b: e.transpose(bb[:, :], sBF[:, kk, :], idbf[:, :]),
                      ("sBF", "idbf"), (bn,))
                    V(lambda e, kk=k, rr=r, bb=b: e.tensor_copy(
                        wqt[:, kk, rr * 128:(rr + 1) * 128], bb[:, :]), (bn,), ("wqt",))

        def rope_elu(nheads, tt):
            for hh in range(nheads):
                a = qtile[:, hh * 128:hh * 128 + 64]
                bq = qtile[:, hh * 128 + 64:hh * 128 + 128]
                V(lambda e, A=a: e.tensor_tensor(y[:, 0:64], A, cosb[:, tt, :], ALU.mult),
                  ("qtile", "cosb"), ("y",))
                V(lambda e, B=bq: e.tensor_tensor(y[:, 64:128], B, sinb[:, tt, :], ALU.mult),
                  ("qtile", "sinb"), ("y",))
                V(lambda e, B=bq: e.tensor_tensor(y[:, 128:192], B, cosb[:, tt, :], ALU.mult),
                  ("qtile", "cosb"), ("y",))
                V(lambda e, A=a: e.tensor_tensor(y[:, 192:256], A, sinb[:, tt, :], ALU.mult),
                  ("qtile", "sinb"), ("y",))
                V(lambda e, A=a: e.tensor_tensor(A, y[:, 0:64], y[:, 64:128], ALU.subtract),
                  ("y",), ("qtile",))
                V(lambda e, B=bq: e.tensor_tensor(B, y[:, 128:192], y[:, 192:256], ALU.add),
                  ("y",), ("qtile",))
            n = nheads * 128
            S(lambda e, m=n: e.activation(y[:, 0:m], qtile[:, 0:m], ACT.Exp),
              ("qtile",), ("y",))
            V(lambda e, m=n: e.tensor_scalar(mki2[:, 0:m], qtile[:, 0:m], 0.0, None, ALU.is_ge),
              ("qtile",), ("mki2",))
            V(lambda e, m=n: e.tensor_scalar(qtile[:, 0:m], qtile[:, 0:m], 1.0, None, ALU.add),
              ("qtile",), ("qtile",))
            V(lambda e, m=n: e.copy_predicated(y[:, 0:m], mki2[:, 0:m], qtile[:, 0:m]),
              ("mki2", "qtile", "y"), ("y",))
            V(lambda e, m=n: e.tensor_copy(
                sBF[:, :, :].rearrange("p a b -> p (a b)")[:, 0:m], y[:, 0:m]),
              ("y",), ("sBF",))

        load_wqt(wk_in, "wk_in", 0, 4)
        for tt in range(NT):
            for k in range(16):
                T(lambda e, kk=k, t=tt: e.matmul(pb[0][:, :], xT[:, kk, t * 128:(t + 1) * 128],
                                                 wqt[:, kk, :], start=(kk == 0), stop=(kk == 15)),
                  ("wqt", "xT"), ("pb0",))
            V(lambda e: e.tensor_copy(qtile[:, :], pb[0][:, :]), ("pb0",), ("qtile",))
            rope_elu(4, tt)
            V(lambda e, t=tt: e.tensor_copy(
                ktok[:, t, :], sBF[:, :, :].rearrange("p a b -> p (a b)")[:, 0:512]),
              ("sBF",), ("ktok",))
            for g in range(4):
                b, bn = pt[g % 3], f"pt{g % 3}"
                T(lambda e, gg=g, bb=b: e.transpose(bb[:, :], sBF[:, gg, :], idbf[:, :]),
                  ("sBF", "idbf"), (bn,))
                V(lambda e, gg=g, t=tt, bb=b: e.tensor_copy(
                    kfT[:, gg, t * 128:(t + 1) * 128], bb[:, :]), (bn,), ("kfT",))
        load_wqt(wv_in, "wv_in", 0, 4)
        for g in range(4):
            V(lambda e, gg=g: e.memset(vaug[:, :, gg * 129 + 128:gg * 129 + 129], 1.0),
              (), ("vaug",))
        for tt in range(NT):
            for k in range(16):
                T(lambda e, kk=k, t=tt: e.matmul(pb[0][:, :], xT[:, kk, t * 128:(t + 1) * 128],
                                                 wqt[:, kk, :], start=(kk == 0), stop=(kk == 15)),
                  ("wqt", "xT"), ("pb0",))
            for g in range(4):
                V(lambda e, gg=g, t=tt: e.tensor_copy(
                    vaug[:, t, gg * 129:gg * 129 + 128], pb[0][:, gg * 128:(gg + 1) * 128]),
                  ("pb0",), ("vaug",))
        snaps = salloc("snaps", [128, 36, 129], BF16)
        stacc = salloc("stacc", [128, 129], F32)
        stfin = salloc("stfin", [128, 516], F32)
        stp = salloc("stp", [128, 516], F32)
        stpb = salloc("stpb", [128, 516], BF16)
        for g in range(4):
            V(lambda e: e.memset(stacc[:, :], 0.0), (), ("stacc",))
            for c in range(NT):
                T(lambda e, gg=g, cc=c: e.matmul(
                    pb[2][:, 0:129], ktok[:, cc, gg * 128:(gg + 1) * 128],
                    vaug[:, cc, gg * 129:(gg + 1) * 129], start=True, stop=True),
                    ("ktok", "vaug"), ("pb2",))
                V(lambda e: e.tensor_tensor(stacc[:, :], stacc[:, :], pb[2][:, 0:129], ALU.add),
                  ("pb2", "stacc"), ("stacc",))
                V(lambda e, gg=g, cc=c: e.tensor_copy(snaps[:, gg * 9 + cc + 1, :], stacc[:, :]),
                  ("stacc",), ("snaps",))
            V(lambda e, gg=g: e.tensor_copy(stfin[:, gg * 129:(gg + 1) * 129], stacc[:, :]),
              ("stacc",), ("stfin",))
        DMA(lambda e: e.dma_start(out=ag2_in.ap(), in_=stfin[:, :]), ("stfin",), ("ag2_in",),
            q="gpsimd")
        G(lambda e: e.collective_compute("AllGather", ALU.bypass,
                                         replica_groups=[[0, 1], [2, 3], [4, 5], [6, 7]],
                                         ins=[ag2_in.ap()], outs=[ag2_out.ap()]),
          ("ag2_in",), ("ag2_out",))
        DMA(lambda e: e.dma_start(out=stp[:, :], in_=ag2_out.ap()[0:128, :]),
            ("ag2_out",), ("stp",))
        V(lambda e: e.tensor_scalar(stp[:, :], stp[:, :], role[:, 0:1], None, ALU.mult),
          ("stp", "role"), ("stp",))
        V(lambda e: e.tensor_copy(stpb[:, :], stp[:, :]), ("stp",), ("stpb",))
        for g in range(4):
            V(lambda e, gg=g: e.tensor_copy(snaps[:, gg * 9, :], stpb[:, gg * 129:(gg + 1) * 129]),
              ("stpb",), ("snaps",))
            for c in range(1, 9):
                V(lambda e, gg=g, cc=c: e.tensor_tensor(
                    snaps[:, gg * 9 + cc, :], snaps[:, gg * 9 + cc, :],
                    stpb[:, gg * 129:(gg + 1) * 129], ALU.add), ("stpb", "snaps"), ("snaps",))
        qfT = salloc("qfT", [128, 4, TOK], BF16)
        scb2 = [salloc("scb", [128, 128], BF16) for _ in range(2)]
        otr2 = [salloc("otr", [128, 128], BF16) for _ in range(2)]
        for g in range(4):
            load_wqt(wq_in, "wq_in", g, 4)
            for tt in range(NT):
                for k in range(16):
                    T(lambda e, kk=k, t=tt: e.matmul(
                        pb[0][:, :], xT[:, kk, t * 128:(t + 1) * 128], wqt[:, kk, :],
                        start=(kk == 0), stop=(kk == 15)), ("wqt", "xT"), ("pb0",))
                V(lambda e: e.tensor_copy(qtile[:, :], pb[0][:, :]), ("pb0",), ("qtile",))
                rope_elu(4, tt)
                for hh in range(4):
                    b, bn = pt[hh % 3], f"pt{hh % 3}"
                    T(lambda e, h2=hh, bb=b: e.transpose(bb[:, :], sBF[:, h2, :], idbf[:, :]),
                      ("sBF", "idbf"), (bn,))
                    V(lambda e, h2=hh, t=tt, bb=b: e.tensor_copy(
                        qfT[:, h2, t * 128:(t + 1) * 128], bb[:, :]), (bn,), ("qfT",))
            for hh in range(4):
                h = g * 4 + hh
                for c in range(NT):
                    par = c % 2
                    psc, pscn = (pb[2], "pb2") if par == 0 else (pb[0], "pb0")
                    pou, poun = (pb[3], "pb3") if par == 0 else (pb[1], "pb1")
                    scb, scn = scb2[par], f"scb{par}"
                    otr, otn = otr2[par], f"otr{par}"
                    ptb, ptn = pt[2 - par], f"pt{2 - par}"
                    d0, d1 = (4, 5) if par == 0 else (6, 7)
                    T(lambda e, gg=g, h2=hh, cc=c, p=psc: e.matmul(
                        p[:, 0:128], kfT[:, gg, cc * 128:(cc + 1) * 128],
                        qfT[:, h2, cc * 128:(cc + 1) * 128], start=True, stop=True),
                        ("kfT", "qfT"), (pscn,))
                    V(lambda e, s=scb, p=psc: e.tensor_tensor(s[:, :], p[:, 0:128], maskT[:, :],
                                                ALU.mult), (pscn, "maskT"), (scn,))
                    T(lambda e, gg=g, cc=c, s=scb, p=pou: e.matmul(
                        p[:, 0:129], s[:, :], vaug[:, cc, gg * 129:(gg + 1) * 129],
                        start=True, stop=False), (scn, "vaug"), (poun,))
                    T(lambda e, gg=g, h2=hh, cc=c, p=pou: e.matmul(
                        p[:, 0:129], qfT[:, h2, cc * 128:(cc + 1) * 128],
                        snaps[:, gg * 9 + cc, :], start=False, stop=True),
                        ("qfT", "snaps"), (poun,))
                    V(lambda e, p=pou, a=d0: e.tensor_scalar(stats[:, a:a + 1], p[:, 128:129],
                                                1e-6, None, ALU.add), (poun,), ("stats",))
                    V(lambda e, a=d0, bq=d1: e.reciprocal(stats[:, bq:bq + 1],
                                                stats[:, a:a + 1]), ("stats",), ("stats",))
                    V(lambda e, s=scb, p=pou, bq=d1: e.tensor_scalar(s[:, :], p[:, 0:128],
                                                stats[:, bq:bq + 1], None, ALU.mult),
                      (poun, "stats"), (scn,))
                    T(lambda e, s=scb, pp=ptb: e.transpose(pp[:, :], s[:, :], idbf[:, :]),
                      (scn, "idbf"), (ptn,))
                    S(lambda e, o=otr, pp=ptb: e.copy(o[:, :], pp[:, :]), (ptn,), (otn,))
                    DMA(lambda e, h3=h, cc=c, o=otr: e.dma_start(
                        out=st_ao.ap()[h3, :, cc * 128:(cc + 1) * 128], in_=o[:, :]),
                        (otn,), ("st_ao",), q="gpsimd")
        pop_scope()
        push_scope()
        wqt2 = salloc("wqt2", [128, 16, 512], BF16)
        DMA(lambda e: e.dma_start(out=xT[:, :, :],
                                  in_=st_ao.ap().rearrange("a p t -> p a t")),
            ("st_ao",), ("xT",))
        for mb in range(4):
            for r in range(4):
                DMA(lambda e, rr=r, m=mb: e.dma_start(
                    out=sA[:, :], in_=wo_in[(m * 4 + rr) * 128:(m * 4 + rr + 1) * 128, :]),
                    ("wo_in",), ("sA",))
                V(lambda e: e.tensor_copy(sBF[:, :, :].rearrange("p a b -> p (a b)"), sA[:, :]),
                  ("sA",), ("sBF",))
                for k in range(16):
                    b, bn = pt[k % 3], f"pt{k % 3}"
                    T(lambda e, kk=k, bb=b: e.transpose(bb[:, :], sBF[:, kk, :], idbf[:, :]),
                      ("sBF", "idbf"), (bn,))
                    V(lambda e, kk=k, rr=r, bb=b: e.tensor_copy(
                        wqt2[:, kk, rr * 128:(rr + 1) * 128], bb[:, :]), (bn,), ("wqt2",))
            for tt in range(NT):
                pbk, pbn = pb[tt % 2], f"pb{tt % 2}"
                for k in range(16):
                    T(lambda e, kk=k, t=tt, p=pbk: e.matmul(
                        p[:, :], xT[:, kk, t * 128:(t + 1) * 128], wqt2[:, kk, :],
                        start=(kk == 0), stop=(kk == 15)), ("wqt2", "xT"), (pbn,))
                V(lambda e, t=tt, m=mb, p=pbk: e.scalar_tensor_tensor(
                    x[:, t, m * 512:(m + 1) * 512], p[:, :], 1.0,
                    x[:, t, m * 512:(m + 1) * 512], ALU.mult, ALU.add), (pbn, "x"), ("x",))
        pop_scope()

    stagefns = [l0_ffn, mingru, l0_ffn, l1_ffn, l1_attn, l1_ffn]
    for i in range(min(upto, 6)):
        stagefns[i]()

    DMA(lambda e: e.dma_start(out=xout.rearrange("(t p) d -> p t d", p=128),
                              in_=x[:, :, :]), ("x",), ("xout_d",))
    _wait("sync", "ld", cnt["ld"])
    barrier()

    with nc.Block() as block:
        @block.sync
        def _(e):
            for f in ops["sync"]:
                f(e)

        @block.gpsimd
        def _(e):
            for f in ops["gpsimd"]:
                f(e)

        @block.tensor
        def _(e):
            for f in ops["tensor"]:
                f(e)

        @block.vector
        def _(e):
            for f in ops["vector"]:
                f(e)

        @block.scalar
        def _(e):
            for f in ops["scalar"]:
                f(e)

    for cm in reversed(top_cms):
        cm.__exit__(None, None, None)
    return nc


def _rope_tables(pos0, n, dim=128):
    inv = 1.0 / (10000.0 ** (np.arange(0, dim, 2, dtype=np.float32) / dim))
    t = np.arange(pos0, pos0 + n, dtype=np.float32)
    fr = np.outer(t, inv)
    return np.cos(fr).astype(np.float32), np.sin(fr).astype(np.float32)


def kernel(L0_attn_x, L0_attn_w_hg, L0_attn_w_out, L0_ffn_w_gate_up, L0_ffn_w_down,
           L1_attn_Wq, L1_attn_Wk, L1_attn_Wv, L1_attn_Wo, L1_ffn_weight, upto=6):
    wl = np.asarray(L1_ffn_weight)
    if wl.dtype == np.uint16:
        wl = wl.view(BF)
    wl = np.ascontiguousarray(wl)
    idbf = np.eye(128, dtype=BF)
    idf = np.eye(128, dtype=np.float32)
    maskT = np.triu(np.ones((128, 128), dtype=np.float32))
    common = dict(
        w_hg=np.ascontiguousarray(L0_attn_w_hg, np.float32),
        w_out=np.ascontiguousarray(L0_attn_w_out, np.float32),
        w_gu=np.ascontiguousarray(L0_ffn_w_gate_up, np.float32),
        w_dn=np.ascontiguousarray(L0_ffn_w_down, np.float32),
        wq_in=np.ascontiguousarray(L1_attn_Wq, np.float32),
        wk_in=np.ascontiguousarray(L1_attn_Wk, np.float32),
        wv_in=np.ascontiguousarray(L1_attn_Wv, np.float32),
        wo_in=np.ascontiguousarray(L1_attn_Wo, np.float32),
        wl_in=wl, idbf_in=idbf, idf_in=idf, maskT_in=maskT,
        sel_in=np.kron(np.eye(4, dtype=np.float32), np.ones((1, 128), np.float32)),
    )
    xf = np.ascontiguousarray(L0_attn_x, np.float32)
    in_maps = []
    for core in range(8):
        b, half = core // 2, core % 2
        cos, sin = _rope_tables(half * TOK, TOK)
        m = dict(common)
        m["xin"] = np.ascontiguousarray(xf[b, half * TOK:(half + 1) * TOK, :])
        m["cos_in"] = cos
        m["sin_in"] = sin
        m["role_in"] = np.full((128, 1), float(half), np.float32)
        in_maps.append(m)
    nc = build(upto)
    res = run_bass_kernel_spmd(nc, in_maps, core_ids=list(range(8)))
    out = np.empty((4, 2048, 2048), np.float32)
    for core in range(8):
        b, half = core // 2, core % 2
        out[b, half * TOK:(half + 1) * TOK, :] = res.results[core]["xout"]
    return out
